# version 03
# baseline (speedup 1.0000x reference)
import numpy as np
from contextlib import ExitStack
import concourse.bass as bass
import concourse.mybir as mybir

F32 = mybir.dt.float32
BF16 = mybir.dt.bfloat16
AF = mybir.ActivationFunctionType
ALU = mybir.AluOpType
AX = mybir.AxisListType

_STAGES = [16] * 7 + [8, 4, 2, 1]


def _f32(x):
    return float(np.float32(x))


class SE:

    def __init__(self, eng, sem, self_chain=True):
        self.e = eng
        self.sem = sem
        self.cnt = 0
        self.self_chain = self_chain

    def w(self, sem_cnt):
        sem, val = sem_cnt
        if val > 0:
            self.e.wait_ge(sem, val)

    def mark(self):
        return (self.sem, self.cnt)

    def __getattr__(self, name):
        fn = getattr(self.e, name)
        if not callable(fn):
            return fn

        def wrapped(*a, **k):
            inst = fn(*a, **k)
            if self.self_chain and self.cnt > 0:
                inst._wait_ge(self.sem, self.cnt)
            self.cnt += 1
            inst.then_inc(self.sem, 1)
            return inst

        return wrapped


QUANT_VEC_OPS = 1 + 1 + 1 + 4 * len(_STAGES) + 1 + 1 + 4 + 2 + 1 + 2 + 1 + 1


def emit_quant_vec(V, qb, out_ap):
    aa, sg = qb["aa"], qb["sg"]
    amax, rr, ss, mm, tt = qb["amax"], qb["rr"], qb["ss"], qb["mm"], qb["tt"]
    xh, acc, bb, cc, qq = qb["xh"], qb["acc"], qb["bb"], qb["cc"], qb["qq"]
    NB = 16
    c0 = V.cnt
    V.tensor_reduce(out=amax[:], in_=aa[:].rearrange("p (a b) -> p a b", b=32),
                    axis=AX.X, op=ALU.max)
    V.tensor_scalar(out=mm[:], in0=amax[:], scalar1=_f32(1e-30),
                    scalar2=_f32(2.0 ** 100), op0=ALU.max, op1=ALU.mult)
    V.memset(rr[:], _f32(2.0 ** 102))
    for E in _STAGES:
        V.tensor_scalar(out=tt[:], in0=mm[:], scalar1=_f32(2.0 ** E),
                        scalar2=None, op0=ALU.is_lt)
        V.tensor_scalar(out=tt[:], in0=tt[:], scalar1=_f32(1.0 - 2.0 ** (-E)),
                        scalar2=_f32(2.0 ** (-E)), op0=ALU.mult, op1=ALU.add)
        V.tensor_tensor(out=mm[:], in0=mm[:], in1=tt[:], op=ALU.mult)
        V.tensor_tensor(out=rr[:], in0=rr[:], in1=tt[:], op=ALU.mult)
    V.reciprocal(out=ss[:], in_=rr[:])
    rbc = rr[:].unsqueeze(2).broadcast_to([128, NB, 32])
    V.tensor_tensor(out=xh[:].rearrange("p (a b) -> p a b", b=32),
                    in0=aa[:].rearrange("p (a b) -> p a b", b=32),
                    in1=rbc, op=ALU.mult)
    V.tensor_scalar(out=acc[:], in0=xh[:], scalar1=0.25, scalar2=None, op0=ALU.is_ge)
    for m in (0.75, 1.25, 1.75):
        V.scalar_tensor_tensor(out=acc[:], in0=xh[:], scalar=m, in1=acc[:],
                               op0=ALU.is_ge, op1=ALU.add)
    V.tensor_scalar(out=bb[:], in0=xh[:], scalar1=2.5, scalar2=None, op0=ALU.is_ge)
    V.scalar_tensor_tensor(out=bb[:], in0=xh[:], scalar=3.5, in1=bb[:],
                           op0=ALU.is_ge, op1=ALU.add)
    V.tensor_scalar(out=cc[:], in0=xh[:], scalar1=5.0, scalar2=None, op0=ALU.is_ge)
    V.scalar_tensor_tensor(out=qq[:], in0=acc[:], scalar=0.5, in1=bb[:],
                           op0=ALU.mult, op1=ALU.add)
    V.scalar_tensor_tensor(out=qq[:], in0=cc[:], scalar=2.0, in1=qq[:],
                           op0=ALU.mult, op1=ALU.add)
    sbc = ss[:].unsqueeze(2).broadcast_to([128, NB, 32])
    V.tensor_tensor(out=qq[:].rearrange("p (a b) -> p a b", b=32),
                    in0=qq[:].rearrange("p (a b) -> p a b", b=32),
                    in1=sbc, op=ALU.mult)
    V.tensor_tensor(out=out_ap, in0=qq[:], in1=sg[:], op=ALU.mult)
    assert V.cnt - c0 == QUANT_VEC_OPS, (V.cnt - c0, QUANT_VEC_OPS)


def quant_bufs(nc, tag=""):
    sb = nc.alloc_sbuf_tensor
    return {
        "aa": sb("q_aa" + tag, [128, 512], F32),
        "sg": sb("q_sg" + tag, [128, 512], F32),
        "amax": sb("q_amax" + tag, [128, 16], F32),
        "rr": sb("q_rr" + tag, [128, 16], F32),
        "ss": sb("q_ss" + tag, [128, 16], F32),
        "mm": sb("q_mm" + tag, [128, 16], F32),
        "tt": sb("q_tt" + tag, [128, 16], F32),
        "xh": sb("q_xh" + tag, [128, 512], F32),
        "acc": sb("q_acc" + tag, [128, 512], F32),
        "bb": sb("q_bb" + tag, [128, 512], F32),
        "cc": sb("q_cc" + tag, [128, 512], F32),
        "qq": sb("q_qq" + tag, [128, 512], F32),
    }



def build_dequant(shapes):
    nc = bass.Bass("TRN2", target_bir_lowering=False)
    ident_d = nc.dram_tensor("ident", [128, 128], F32, kind="ExternalInput").ap()
    ins, outs, strips = {}, {}, []
    for name, R, C in shapes:
        assert R % 128 == 0 and C % 512 == 0
        ins[name] = nc.dram_tensor(name, [R, C], F32, kind="ExternalInput").ap()
        outs[name] = nc.dram_tensor(name + "_dq", [R, C], BF16, kind="ExternalOutput").ap()
        for rt in range(R // 128):
            for cs in range(C // 512):
                strips.append((name, rt, cs))
    nst = len(strips)
    VC = QUANT_VEC_OPS + 1
    SC = 3
    TC = 8

    sb = nc.alloc_sbuf_tensor
    pst = nc.alloc_psum_tensor
    es = ExitStack()
    sem = lambda n: es.enter_context(nc.semaphore(n))

    ident_s = sb("ident_s", [128, 128], F32)
    identb = sb("identb", [128, 128], BF16)
    ldb = [sb("ld0", [128, 512], F32), sb("ld1", [128, 512], F32)]
    wt = sb("wt", [128, 512], F32)
    dqt = sb("dqt", [128, 512], BF16)
    stb = [sb("st0", [128, 512], BF16), sb("st1", [128, 512], BF16)]
    qb = quant_bufs(nc)
    psA = pst("psA", [128, 512], F32)
    psB = pst("psB", [128, 512], BF16)

    smV, smS, smT = sem("smV"), sem("smS"), sem("smT")
    idm = sem("idm")
    lds = [sem("lda"), sem("ldb")]
    sts = [sem("sta"), sem("stb")]

    def v_after_dqt(i):
        return 1 + i * VC + QUANT_VEC_OPS

    def v_after_staging(i):
        return 1 + (i + 1) * VC

    with nc.Block() as block:
        @block.sync
        def _(sync):
            sync.dma_start(out=ident_s[:], in_=ident_d[:]).then_inc(idm, 16)
            for i, (name, rt, cs) in enumerate(strips):
                inst = sync.dma_start(
                    out=ldb[i % 2][:],
                    in_=ins[name][rt * 128:(rt + 1) * 128, cs * 512:(cs + 1) * 512])
                if i >= 2:
                    inst._wait_ge(smT, TC * (i - 2) + 4)
                inst.then_inc(lds[i % 2], 16)

        @block.gpsimd
        def _(g):
            for i, (name, rt, cs) in enumerate(strips):
                g.wait_ge(smV, v_after_staging(i))
                g.dma_start(out=outs[name][rt * 128:(rt + 1) * 128, cs * 512:(cs + 1) * 512],
                            in_=stb[i % 2][:]).then_inc(sts[i % 2], 16)
            for par in range(2):
                n = len([1 for i in range(nst) if i % 2 == par])
                if n:
                    g.wait_ge(sts[par], 16 * n)

        @block.scalar
        def _(s):
            S = SE(s, smS)
            for i in range(nst):
                S.w((smT, TC * i + 4))
                if i > 0:
                    S.w((smV, v_after_dqt(i - 1)))
                S.copy(out=wt[:], in_=psA[:])
                S.activation(out=qb["aa"][:], in_=wt[:], func=AF.Abs)
                S.activation(out=qb["sg"][:], in_=wt[:], func=AF.Sign)
            assert S.cnt == SC * nst

        @block.vector
        def _(v):
            V = SE(v, smV)
            V.w((idm, 16))
            V.tensor_copy(out=identb[:], in_=ident_s[:])
            for i in range(nst):
                V.w((smS, SC * i + SC))
                emit_quant_vec(V, qb, dqt[:])
                V.w((smT, TC * i + TC))
                if i >= 2:
                    V.w((sts[i % 2], 16 * (i // 2)))
                V.tensor_copy(out=stb[i % 2][:], in_=psB[:])
            assert V.cnt == 1 + VC * nst

        @block.tensor
        def _(t):
            T = SE(t, smT, self_chain=False)
            T.w((idm, 16))
            T.w((smV, 1))
            for i in range(nst):
                T.w((lds[i % 2], 16 * (i // 2 + 1)))
                if i > 0:
                    T.w((smS, SC * (i - 1) + 1))
                buf = ldb[i % 2]
                for kk in range(4):
                    T.transpose(psA[:, kk * 128:(kk + 1) * 128],
                                buf[:, kk * 128:(kk + 1) * 128], ident_s[:])
                T.w((smV, v_after_dqt(i)))
                if i > 0:
                    T.w((smV, v_after_staging(i - 1)))
                for kk in range(4):
                    T.transpose(psB[:, kk * 128:(kk + 1) * 128],
                                dqt[:, kk * 128:(kk + 1) * 128], identb[:])
            assert T.cnt == TC * nst

    return nc, [n for n, _, _ in shapes]


def ref_dequant(w):
    FP4_GRID = np.array([0.0, 0.5, 1.0, 1.5, 2.0, 3.0, 4.0, 6.0], dtype=w.dtype)
    FP4_MIDS = np.array([0.25, 0.75, 1.25, 1.75, 2.5, 3.5, 5.0], dtype=w.dtype)
    K, N = w.shape
    wb = w.reshape(K // 32, 32, N)
    amax = np.max(np.abs(wb), axis=1, keepdims=True)
    expo = np.floor(np.log2(np.maximum(amax, 1e-30))) - 2.0
    scales = np.exp2(expo)
    x = wb / scales
    idx = np.digitize(np.abs(x), FP4_MIDS)
    q = np.sign(x) * FP4_GRID[idx]
    return (q * scales).reshape(K, N)




class PEng:
    def __init__(self, name, sems, eng=None, live=False, self_chain=True):
        self.name, self.sems, self.e = name, sems, eng
        self.live, self.self_chain, self.cnt = live, self_chain, 0

    def mark(self):
        return (self.name, self.cnt)

    def w(self, dep):
        if dep is None:
            return
        nm, val = dep
        if nm == self.name or val <= 0:
            return
        if self.live:
            self.e.wait_ge(self.sems[nm], val)

    def __getattr__(self, op):
        def f(*a, **k):
            self.cnt += 1
            if self.live:
                inst = getattr(self.e, op)(*a, **k)
                if self.self_chain and self.cnt > 1:
                    inst._wait_ge(self.sems[self.name], self.cnt - 1)
                inst.then_inc(self.sems[self.name], 1)
                return inst
            return None
        return f


class PDma:
    def __init__(self, name, sems, eng=None, live=False):
        self.name, self.sems, self.e = name, sems, eng
        self.live, self.cnt = live, 0

    def dma(self, out=None, in_=None, wait=None):
        self.cnt += 1
        if self.live:
            if self.cnt > 1:
                self.e.wait_ge(self.sems[self.name], 16 * (self.cnt - 1))
            inst = self.e.dma_start(out=out, in_=in_)
            if wait is not None:
                nm, val = wait
                if val > 0:
                    inst._wait_ge(self.sems[nm], val)
            inst.then_inc(self.sems[self.name], 16)
        return (self.name, self.cnt * 16)

    def mark(self):
        return (self.name, self.cnt * 16)

    def w(self, dep):
        if dep is None:
            return
        nm, val = dep
        if nm == self.name or val <= 0:
            return
        if self.live:
            self.e.wait_ge(self.sems[nm], val)

    def drain(self):
        if self.live and self.cnt:
            self.e.wait_ge(self.sems[self.name], self.cnt * 16)


def run_passes(nc, sems, prog):
    with nc.Block() as block:
        @block.sync
        def _(sync):
            prog(PEng("T", sems), PEng("V", sems), PEng("S", sems),
                 PDma("SY", sems, sync, True), PDma("GP", sems))

        @block.gpsimd
        def _(g):
            prog(PEng("T", sems), PEng("V", sems), PEng("S", sems),
                 PDma("SY", sems), PDma("GP", sems, g, True))

        @block.scalar
        def _(s):
            prog(PEng("T", sems), PEng("V", sems), PEng("S", sems, s, True),
                 PDma("SY", sems), PDma("GP", sems))

        @block.vector
        def _(v):
            prog(PEng("T", sems), PEng("V", sems, v, True), PEng("S", sems),
                 PDma("SY", sems), PDma("GP", sems))

        @block.tensor
        def _(t):
            prog(PEng("T", sems, t, True, self_chain=False), PEng("V", sems),
                 PEng("S", sems), PDma("SY", sems), PDma("GP", sems))



def build_attn(n_tok, dbg_at=None):
    assert n_tok % 512 == 0
    NS = n_tok // 512
    nc = bass.Bass("TRN2", target_bir_lowering=False)
    dr = lambda n, s, dt=F32, k="ExternalInput": nc.dram_tensor(n, s, dt, kind=k).ap()
    xb = dr("xb", [n_tok, 2048])
    Wq_s = dr("Wq_s", [512, 2048])
    Wk_s = dr("Wk_s", [512, 2048])
    Wv_s = dr("Wv_s", [512, 2048])
    Wbeta_s = dr("Wbeta_s", [4, 2048])
    Wg1 = dr("Wg1", [128, 2048])
    Wg2_s = dr("Wg2_s", [512, 128])
    gnorm_cols = dr("gnorm_cols", [128, 4])
    ident_d = dr("ident", [128, 128])
    mask_ln_d = dr("mask_lowneg", [128, 128])
    mask_tu_d = dr("mask_triu", [128, 128], BF16)
    og_out = dr("og_out", [512, n_tok], F32, k="ExternalOutput")
    dbg_out = dr("dbg_out", [128, 512], F32, k="ExternalOutput") if dbg_at is not None else None

    sb = nc.alloc_sbuf_tensor
    pst = nc.alloc_psum_tensor
    es = ExitStack()

    ident_s = sb("ident_s", [128, 128], F32)
    identb = sb("identb", [128, 128], BF16)
    mask_ln = sb("mask_ln", [128, 128], F32)
    mask_tu = sb("mask_tu", [128, 128], BF16)
    gnc = sb("gnc", [128, 4], F32)
    wqT = sb("wqT", [128, 16 * 512], BF16)
    wkT = sb("wkT", [128, 16 * 512], BF16)
    wvT = sb("wvT", [128, 16 * 512], BF16)
    wbT = sb("wbT", [128, 16 * 4], BF16)
    wg1T = sb("wg1T", [128, 16 * 128], BF16)
    wg2T = sb("wg2T", [128, 4 * 128], BF16)
    wld = [sb("wld0", [128, 2048], F32), sb("wld1", [128, 2048], F32)]
    x_tm = [sb("x_tm0", [128, 2048], F32), sb("x_tm1", [128, 2048], F32)]
    x_cm = sb("x_cm", [128, 16 * 512], BF16)
    q_h = [sb(f"q_{h}", [128, 512], BF16) for h in range(4)]
    kr_h = [sb(f"kr_{h}", [128, 512], BF16) for h in range(4)]
    k_h = [sb(f"k_{h}", [128, 512], BF16) for h in range(4)]
    v_h = [sb(f"v_{h}", [128, 512], BF16) for h in range(4)]
    ksq = sb("ksq", [128, 512], F32)
    xg1 = sb("xg1", [128, 512], BF16)
    beta_sb = sb("beta_sb", [4, 512], F32)
    rowb = [sb(f"rowb{i}", [1, 512], F32) for i in range(3)]
    ones_col = sb("ones_col", [128, 1], F32)
    ones_row = sb("ones_row", [1, 128], F32)
    Ablk = sb("Ablk", [32, 10 * 32], F32)
    Ablk_bf = sb("Ablk_bf", [32, 6 * 32], BF16)
    W32 = [sb(f"W32_{j}", [32, 32], F32) for j in range(5)]
    Z32 = [sb(f"Z32_{j}", [32, 32], F32) for j in range(5)]
    Gg = [sb("Gg_a", [32, 32], F32), sb("Gg_c", [32, 32], F32)]
    PTb4 = sb("PTb4", [32, 4 * 128], BF16)
    Rpre = sb("Rpre", [128, 128], BF16)
    rb4 = sb("rb4", [32, 4 * 128], F32)
    Ub4 = sb("Ub4", [32, 4 * 128], BF16)
    Ktb4 = sb("Ktb4", [32, 4 * 128], BF16)
    betaT4 = sb("betaT4", [32, 16 * 4], F32)
    nbetaT4 = sb("nbetaT4", [32, 16 * 4], F32)
    S_f = [sb(f"Sf_{h}", [128, 128], F32) for h in range(4)]
    S_b = [sb(f"Sb_{h}", [128, 128], BF16) for h in range(4)]
    ogf = sb("ogf", [128, 128], F32)
    ogsq = sb("ogsq", [128, 128], F32)
    gate = sb("gate", [128, 128], BF16)
    og_st = [sb(f"og_st{h}", [128, 512], F32) for h in range(4)]
    rb_s = sb("rb_s", [1, 128], F32)
    rb_t = sb("rb_t", [1, 128], F32)
    rb_u = sb("rb_u", [1, 128], F32)
    rb_v = sb("rb_v", [1, 128], F32)

    psX = pst("psX", [128, 512], F32)
    psQ = pst("psQ", [128, 512], F32)
    psC = pst("psC", [128, 512], F32)
    psR = pst("psR", [128, 512], F32)
    psBt = pst("psBt", [128, 1024], BF16)
    psRow = pst("psRow", [128, 512], F32)

    sems = {}
    for nm in ("T", "V", "S", "SY", "GP"):
        sems[nm] = es.enter_context(nc.semaphore("sem" + nm))

    EPS = _f32(1e-6)

    def prog(T, V, S, SY, GP):
        SY.dma(ident_s[:], ident_d)
        SY.dma(mask_ln[:], mask_ln_d)
        SY.dma(mask_tu[:], mask_tu_d)
        m_cst = SY.dma(gnc[:], gnorm_cols)
        V.w(m_cst)
        V.tensor_copy(out=identb[:], in_=ident_s[:])
        V.memset(ones_col[:], 1.0)
        V.memset(ones_row[:], 1.0)
        for h in range(4):
            V.memset(S_f[h][:], 0.0)
            V.memset(S_b[h][:], 0.0)

        loads = []
        for mat in range(3):
            for m in range(4):
                loads.append(("qkv", mat, m))
        loads.append(("wg1", 0, 0))
        loads.append(("wbeta", 0, 0))
        for t4 in range(4):
            loads.append(("wg2", 0, t4))
        tmark_after_load = [None] * len(loads)
        vmark_after_load = [None] * len(loads)
        ld_marks = []
        for i, (kind, mat, m) in enumerate(loads):
            src = None
            if kind == "qkv":
                src = (Wq_s, Wk_s, Wv_s)[mat][m * 128:(m + 1) * 128, :]
                dst = wld[i % 2][:, :]
            elif kind == "wg1":
                src = Wg1
                dst = wld[i % 2][:, :]
            elif kind == "wbeta":
                src = Wbeta_s
                dst = wld[i % 2][0:4, :]
            else:
                src = Wg2_s[m * 128:(m + 1) * 128, :]
                dst = wld[i % 2][:, 0:128]
            wait = tmark_after_load[i - 2] if i >= 2 else None
            ld_marks.append(SY.dma(dst, src, wait=wait))

            T.w(ld_marks[i])
            T.w(V.mark())
            if kind == "qkv":
                dstT = (wqT, wkT, wvT)[mat]
                for g in range(4):
                    if g > 0:
                        T.w(V.mark())
                    for kk4 in range(4):
                        kk = g * 4 + kk4
                        T.transpose(psX[:, kk4 * 128:(kk4 + 1) * 128],
                                    wld[i % 2][:, kk * 128:(kk + 1) * 128], ident_s[:])
                    mt = T.mark()
                    V.w(mt)
                    for kk4 in range(4):
                        kk = g * 4 + kk4
                        V.tensor_copy(out=dstT[:, kk * 512 + m * 128: kk * 512 + m * 128 + 128],
                                      in_=psX[:, kk4 * 128:(kk4 + 1) * 128])
            elif kind == "wg1":
                for g in range(4):
                    if g > 0:
                        T.w(V.mark())
                    for kk4 in range(4):
                        kk = g * 4 + kk4
                        T.transpose(psX[:, kk4 * 128:(kk4 + 1) * 128],
                                    wld[i % 2][:, kk * 128:(kk + 1) * 128], ident_s[:])
                    mt = T.mark()
                    V.w(mt)
                    for kk4 in range(4):
                        kk = g * 4 + kk4
                        V.tensor_copy(out=wg1T[:, kk * 128:(kk + 1) * 128],
                                      in_=psX[:, kk4 * 128:(kk4 + 1) * 128])
            elif kind == "wbeta":
                for g in range(4):
                    if g > 0:
                        T.w(V.mark())
                    for kk4 in range(4):
                        kk = g * 4 + kk4
                        T.transpose(psX[:, kk4 * 128:kk4 * 128 + 4],
                                    wld[i % 2][0:4, kk * 128:(kk + 1) * 128], ident_s[0:4, 0:4])
                    mt = T.mark()
                    V.w(mt)
                    for kk4 in range(4):
                        kk = g * 4 + kk4
                        V.tensor_copy(out=wbT[:, kk * 4:(kk + 1) * 4],
                                      in_=psX[:, kk4 * 128:kk4 * 128 + 4])
            else:
                T.transpose(psX[:, 0:128], wld[i % 2][:, 0:128], ident_s[:])
                mt = T.mark()
                V.w(mt)
                V.tensor_copy(out=wg2T[:, m * 128:(m + 1) * 128], in_=psX[:, 0:128])
            tmark_after_load[i] = T.mark()
            vmark_after_load[i] = V.mark()

        xld_marks = {}
        t_after_x = {}
        for si in range(NS):
            for tt in range(4):
                gi = si * 4 + tt
                wait = t_after_x.get(gi - 2)
                xld_marks[gi] = SY.dma(x_tm[gi % 2][:],
                                       xb[gi * 128:(gi + 1) * 128, :], wait=wait)
                T.w(xld_marks[gi])
                T.w(V.mark())
                for g in range(4):
                    if g > 0:
                        T.w(V.mark())
                    for kk4 in range(4):
                        kk = g * 4 + kk4
                        T.transpose(psX[:, kk4 * 128:(kk4 + 1) * 128],
                                    x_tm[gi % 2][:, kk * 128:(kk + 1) * 128], ident_s[:])
                    mt = T.mark()
                    V.w(mt)
                    for kk4 in range(4):
                        kk = g * 4 + kk4
                        V.tensor_copy(out=x_cm[:, kk * 512 + tt * 128: kk * 512 + tt * 128 + 128],
                                      in_=psX[:, kk4 * 128:(kk4 + 1) * 128])
                t_after_x[gi] = T.mark()

            m_xcm = V.mark()

            T.w(m_xcm)
            for kk in range(16):
                T.matmul(psRow[0:4, :], wbT[:, kk * 4:(kk + 1) * 4],
                         x_cm[:, kk * 512:(kk + 1) * 512],
                         start=(kk == 0), stop=(kk == 15))
            S.w(T.mark())
            S.activation(out=beta_sb[:], in_=psRow[0:4, :], func=AF.Sigmoid)
            V.w(S.mark())
            V.tensor_scalar(out=beta_sb[:], in0=beta_sb[:], scalar1=2.0,
                            scalar2=None, op0=ALU.mult)
            T.w(V.mark())
            for j16 in range(16):
                T.transpose(psX[0:32, j16 * 4:j16 * 4 + 4],
                            beta_sb[0:4, j16 * 32:(j16 + 1) * 32], ident_s[0:4, 0:4])
            V.w(T.mark())
            V.tensor_copy(out=betaT4[:, :], in_=psX[0:32, 0:64])
            V.tensor_scalar(out=nbetaT4[:, :], in0=betaT4[:, :], scalar1=-1.0,
                            scalar2=None, op0=ALU.mult)

            for h in range(4):
                for mat, (wT, dst) in enumerate(((wqT, q_h[h]), (wkT, kr_h[h]), (wvT, v_h[h]))):
                    T.w(V.mark())
                    T.w(S.mark())
                    for kk in range(16):
                        T.matmul(psQ[:, :], wT[:, kk * 512 + h * 128: kk * 512 + h * 128 + 128],
                                 x_cm[:, kk * 512:(kk + 1) * 512],
                                 start=(kk == 0), stop=(kk == 15))
                    mt = T.mark()
                    S.w(mt)
                    if mat == 0:
                        S.activation(out=dst[:], in_=psQ[:, :], func=AF.Silu)
                    elif mat == 1:
                        S.activation(out=dst[:], in_=psQ[:, :], func=AF.Silu)
                    else:
                        S.copy(out=dst[:], in_=psQ[:, :])
                V.w(S.mark())
                V.tensor_tensor(out=ksq[:], in0=kr_h[h][:], in1=kr_h[h][:], op=ALU.mult)
                T.w(V.mark())
                T.matmul(psRow[0:1, :], ones_col[:], ksq[:], start=True, stop=True)
                V.w(T.mark())
                V.tensor_scalar(out=rowb[0][:], in0=psRow[0:1, :], scalar1=EPS,
                                scalar2=None, op0=ALU.add)
                S.w(V.mark())
                S.activation(out=rowb[1][:], in_=rowb[0][:], func=AF.Sqrt)
                V.w(S.mark())
                V.reciprocal(out=rowb[2][:], in_=rowb[1][:])
                V.tensor_tensor(out=rowb[1][:], in0=rowb[2][:], in1=rowb[2][:], op=ALU.mult)
                V.tensor_tensor(out=rowb[1][:], in0=rowb[1][:], in1=rowb[0][:], op=ALU.mult)
                V.tensor_scalar(out=rowb[1][:], in0=rowb[1][:], scalar1=-0.5,
                                scalar2=1.5, op0=ALU.mult, op1=ALU.add)
                V.tensor_tensor(out=rowb[2][:], in0=rowb[2][:], in1=rowb[1][:], op=ALU.mult)
                T.w(V.mark())
                T.matmul(psX[:, 0:512], ones_row[:], rowb[2][:], start=True, stop=True)
                V.w(T.mark())
                V.tensor_tensor(out=k_h[h][:], in0=kr_h[h][:], in1=psX[:, :], op=ALU.mult)

            T.w(V.mark())
            for kk in range(16):
                T.matmul(psQ[:, :], wg1T[:, kk * 128:(kk + 1) * 128],
                         x_cm[:, kk * 512:(kk + 1) * 512],
                         start=(kk == 0), stop=(kk == 15))
            V.w(T.mark())
            V.tensor_copy(out=xg1[:], in_=psQ[:, :])

            for c_loc in range(4):
                for h in range(4):
                    cs = slice(c_loc * 128, (c_loc + 1) * 128)
                    Kc = k_h[h][:, cs]
                    Qc = q_h[h][:, cs]
                    Vc = v_h[h][:, cs]

                    def sb32(b):
                        return slice((c_loc * 4 + b) * 32, (c_loc * 4 + b) * 32 + 32)

                    def bcol(b, neg=False):
                        t = nbetaT4 if neg else betaT4
                        j = (c_loc * 4 + b) * 4 + h
                        return t[:, j:j + 1]

                    pairs = [(b, b) for b in range(4)] + \
                            [(b, b2) for b in range(4) for b2 in range(b + 1, 4)]
                    T.w(V.mark())
                    for idx, (b, b2) in enumerate(pairs):
                        T.matmul(psQ[0:32, idx * 32:(idx + 1) * 32],
                                 Kc[:, b * 32:(b + 1) * 32], Kc[:, b2 * 32:(b2 + 1) * 32],
                                 start=True, stop=True)
                    V.w(T.mark())
                    V.tensor_copy(out=Ablk[:, :], in_=psQ[0:32, 0:320])
                    V.tensor_copy(out=Ablk_bf[:, :], in_=psQ[0:32, 128:320])
                    T.w(V.mark())
                    T.matmul(psR[:, 0:128], S_b[h][:], Kc, start=True, stop=True)
                    V.w(T.mark())
                    V.scalar_tensor_tensor(out=Rpre[:], in0=psR[:, 0:128], scalar=-1.0,
                                           in1=Vc, op0=ALU.mult, op1=ALU.add)
                    T.w(V.mark())
                    for b in range(4):
                        T.transpose(psBt[0:32, b * 128:(b + 1) * 128],
                                    Rpre[:, b * 32:(b + 1) * 32], identb[:])
                        T.transpose(psBt[0:32, 512 + b * 128:512 + (b + 1) * 128],
                                    Kc[:, b * 32:(b + 1) * 32], identb[:])
                    V.w(T.mark())
                    for b in range(4):
                        V.tensor_scalar(out=rb4[:, b * 128:(b + 1) * 128],
                                        in0=psBt[0:32, b * 128:(b + 1) * 128],
                                        scalar1=bcol(b), scalar2=None, op0=ALU.mult)
                        V.tensor_copy(out=Ktb4[:, b * 128:(b + 1) * 128],
                                      in_=psBt[0:32, 512 + b * 128:512 + (b + 1) * 128])
                    T.w(V.mark())
                    for b in range(4):
                        T.matmul(psQ[0:32, 320 + 0:320 + 128] if False else
                                 psC[0:32, b * 128:(b + 1) * 128],
                                 Kc[:, b * 32:(b + 1) * 32], Qc, start=True, stop=True)
                    V.w(T.mark())
                    for b in range(4):
                        V.tensor_tensor(out=PTb4[:, b * 128:(b + 1) * 128],
                                        in0=psC[0:32, b * 128:(b + 1) * 128],
                                        in1=mask_tu[b * 32:b * 32 + 32, :], op=ALU.mult)
                    if dbg_at == ("PT", si, c_loc, h):
                        V.tensor_copy(out=x_tm[0][0:32, 0:512], in_=PTb4[:, 0:512])
                        SY.dma(dbg_out[0:32, :], x_tm[0][0:32, 0:512], wait=V.mark())
                    for b in range(4):
                        didx = b
                        V.scalar_tensor_tensor(out=W32[0][:], in0=Ablk[:, didx * 32:(didx + 1) * 32],
                                               scalar=bcol(b), in1=mask_ln[0:32, 0:32],
                                               op0=ALU.mult, op1=ALU.mult)
                        T.w(V.mark())
                        T.transpose(psC[0:32, 448:480], W32[0][:], ident_s[0:32, 0:32])
                        V.w(T.mark())
                        V.tensor_copy(out=Z32[0][:], in_=psC[0:32, 448:480])
                        for j in range(1, 5):
                            T.w(V.mark())
                            T.matmul(psC[0:32, 448:480], Z32[j - 1][:], W32[j - 1][:], start=True, stop=True)
                            T.matmul(psC[0:32, 480:512], W32[j - 1][:], Z32[j - 1][:], start=True, stop=True)
                            V.w(T.mark())
                            V.tensor_copy(out=W32[j][:], in_=psC[0:32, 448:480])
                            V.tensor_copy(out=Z32[j][:], in_=psC[0:32, 480:512])
                        V.tensor_tensor(out=Gg[0][:], in0=Z32[0][:], in1=ident_s[0:32, 0:32], op=ALU.add)
                        for j in range(1, 5):
                            T.w(V.mark())
                            T.matmul(psC[0:32, 448:480], W32[j][:], Gg[(j - 1) % 2][:], start=True, stop=True)
                            V.w(T.mark())
                            V.tensor_tensor(out=Gg[j % 2][:], in0=psC[0:32, 448:480],
                                            in1=Gg[(j - 1) % 2][:], op=ALU.add)
                        G4 = Gg[0]
                        T.w(V.mark())
                        T.matmul(psR[0:32, 128:256], G4[:], rb4[:, b * 128:(b + 1) * 128],
                                 start=True, stop=True)
                        V.w(T.mark())
                        V.tensor_copy(out=Ub4[:, b * 128:(b + 1) * 128], in_=psR[0:32, 128:256])
                        for b2 in range(b + 1, 4):
                            pidx = 4 + [(bb, bb2) for bb in range(4) for bb2 in range(bb + 1, 4)].index((b, b2))
                            T.w(V.mark())
                            T.matmul(psR[0:32, 256:384],
                                     Ablk_bf[:, (pidx - 4) * 32:(pidx - 3) * 32],
                                     Ub4[:, b * 128:(b + 1) * 128], start=True, stop=True)
                            V.w(T.mark())
                            V.scalar_tensor_tensor(out=rb4[:, b2 * 128:(b2 + 1) * 128],
                                                   in0=psR[0:32, 256:384], scalar=bcol(b2, neg=True),
                                                   in1=rb4[:, b2 * 128:(b2 + 1) * 128],
                                                   op0=ALU.mult, op1=ALU.add)
                    if dbg_at == ("PT2", si, c_loc, h):
                        V.tensor_copy(out=x_tm[0][0:32, 0:512], in_=PTb4[:, 0:512])
                        SY.dma(dbg_out[0:32, :], x_tm[0][0:32, 0:512], wait=V.mark())
                    T.w(V.mark())
                    T.matmul(psR[:, 384:512], S_b[h][:], Qc, start=True, stop=False)
                    for b in range(4):
                        T.matmul(psR[:, 384:512], Ub4[:, b * 128:(b + 1) * 128],
                                 PTb4[:, b * 128:(b + 1) * 128],
                                 start=False, stop=(b == 3))
                    if dbg_at == (si, c_loc, h):
                        V.w(T.mark())
                        V.tensor_copy(out=x_tm[0][:, 0:128], in_=psR[:, 0:128])
                        V.tensor_copy(out=x_tm[0][:, 384:512], in_=psR[:, 384:512])
                        mdbg = V.mark()
                        SY.dma(dbg_out, x_tm[0][:, 0:512], wait=mdbg)
                    for b in range(4):
                        T.matmul(psQ[:, 384:512], Ktb4[:, b * 128:(b + 1) * 128],
                                 Ub4[:, b * 128:(b + 1) * 128],
                                 start=(b == 0), stop=(b == 3))
                    V.w(T.mark())
                    V.tensor_tensor(out=S_f[h][:], in0=S_f[h][:], in1=psQ[:, 384:512], op=ALU.add)
                    V.tensor_copy(out=S_b[h][:], in_=S_f[h][:])
                    T.matmul(psRow[:, 0:128], wg2T[:, h * 128:(h + 1) * 128],
                             xg1[:, cs], start=True, stop=True)
                    S.w(T.mark())
                    S.activation(out=gate[:], in_=psRow[:, 0:128], func=AF.Sigmoid)
                    V.w(S.mark())
                    V.tensor_tensor(out=ogf[:], in0=psR[:, 384:512], in1=gate[:], op=ALU.mult)
                    V.tensor_tensor(out=ogsq[:], in0=ogf[:], in1=ogf[:], op=ALU.mult)
                    T.w(V.mark())
                    T.matmul(psRow[0:1, 128:256], ones_col[:], ogsq[:], start=True, stop=True)
                    V.w(T.mark())
                    V.tensor_scalar(out=rb_s[:], in0=psRow[0:1, 128:256],
                                    scalar1=_f32(1.0 / 128.0), scalar2=EPS,
                                    op0=ALU.mult, op1=ALU.add)
                    S.w(V.mark())
                    S.activation(out=rb_t[:], in_=rb_s[:], func=AF.Sqrt)
                    V.w(S.mark())
                    V.reciprocal(out=rb_u[:], in_=rb_t[:])
                    V.tensor_tensor(out=rb_v[:], in0=rb_u[:], in1=rb_u[:], op=ALU.mult)
                    V.tensor_tensor(out=rb_v[:], in0=rb_v[:], in1=rb_s[:], op=ALU.mult)
                    V.tensor_scalar(out=rb_v[:], in0=rb_v[:], scalar1=-0.5,
                                    scalar2=1.5, op0=ALU.mult, op1=ALU.add)
                    V.tensor_tensor(out=rb_u[:], in0=rb_u[:], in1=rb_v[:], op=ALU.mult)
                    T.w(V.mark())
                    T.matmul(psRow[:, 256:384], ones_row[:], rb_u[0:1, :], start=True, stop=True)
                    V.w(T.mark())
                    V.tensor_tensor(out=ogf[:], in0=ogf[:], in1=psRow[:, 256:384], op=ALU.mult)
                    if si > 0 and c_loc == 0 and h == 0:
                        V.w(("GP", 16 * 4 * si))
                    V.tensor_scalar(out=og_st[h][:, cs], in0=ogf[:],
                                    scalar1=gnc[:, h:h + 1], scalar2=None, op0=ALU.mult)

            mv = V.mark()
            for h in range(4):
                GP.dma(og_out[h * 128:(h + 1) * 128, si * 512:(si + 1) * 512],
                       og_st[h][:], wait=mv)
        GP.drain()
        SY.drain()

    run_passes(nc, sems, prog)
    return nc



def build_tail(n_tok):
    assert n_tok % 512 == 0
    NH = n_tok // 512
    nc = bass.Bass("TRN2", target_bir_lowering=False)
    dr = lambda n, s, dt=F32, k="ExternalInput": nc.dram_tensor(n, s, dt, kind=k).ap()
    og_d = dr("og_full", [2048, n_tok])
    x_d = dr("x_slice", [n_tok, 2048])
    Wo_d = dr("Wo", [2048, 2048])
    up_d = dr("up_dq", [2048, 8192], BF16)
    dn_d = dr("dn_dq", [8192, 2048], BF16)
    ub_d = dr("ub_cols", [128, 64])
    db_d = dr("db_cols", [128, 16])
    ident_d = dr("ident", [128, 128])
    y_d = dr("y_out", [n_tok, 2048], F32, k="ExternalOutput")

    sb = nc.alloc_sbuf_tensor
    pst = nc.alloc_psum_tensor
    es = ExitStack()

    ident_s = sb("ident_s", [128, 128], F32)
    ub = sb("ub", [128, 64], F32)
    db = sb("db", [128, 16], F32)
    _obase = nc.bump_sbuf(65536)[0]
    h_sb = nc.alloc_sbuf_tensor_at("h_sb", [128, 64 * 512], BF16, offset=_obase)
    x_cm = nc.alloc_sbuf_tensor_at("x_cm", [128, 16 * 512], F32, offset=_obase)
    og_sb = nc.alloc_sbuf_tensor_at("og_sb", [128, 16 * 512], BF16, offset=_obase + 32768)
    x_tm = [nc.alloc_sbuf_tensor_at("x_tm0", [128, 2048], F32, offset=_obase + 49152),
            nc.alloc_sbuf_tensor_at("x_tm1", [128, 2048], F32, offset=_obase + 57344)]
    wld = x_tm
    woT = sb("woT", [128, 16 * 128], BF16)
    y_f = sb("y_f", [128, 16 * 512], F32)
    _xbase = nc.bump_sbuf(16384)[0]
    x1b = nc.alloc_sbuf_tensor_at("x1b", [128, 16 * 512], BF16, offset=_xbase)
    outb = nc.alloc_sbuf_tensor_at("outb", [128, 2048], F32, offset=_xbase + 8192)
    sqb = sb("sqb", [128, 512], F32)
    upw = [sb("upw0", [128, 16 * 256], BF16), sb("upw1", [128, 16 * 256], BF16)]
    dnw = [sb("dnw0", [128, 64 * 128], BF16), sb("dnw1", [128, 64 * 128], BF16)]
    z_f = sb("z_f", [128, 16 * 512], F32)
    rowm = sb("rowm", [1, 512], F32)
    rowq = sb("rowq", [1, 512], F32)
    rowr = sb("rowr", [1, 512], F32)
    rowt = sb("rowt", [1, 512], F32)
    ones_col = sb("ones_col", [128, 1], F32)
    ones_row = sb("ones_row", [1, 128], F32)

    psA = pst("psA", [128, 512], F32)
    psB_ = pst("psB", [128, 512], F32)
    psT = pst("psT", [128, 512], F32)
    psRow = pst("psRow", [128, 512], F32)

    sems = {}
    for nm in ("T", "V", "S", "SY", "GP"):
        sems[nm] = es.enter_context(nc.semaphore("sem" + nm))

    def ln_rows(T, V, S, src_tiles, eps):
        T.w(V.mark())
        for k in range(16):
            T.matmul(psRow[0:1, 0:512], ones_col[:], src_tiles[k],
                     start=(k == 0), stop=(k == 15))
        V.w(T.mark())
        V.tensor_scalar(out=rowm[:], in0=psRow[0:1, 0:512], scalar1=_f32(1.0 / 2048.0),
                        scalar2=None, op0=ALU.mult)
        for k in range(16):
            V.tensor_tensor(out=sqb[:], in0=src_tiles[k], in1=src_tiles[k], op=ALU.mult)
            T.w(V.mark())
            T.matmul(psRow[0:1, 0:512], ones_col[:], sqb[:],
                     start=(k == 0), stop=(k == 15))
            if k < 15:
                V.w(T.mark())
        V.w(T.mark())
        V.tensor_scalar(out=rowq[:], in0=psRow[0:1, 0:512], scalar1=_f32(1.0 / 2048.0),
                        scalar2=None, op0=ALU.mult)
        V.tensor_tensor(out=rowt[:], in0=rowm[:], in1=rowm[:], op=ALU.mult)
        V.tensor_tensor(out=rowq[:], in0=rowq[:], in1=rowt[:], op=ALU.subtract)
        V.tensor_scalar(out=rowq[:], in0=rowq[:], scalar1=_f32(eps), scalar2=None, op0=ALU.add)
        S.w(V.mark())
        S.activation(out=rowt[:], in_=rowq[:], func=AF.Sqrt)
        V.w(S.mark())
        V.reciprocal(out=rowr[:], in_=rowt[:])
        V.tensor_tensor(out=rowt[:], in0=rowr[:], in1=rowr[:], op=ALU.mult)
        V.tensor_tensor(out=rowt[:], in0=rowt[:], in1=rowq[:], op=ALU.mult)
        V.tensor_scalar(out=rowt[:], in0=rowt[:], scalar1=-0.5, scalar2=1.5,
                        op0=ALU.mult, op1=ALU.add)
        V.tensor_tensor(out=rowr[:], in0=rowr[:], in1=rowt[:], op=ALU.mult)
        T.w(V.mark())
        T.matmul(psRow[:, 0:512], ones_row[:], rowm[0:1, 0:128], start=True, stop=False)
        T.matmul(psRow[:, 0:512], ones_row[:], rowm[0:1, 0:128], start=False, stop=True)

    EPS_LN = 1e-5

    def prog(T, V, S, SY, GP):
        m_c = SY.dma(ident_s[:], ident_d)
        SY.dma(ub[:], ub_d)
        m_c2 = SY.dma(db[:], db_d)
        V.w(m_c2)
        V.memset(ones_col[:], 1.0)
        V.memset(ones_row[:], 1.0)

        for half in range(NH):
            t0 = half * 512
            for tt in range(4):
                mld = SY.dma(x_tm[tt % 2][:], x_d[t0 + tt * 128: t0 + (tt + 1) * 128, :],
                             wait=T.mark())
                T.w(mld)
                T.w(V.mark())
                for g in range(4):
                    if g > 0:
                        T.w(V.mark())
                    for kk4 in range(4):
                        kk = g * 4 + kk4
                        T.transpose(psT[:, kk4 * 128:(kk4 + 1) * 128],
                                    x_tm[tt % 2][:, kk * 128:(kk + 1) * 128], ident_s[:])
                    V.w(T.mark())
                    for kk4 in range(4):
                        kk = g * 4 + kk4
                        V.tensor_copy(out=x_cm[:, kk * 512 + tt * 128: kk * 512 + tt * 128 + 128],
                                      in_=psT[:, kk4 * 128:(kk4 + 1) * 128])
            m_og = GP.dma(og_sb[:].rearrange("p (a c) -> p a c", a=16),
                          og_d[:, t0:t0 + 512].rearrange("(a p) c -> p a c", p=128),
                          wait=T.mark())
            for m in range(16):
                mld = SY.dma(wld[m % 2][:], Wo_d[m * 128:(m + 1) * 128, :],
                             wait=T.mark())
                T.w(mld)
                T.w(V.mark())
                for g in range(4):
                    if g > 0:
                        T.w(V.mark())
                    for kk4 in range(4):
                        kk = g * 4 + kk4
                        T.transpose(psT[:, kk4 * 128:(kk4 + 1) * 128],
                                    wld[m % 2][:, kk * 128:(kk + 1) * 128], ident_s[:])
                    V.w(T.mark())
                    for kk4 in range(4):
                        kk = g * 4 + kk4
                        V.tensor_copy(out=woT[:, kk * 128:(kk + 1) * 128],
                                      in_=psT[:, kk4 * 128:(kk4 + 1) * 128])
                T.w(V.mark())
                T.w(m_og)
                for k in range(16):
                    T.matmul(psA[:, 0:512], woT[:, k * 128:(k + 1) * 128],
                             og_sb[:, k * 512:(k + 1) * 512],
                             start=(k == 0), stop=(k == 15))
                V.w(T.mark())
                V.tensor_tensor(out=y_f[:, m * 512:(m + 1) * 512], in0=psA[:, 0:512],
                                in1=x_cm[:, m * 512:(m + 1) * 512], op=ALU.add)
            ytiles = [y_f[:, k * 512:(k + 1) * 512] for k in range(16)]
            T.w(V.mark())
            for k in range(16):
                T.matmul(psRow[0:1, 0:512], ones_col[:], ytiles[k],
                         start=(k == 0), stop=(k == 15))
            V.w(T.mark())
            V.tensor_scalar(out=rowm[:], in0=psRow[0:1, 0:512],
                            scalar1=_f32(1.0 / 2048.0), scalar2=None, op0=ALU.mult)
            for k in range(16):
                V.tensor_tensor(out=sqb[:], in0=ytiles[k], in1=ytiles[k], op=ALU.mult)
                T.w(V.mark())
                T.matmul(psRow[0:1, 0:512], ones_col[:], sqb[:],
                         start=(k == 0), stop=(k == 15))
                V.w(T.mark())
            V.tensor_scalar(out=rowq[:], in0=psRow[0:1, 0:512],
                            scalar1=_f32(1.0 / 2048.0), scalar2=None, op0=ALU.mult)
            V.tensor_tensor(out=rowt[:], in0=rowm[:], in1=rowm[:], op=ALU.mult)
            V.tensor_tensor(out=rowq[:], in0=rowq[:], in1=rowt[:], op=ALU.subtract)
            V.tensor_scalar(out=rowq[:], in0=rowq[:], scalar1=_f32(EPS_LN),
                            scalar2=None, op0=ALU.add)
            S.w(V.mark())
            S.activation(out=rowt[:], in_=rowq[:], func=AF.Sqrt)
            V.w(S.mark())
            V.reciprocal(out=rowr[:], in_=rowt[:])
            V.tensor_tensor(out=rowt[:], in0=rowr[:], in1=rowr[:], op=ALU.mult)
            V.tensor_tensor(out=rowt[:], in0=rowt[:], in1=rowq[:], op=ALU.mult)
            V.tensor_scalar(out=rowt[:], in0=rowt[:], scalar1=-0.5, scalar2=1.5,
                            op0=ALU.mult, op1=ALU.add)
            V.tensor_tensor(out=rowr[:], in0=rowr[:], in1=rowt[:], op=ALU.mult)
            T.w(V.mark())
            T.matmul(psA[:, 0:512], ones_row[:], rowm[0:1, 0:512], start=True, stop=True)
            T.matmul(psB_[:, 0:512], ones_row[:], rowr[0:1, 0:512], start=True, stop=True)
            V.w(T.mark())
            for k in range(16):
                V.tensor_tensor(out=y_f[:, k * 512:(k + 1) * 512], in0=ytiles[k],
                                in1=psA[:, 0:512], op=ALU.subtract)
                V.tensor_tensor(out=y_f[:, k * 512:(k + 1) * 512], in0=ytiles[k],
                                in1=psB_[:, 0:512], op=ALU.mult)
                V.tensor_copy(out=x1b[:, k * 512:(k + 1) * 512], in_=ytiles[k])
            for mg in range(32):
                src = up_d[:, mg * 256:(mg + 1) * 256].rearrange("(a p) c -> p a c", p=128)
                mld = SY.dma(upw[mg % 2][:].rearrange("p (a c) -> p a c", a=16), src,
                             wait=T.mark())
                T.w(mld)
                T.w(S.mark())
                if mg == 0:
                    T.w(V.mark())
                for m2 in range(2):
                    m = mg * 2 + m2
                    for k in range(16):
                        T.matmul(psA[:, 0:512] if m2 == 0 else psB_[:, 0:512],
                                 upw[mg % 2][:, k * 256 + m2 * 128: k * 256 + m2 * 128 + 128],
                                 x1b[:, k * 512:(k + 1) * 512],
                                 start=(k == 0), stop=(k == 15))
                S.w(T.mark())
                for m2 in range(2):
                    m = mg * 2 + m2
                    S.activation(out=h_sb[:, m * 512:(m + 1) * 512],
                                 in_=psA[:, 0:512] if m2 == 0 else psB_[:, 0:512],
                                 func=AF.Gelu_apprx_tanh, bias=ub[:, m:m + 1])
            for m in range(16):
                src = dn_d[:, m * 128:(m + 1) * 128].rearrange("(a p) c -> p a c", p=128)
                mld = SY.dma(dnw[m % 2][:].rearrange("p (a c) -> p a c", a=64), src,
                             wait=T.mark())
                T.w(mld)
                T.w(S.mark())
                T.w(V.mark())
                for kf in range(64):
                    T.matmul(psA[:, 0:512], dnw[m % 2][:, kf * 128:(kf + 1) * 128],
                             h_sb[:, kf * 512:(kf + 1) * 512],
                             start=(kf == 0), stop=(kf == 63))
                V.w(T.mark())
                V.tensor_scalar(out=z_f[:, m * 512:(m + 1) * 512], in0=psA[:, 0:512],
                                scalar1=db[:, m:m + 1], scalar2=None, op0=ALU.add)
                V.tensor_tensor(out=z_f[:, m * 512:(m + 1) * 512],
                                in0=z_f[:, m * 512:(m + 1) * 512],
                                in1=y_f[:, m * 512:(m + 1) * 512], op=ALU.add)
            ztiles = [z_f[:, k * 512:(k + 1) * 512] for k in range(16)]
            T.w(V.mark())
            for k in range(16):
                T.matmul(psRow[0:1, 0:512], ones_col[:], ztiles[k],
                         start=(k == 0), stop=(k == 15))
            V.w(T.mark())
            V.tensor_scalar(out=rowm[:], in0=psRow[0:1, 0:512],
                            scalar1=_f32(1.0 / 2048.0), scalar2=None, op0=ALU.mult)
            for k in range(16):
                V.tensor_tensor(out=sqb[:], in0=ztiles[k], in1=ztiles[k], op=ALU.mult)
                T.w(V.mark())
                T.matmul(psRow[0:1, 0:512], ones_col[:], sqb[:],
                         start=(k == 0), stop=(k == 15))
                V.w(T.mark())
            V.tensor_scalar(out=rowq[:], in0=psRow[0:1, 0:512],
                            scalar1=_f32(1.0 / 2048.0), scalar2=None, op0=ALU.mult)
            V.tensor_tensor(out=rowt[:], in0=rowm[:], in1=rowm[:], op=ALU.mult)
            V.tensor_tensor(out=rowq[:], in0=rowq[:], in1=rowt[:], op=ALU.subtract)
            V.tensor_scalar(out=rowq[:], in0=rowq[:], scalar1=_f32(EPS_LN),
                            scalar2=None, op0=ALU.add)
            S.w(V.mark())
            S.activation(out=rowt[:], in_=rowq[:], func=AF.Sqrt)
            V.w(S.mark())
            V.reciprocal(out=rowr[:], in_=rowt[:])
            V.tensor_tensor(out=rowt[:], in0=rowr[:], in1=rowr[:], op=ALU.mult)
            V.tensor_tensor(out=rowt[:], in0=rowt[:], in1=rowq[:], op=ALU.mult)
            V.tensor_scalar(out=rowt[:], in0=rowt[:], scalar1=-0.5, scalar2=1.5,
                            op0=ALU.mult, op1=ALU.add)
            V.tensor_tensor(out=rowr[:], in0=rowr[:], in1=rowt[:], op=ALU.mult)
            T.w(V.mark())
            T.matmul(psA[:, 0:512], ones_row[:], rowm[0:1, 0:512], start=True, stop=True)
            T.matmul(psB_[:, 0:512], ones_row[:], rowr[0:1, 0:512], start=True, stop=True)
            V.w(T.mark())
            for k in range(16):
                V.tensor_tensor(out=z_f[:, k * 512:(k + 1) * 512], in0=ztiles[k],
                                in1=psA[:, 0:512], op=ALU.subtract)
                V.tensor_tensor(out=z_f[:, k * 512:(k + 1) * 512], in0=ztiles[k],
                                in1=psB_[:, 0:512], op=ALU.mult)
            for tt in range(4):
                T.w(V.mark())
                for k in range(16):
                    if k > 0 and k % 4 == 0:
                        T.w(V.mark())
                    T.transpose(psT[:, (k % 4) * 128:(k % 4 + 1) * 128],
                                z_f[:, k * 512 + tt * 128: k * 512 + tt * 128 + 128],
                                ident_s[:])
                    if k % 4 == 3:
                        V.w(T.mark())
                        for k4 in range(k - 3, k + 1):
                            V.tensor_copy(out=outb[:, k4 * 128:(k4 + 1) * 128],
                                          in_=psT[:, (k4 % 4) * 128:(k4 % 4 + 1) * 128])
                mo = V.mark()
                GP.dma(y_d[t0 + tt * 128: t0 + (tt + 1) * 128, :], outb[:], wait=mo)
                V.w(GP.mark())
        GP.drain()
        SY.drain()

    run_passes(nc, sems, prog)
    return nc




def build_fused(n_own=1024, n_pre=3072, FF=8192):
    assert n_own % 512 == 0 and n_pre % 512 == 0 and FF % 512 == 0
    NS_OWN = n_own // 512
    NS_PRE = n_pre // 512
    NF = FF // 128
    nc = bass.Bass("TRN2", target_bir_lowering=False)
    dr = lambda n, s, dt=F32, k="ExternalInput": nc.dram_tensor(n, s, dt, kind=k).ap()
    x_pre_d = dr("x_pre", [max(n_pre, 512), 2048])
    x_own_d = dr("x_own", [n_own, 2048])
    Wq_d = dr("Wq", [2048, 2048])
    Wk_d = dr("Wk", [2048, 2048])
    Wv_d = dr("Wv", [2048, 2048])
    Wb_d = dr("Wbeta", [16, 2048])
    Wg1_d = dr("Wg1", [128, 2048])
    Wg2_d = dr("Wg2", [2048, 128])
    gnc_d = dr("gnorm_cols", [128, 16])
    Wo_d = dr("Wo", [2048, 2048])
    up_d = dr("up_w", [2048, FF])
    dn_d = dr("dn_w", [FF, 2048])
    ub_d = dr("ub_cols", [128, FF // 128])
    db_d = dr("db_row", [1, 2048])
    ident_d = dr("ident", [128, 128])
    mask_ln_d = dr("mask_lowneg", [128, 128])
    mask_tu_d = dr("mask_triu", [128, 128], BF16)
    y_d = dr("y_out", [n_own, 2048], F32, k="ExternalOutput")
    di = lambda n, s, dt: nc.dram_tensor(n, s, dt, kind="Internal").ap()
    wqT_d = di("s_wqT", [2048, 2048], BF16)
    wkT_d = di("s_wkT", [2048, 2048], BF16)
    wvT_d = di("s_wvT", [2048, 2048], BF16)
    woT_d = di("s_woT", [2048, 2048], BF16)
    updq_d = di("s_updq", [2048, FF], BF16)
    dndq_d = di("s_dndq", [FF, 2048], BF16)

    sb = nc.alloc_sbuf_tensor
    sba = nc.alloc_sbuf_tensor_at
    pst = nc.alloc_psum_tensor
    es = ExitStack()

    ident_s = sb("ident_s", [128, 128], F32)
    identb = sb("identb", [128, 128], BF16)
    mask_ln = sb("mask_ln", [128, 128], F32)
    mask_tu = sb("mask_tu", [128, 128], BF16)
    gnc = sb("gnc", [128, 16], F32)
    ub = sb("ub", [128, FF // 128], F32)
    ones_col = sb("ones_col", [128, 1], F32)
    ones_row = sb("ones_row", [1, 128], F32)
    wbT = sb("wbT", [128, 16 * 16], BF16)
    wg1T = sb("wg1T", [128, 16 * 128], BF16)
    wg2T = sb("wg2T", [128, 16 * 128], BF16)
    S_f = [sb(f"Sf_{h}", [128, 128], F32) for h in range(16)]
    S_b = [sb(f"Sb_{h}", [128, 128], BF16) for h in range(16)]
    bstk = [sb(f"bstk{b}", [128, 1], F32) for b in range(4)]
    nbstk = [sb(f"nbstk{b}", [128, 1], F32) for b in range(4)]
    colm = sb("colm", [128, 1], F32)
    colq = sb("colq", [128, 1], F32)
    colr = sb("colr", [128, 1], F32)
    colt = sb("colt", [128, 1], F32)
    og_all = sb("og_all", [128, 16 * 512], BF16)

    baseA = nc.bump_sbuf(65536)[0]
    h_sb = sba("h_sb", [128, (NF // 2) * 512], BF16, offset=baseA)
    y1 = [sba(f"y1_{t}", [128, 2048], F32, offset=baseA + 32768 + t * 8192)
          for t in range(4)]
    q_h = sba("q_h", [128, 16 * 512], BF16, offset=baseA)
    kr_h = sba("kr_h", [128, 16 * 512], BF16, offset=baseA + 16384)
    k_hh = sba("k_hh", [128, 16 * 512], BF16, offset=baseA + 32768)
    v_h = sba("v_h", [128, 16 * 512], BF16, offset=baseA + 49152)
    _dw = baseA + 16384
    Nbd = [sba(f"Nbd{b}", [128, 128], F32, offset=_dw + b * 512) for b in range(4)]
    AoffBD = sba("AoffBD", [128, 6 * 128], BF16, offset=_dw + 2048)
    W32 = [sba(f"W32_{j}", [128, 128], BF16, offset=_dw + 3584 + j * 256) for j in range(5)]
    Z32 = [sba(f"Z32_{j}", [128, 128], BF16, offset=_dw + 4864 + j * 256) for j in range(5)]
    Gg = [sba("Gg_a", [128, 128], BF16, offset=_dw + 6144),
          sba("Gg_c", [128, 128], BF16, offset=_dw + 6400)]
    G4bd = [sba(f"G4bd_{b}", [128, 128], BF16, offset=_dw + 6656 + b * 256) for b in range(4)]
    rbS = [sba(f"rbS{b}", [128, 128], BF16, offset=_dw + 7680 + b * 256) for b in range(4)]
    ubS = [sba(f"ubS{b}", [128, 128], BF16, offset=_dw + 8704 + b * 256) for b in range(4)]
    RpreG = sba("RpreG", [128, 4 * 128], BF16, offset=_dw + 9728)
    UtmG = sba("UtmG", [128, 4 * 128], BF16, offset=_dw + 10752)
    KtmG = sba("KtmG", [128, 4 * 128], BF16, offset=_dw + 11776)
    PTh = sba("PTh", [128, 128], BF16, offset=_dw + 12800)
    ogf = sba("ogf", [128, 128], F32, offset=_dw + 13056)
    ogsq = sba("ogsq", [128, 128], F32, offset=_dw + 13568)
    gate = sba("gate", [128, 128], BF16, offset=_dw + 14080)

    baseB = nc.bump_sbuf(65536)[0]
    x_tm = [sba("x_tm0", [128, 2048], F32, offset=baseB),
            sba("x_tm1", [128, 2048], F32, offset=baseB + 8192)]
    _b2 = baseB + 16384
    qw_aa = sba("qw_aa", [128, 2048], F32, offset=_b2)
    qw_sg = sba("qw_sg", [128, 2048], BF16, offset=_b2 + 8192)
    qw_ac = sba("qw_ac", [128, 2048], BF16, offset=_b2 + 12288)
    qw_bb = sba("qw_bb", [128, 2048], BF16, offset=_b2 + 16384)
    qw_cc = sba("qw_cc", [128, 2048], BF16, offset=_b2 + 20480)
    qw_st = [sba("qw_st0", [128, 2048], BF16, offset=_b2 + 24576),
             sba("qw_st1", [128, 2048], BF16, offset=_b2 + 28672)]
    qc_am = sba("qc_am", [128, 64], F32, offset=_b2 + 32768)
    qc_rr = sba("qc_rr", [128, 64], F32, offset=_b2 + 33024)
    qc_ss = sba("qc_ss", [128, 64], F32, offset=_b2 + 33280)
    qc_mm = sba("qc_mm", [128, 64], F32, offset=_b2 + 33536)
    qc_tt = sba("qc_tt", [128, 64], F32, offset=_b2 + 33792)
    wstr = [sba("wstr0", [128, 16 * 128], BF16, offset=_b2),
            sba("wstr1", [128, 16 * 128], BF16, offset=_b2 + 4096)]
    rowm = sba("rowm", [1, 512], F32, offset=_b2 + 8192)
    rowq = sba("rowq", [1, 512], F32, offset=_b2 + 10240)
    rowr = sba("rowr", [1, 512], F32, offset=_b2 + 12288)
    ksq = sba("ksq", [128, 512], F32, offset=_b2 + 14336)
    worhs = [sba("worhs0", [128, 512], BF16, offset=_b2),
             sba("worhs1", [128, 512], BF16, offset=_b2 + 1024)]
    dnw = [sba("dnw0", [128, 512], BF16, offset=_b2 + 2048),
           sba("dnw1", [128, 512], BF16, offset=_b2 + 3072)]
    upw = [sba("upw0", [128, 16 * 256], BF16, offset=_b2 + 4096),
           sba("upw1", [128, 16 * 256], BF16, offset=_b2 + 12288)]
    sq_t = sba("sq_t", [128, 2048], F32, offset=_b2 + 20480)
    dbrow = sba("dbrow", [1, 2048], F32, offset=_b2 + 34304)
    beta_sb = sba("beta_sb", [16, 512], F32, offset=_b2 + 42496)
    betaT4 = sba("betaT4", [32, 16 * 16], F32, offset=_b2 + 44544)
    rb_s = sba("rb_s", [1, 128], F32, offset=_b2 + 45568)
    rb_t = sba("rb_t", [1, 128], F32, offset=_b2 + 46080)
    rb_u = sba("rb_u", [1, 128], F32, offset=_b2 + 46592)
    rb_v = sba("rb_v", [1, 128], F32, offset=_b2 + 47104)
    xg1_buf = sba("xg1_buf", [128, 512], BF16, offset=_b2 + 47616)

    baseC = nc.bump_sbuf(16384)[0]
    x_cm = sba("x_cm", [128, 16 * 512], BF16, offset=baseC)
    x1b = sba("x1b", [128, 16 * 512], BF16, offset=baseC)

    psX = pst("psX", [128, 512], F32)
    psQ = pst("psQ", [128, 512], F32)
    psC = pst("psC", [128, 512], F32)
    psR = pst("psR", [128, 512], F32)
    psBt = pst("psBt", [128, 1024], BF16)
    psRow = pst("psRow", [128, 512], F32)

    sems = {}
    for nm in ("T", "V", "S", "SY", "GP"):
        sems[nm] = es.enter_context(nc.semaphore("sem" + nm))

    EPS = _f32(1e-6)
    EPS_LN = _f32(1e-5)
    pairs_off = [(b, b2) for b in range(4) for b2 in range(b + 1, 4)]

    def prog(T, V, S, SY, GP):
        SY.dma(ident_s[:], ident_d)
        SY.dma(mask_ln[:], mask_ln_d)
        SY.dma(mask_tu[:], mask_tu_d)
        SY.dma(gnc[:], gnc_d)
        SY.dma(ub[:], ub_d)
        m_c = SY.dma(dbrow[:], db_d)
        V.w(m_c)
        V.tensor_copy(out=identb[:], in_=ident_s[:])
        V.memset(ones_col[:], 1.0)
        V.memset(ones_row[:], 1.0)
        for h in range(16):
            V.memset(S_f[h][:], 0.0)
            V.memset(S_b[h][:], 0.0)

        mld = SY.dma(x_tm[0][0:16, :], Wb_d, wait=T.mark())
        T.w(mld)
        for kk in range(16):
            T.transpose(psX[:, (kk % 4) * 128:(kk % 4) * 128 + 16],
                        x_tm[0][0:16, kk * 128:(kk + 1) * 128],
                        ident_s[0:16, 0:16])
            if kk % 4 == 3:
                V.w(T.mark())
                for k2 in range(kk - 3, kk + 1):
                    V.tensor_copy(out=wbT[:, k2 * 16:(k2 + 1) * 16],
                                  in_=psX[:, (k2 % 4) * 128:(k2 % 4) * 128 + 16])
                if kk < 15:
                    T.w(V.mark())
        mld = SY.dma(x_tm[1][:], Wg1_d, wait=T.mark())
        T.w(mld)
        T.w(V.mark())
        for g in range(4):
            if g > 0:
                T.w(V.mark())
            for k4 in range(4):
                kk = g * 4 + k4
                T.transpose(psX[:, k4 * 128:(k4 + 1) * 128],
                            x_tm[1][:, kk * 128:(kk + 1) * 128], ident_s[:])
            V.w(T.mark())
            for k4 in range(4):
                kk = g * 4 + k4
                V.tensor_copy(out=wg1T[:, kk * 128:(kk + 1) * 128],
                              in_=psX[:, k4 * 128:(k4 + 1) * 128])
        for hh in range(16):
            mld = SY.dma(x_tm[hh % 2][:, 0:128], Wg2_d[hh * 128:(hh + 1) * 128, :],
                         wait=T.mark())
            T.w(mld)
            T.w(V.mark())
            T.transpose(psX[:, 0:128], x_tm[hh % 2][:, 0:128], ident_s[:])
            V.w(T.mark())
            V.tensor_copy(out=wg2T[:, hh * 128:(hh + 1) * 128], in_=psX[:, 0:128])

        for wi, (src_d, dst_d) in enumerate(((Wq_d, wqT_d), (Wk_d, wkT_d),
                                             (Wv_d, wvT_d), (Wo_d, woT_d))):
            for m in range(16):
                i = wi * 16 + m
                mld = SY.dma(x_tm[i % 2][:], src_d[m * 128:(m + 1) * 128, :],
                             wait=T.mark())
                T.w(mld)
                T.w(V.mark())
                T.w(GP.mark())
                for g in range(4):
                    if g > 0:
                        T.w(V.mark())
                    for k4 in range(4):
                        kk = g * 4 + k4
                        T.transpose(psX[:, k4 * 128:(k4 + 1) * 128],
                                    x_tm[i % 2][:, kk * 128:(kk + 1) * 128],
                                    ident_s[:])
                    V.w(T.mark())
                    for k4 in range(4):
                        kk = g * 4 + k4
                        V.tensor_copy(out=wstr[i % 2][:, kk * 128:(kk + 1) * 128],
                                      in_=psX[:, k4 * 128:(k4 + 1) * 128])
                GP.dma(dst_d[:, m * 128:(m + 1) * 128].rearrange("(a p) c -> p a c", p=128),
                       wstr[i % 2][:].rearrange("p (a c) -> p a c", a=16),
                       wait=V.mark())

        NBQ = 64
        for src_d, dst_d, RT, CT in ((up_d, updq_d, 2048, FF),
                                     (dn_d, dndq_d, FF, 2048)):
            idx = 0
            for rt in range(RT // 128):
                for cs in range(CT // 2048):
                    par = idx % 2
                    mld = SY.dma(x_tm[par][:],
                                 src_d[rt * 128:(rt + 1) * 128, cs * 2048:(cs + 1) * 2048],
                                 wait=T.mark())
                    T.w(mld)
                    T.w(S.mark())
                    if idx == 0:
                        T.w(V.mark())
                    for g in range(4):
                        if g > 0:
                            T.w(S.mark())
                        for k4 in range(4):
                            kk = g * 4 + k4
                            T.transpose(psX[:, k4 * 128:(k4 + 1) * 128],
                                        x_tm[par][:, kk * 128:(kk + 1) * 128],
                                        ident_s[:])
                        S.w(T.mark())
                        if g == 0:
                            S.w(V.mark())
                            if idx == 0:
                                S.w(GP.mark())
                        S.activation(out=qw_aa[:, g * 512:(g + 1) * 512],
                                     in_=psX[:, 0:512], func=AF.Abs)
                        S.activation(out=qw_sg[:, g * 512:(g + 1) * 512],
                                     in_=psX[:, 0:512], func=AF.Sign)
                    V.w(S.mark())
                    V.tensor_reduce(out=qc_am[:],
                                    in_=qw_aa[:].rearrange("p (a b) -> p a b", b=32),
                                    axis=AX.X, op=ALU.max)
                    V.tensor_scalar(out=qc_mm[:], in0=qc_am[:], scalar1=_f32(1e-30),
                                    scalar2=_f32(2.0 ** 100), op0=ALU.max, op1=ALU.mult)
                    V.memset(qc_rr[:], _f32(2.0 ** 102))
                    for E in _STAGES:
                        V.tensor_scalar(out=qc_tt[:], in0=qc_mm[:], scalar1=_f32(2.0 ** E),
                                        scalar2=None, op0=ALU.is_lt)
                        V.tensor_scalar(out=qc_tt[:], in0=qc_tt[:],
                                        scalar1=_f32(1.0 - 2.0 ** (-E)),
                                        scalar2=_f32(2.0 ** (-E)), op0=ALU.mult, op1=ALU.add)
                        V.tensor_tensor(out=qc_mm[:], in0=qc_mm[:], in1=qc_tt[:], op=ALU.mult)
                        V.tensor_tensor(out=qc_rr[:], in0=qc_rr[:], in1=qc_tt[:], op=ALU.mult)
                    V.reciprocal(out=qc_ss[:], in_=qc_rr[:])
                    rbc = qc_rr[:].unsqueeze(2).broadcast_to([128, NBQ, 32])
                    V.tensor_tensor(out=qw_aa[:].rearrange("p (a b) -> p a b", b=32),
                                    in0=qw_aa[:].rearrange("p (a b) -> p a b", b=32),
                                    in1=rbc, op=ALU.mult)
                    xh = qw_aa
                    V.tensor_scalar(out=qw_ac[:], in0=xh[:], scalar1=0.25,
                                    scalar2=None, op0=ALU.is_ge)
                    for mmid in (0.75, 1.25, 1.75):
                        V.scalar_tensor_tensor(out=qw_ac[:], in0=xh[:], scalar=mmid,
                                               in1=qw_ac[:], op0=ALU.is_ge, op1=ALU.add)
                    V.tensor_scalar(out=qw_bb[:], in0=xh[:], scalar1=2.5,
                                    scalar2=None, op0=ALU.is_ge)
                    V.scalar_tensor_tensor(out=qw_bb[:], in0=xh[:], scalar=3.5,
                                           in1=qw_bb[:], op0=ALU.is_ge, op1=ALU.add)
                    V.tensor_scalar(out=qw_cc[:], in0=xh[:], scalar1=5.0,
                                    scalar2=None, op0=ALU.is_ge)
                    V.scalar_tensor_tensor(out=qw_ac[:], in0=qw_ac[:], scalar=0.5,
                                           in1=qw_bb[:], op0=ALU.mult, op1=ALU.add)
                    V.scalar_tensor_tensor(out=qw_ac[:], in0=qw_cc[:], scalar=2.0,
                                           in1=qw_ac[:], op0=ALU.mult, op1=ALU.add)
                    sbc = qc_ss[:].unsqueeze(2).broadcast_to([128, NBQ, 32])
                    V.tensor_tensor(out=qw_bb[:].rearrange("p (a b) -> p a b", b=32),
                                    in0=qw_ac[:].rearrange("p (a b) -> p a b", b=32),
                                    in1=sbc, op=ALU.mult)
                    V.tensor_tensor(out=qw_cc[:], in0=qw_bb[:], in1=qw_sg[:], op=ALU.mult)
                    T.w(V.mark())
                    for g in range(4):
                        if g > 0:
                            T.w(V.mark())
                        for k4 in range(4):
                            kk = g * 4 + k4
                            T.transpose(psBt[:, k4 * 128:(k4 + 1) * 128],
                                        qw_cc[:, kk * 128:(kk + 1) * 128], identb[:])
                        V.w(T.mark())
                        if g == 0:
                            V.w(GP.mark())
                        for k4 in range(4):
                            kk = g * 4 + k4
                            V.tensor_copy(out=qw_st[par][:, kk * 128:(kk + 1) * 128],
                                          in_=psBt[:, k4 * 128:(k4 + 1) * 128])
                    GP.dma(dst_d[rt * 128:(rt + 1) * 128, cs * 2048:(cs + 1) * 2048],
                           qw_st[par][:], wait=V.mark())
                    idx += 1

        def proj_strip(si, xsrc_d, own):
            S.w(GP.mark())
            V.w(GP.mark())
            for tt in range(4):
                mld = SY.dma(x_tm[tt % 2][:],
                             xsrc_d[si * 512 + tt * 128: si * 512 + (tt + 1) * 128, :],
                             wait=T.mark())
                T.w(mld)
                T.w(V.mark())
                for g in range(4):
                    if g > 0:
                        T.w(V.mark())
                    for k4 in range(4):
                        kk = g * 4 + k4
                        T.transpose(psX[:, k4 * 128:(k4 + 1) * 128],
                                    x_tm[tt % 2][:, kk * 128:(kk + 1) * 128],
                                    ident_s[:])
                    V.w(T.mark())
                    for k4 in range(4):
                        kk = g * 4 + k4
                        V.tensor_copy(out=x_cm[:, kk * 512 + tt * 128: kk * 512 + tt * 128 + 128],
                                      in_=psX[:, k4 * 128:(k4 + 1) * 128])
            T.w(V.mark())
            T.w(S.mark())
            for kk in range(16):
                T.matmul(psRow[0:16, :], wbT[:, kk * 16:(kk + 1) * 16],
                         x_cm[:, kk * 512:(kk + 1) * 512],
                         start=(kk == 0), stop=(kk == 15))
            S.w(T.mark())
            S.activation(out=beta_sb[:], in_=psRow[0:16, :], func=AF.Sigmoid)
            V.w(S.mark())
            V.tensor_scalar(out=beta_sb[:], in0=beta_sb[:], scalar1=2.0,
                            scalar2=None, op0=ALU.mult)
            T.w(V.mark())
            for j16 in range(16):
                T.transpose(psX[0:32, (j16 % 8) * 16:(j16 % 8) * 16 + 16],
                            beta_sb[0:16, j16 * 32:(j16 + 1) * 32],
                            ident_s[0:16, 0:16])
                if j16 % 8 == 7:
                    V.w(T.mark())
                    for j2 in range(j16 - 7, j16 + 1):
                        V.tensor_copy(out=betaT4[:, j2 * 16:(j2 + 1) * 16],
                                      in_=psX[0:32, (j2 % 8) * 16:(j2 % 8) * 16 + 16])
                    if j16 < 15:
                        T.w(V.mark())
            mats = [(wkT_d, kr_h, "silu"), (wvT_d, v_h, "copy")]
            if own:
                mats.append((wqT_d, q_h, "silu"))
            for wT_d, dst, act in mats:
                for m in range(16):
                    mld = SY.dma(wstr[m % 2][:].rearrange("p (a c) -> p a c", a=16),
                                 wT_d[:, m * 128:(m + 1) * 128].rearrange("(a p) c -> p a c", p=128),
                                 wait=T.mark())
                    T.w(mld)
                    for kk in range(16):
                        T.matmul(psQ[:, :], wstr[m % 2][:, kk * 128:(kk + 1) * 128],
                                 x_cm[:, kk * 512:(kk + 1) * 512],
                                 start=(kk == 0), stop=(kk == 15))
                    S.w(T.mark())
                    if act == "copy":
                        S.copy(out=dst[:, m * 512:(m + 1) * 512], in_=psQ[:, :])
                    else:
                        S.activation(out=dst[:, m * 512:(m + 1) * 512], in_=psQ[:, :],
                                     func=AF.Silu)
                    T.w(S.mark())
            if own:
                T.w(V.mark())
                for kk in range(16):
                    T.matmul(psQ[:, :], wg1T[:, kk * 128:(kk + 1) * 128],
                             x_cm[:, kk * 512:(kk + 1) * 512],
                             start=(kk == 0), stop=(kk == 15))
                V.w(T.mark())
                V.tensor_copy(out=xg1_buf[:], in_=psQ[:, :])
            for h in range(16):
                V.w(S.mark())
                V.tensor_tensor(out=ksq[:], in0=kr_h[:, h * 512:(h + 1) * 512],
                                in1=kr_h[:, h * 512:(h + 1) * 512], op=ALU.mult)
                T.w(V.mark())
                T.matmul(psRow[0:1, :], ones_col[:], ksq[:], start=True, stop=True)
                V.w(T.mark())
                V.tensor_scalar(out=rowm[:], in0=psRow[0:1, :], scalar1=EPS,
                                scalar2=None, op0=ALU.add)
                S.w(V.mark())
                S.activation(out=rowq[:], in_=rowm[:], func=AF.Sqrt)
                V.w(S.mark())
                V.reciprocal(out=rowr[:], in_=rowq[:])
                V.tensor_tensor(out=rowq[:], in0=rowr[:], in1=rowr[:], op=ALU.mult)
                V.tensor_tensor(out=rowq[:], in0=rowq[:], in1=rowm[:], op=ALU.mult)
                V.tensor_scalar(out=rowq[:], in0=rowq[:], scalar1=-0.5, scalar2=1.5,
                                op0=ALU.mult, op1=ALU.add)
                V.tensor_tensor(out=rowr[:], in0=rowr[:], in1=rowq[:], op=ALU.mult)
                T.w(V.mark())
                T.matmul(psX[:, 0:512], ones_row[:], rowr[0:1, :], start=True, stop=True)
                V.w(T.mark())
                V.tensor_tensor(out=k_hh[:, h * 512:(h + 1) * 512],
                                in0=kr_h[:, h * 512:(h + 1) * 512], in1=psX[:, :],
                                op=ALU.mult)

        def delta_chunk(c_loc, own):
            cs0 = c_loc * 128
            if c_loc == 0:
                V.w(S.mark())
                V.w(T.mark())
                for b in range(4):
                    V.memset(Nbd[b][:], 0.0)
                V.memset(AoffBD[:], 0.0)
            for grp in range(4):
                hs = [grp * 4 + i for i in range(4)]
                T.w(V.mark())
                T.w(S.mark())
                for i, h in enumerate(hs):
                    kk_ap = k_hh[:, h * 512 + cs0: h * 512 + cs0 + 128]
                    T.matmul(psQ[:, i * 128:(i + 1) * 128], kk_ap, kk_ap,
                             start=True, stop=True)
                V.w(T.mark())
                for b in range(4):
                    j16 = c_loc * 4 + b
                    for i, h in enumerate(hs):
                        V.tensor_copy(out=bstk[b][32 * i:32 * (i + 1), 0:1],
                                      in_=betaT4[:, j16 * 16 + h: j16 * 16 + h + 1])
                    V.tensor_scalar(out=nbstk[b][:, 0:1], in0=bstk[b][:, 0:1],
                                    scalar1=-1.0, scalar2=None, op0=ALU.mult)
                for i in range(4):
                    for b in range(4):
                        V.scalar_tensor_tensor(
                            out=Nbd[b][32 * i:32 * (i + 1), 32 * i:32 * (i + 1)],
                            in0=psQ[32 * b:32 * (b + 1),
                                    i * 128 + 32 * b: i * 128 + 32 * (b + 1)],
                            scalar=bstk[b][32 * i:32 * (i + 1), 0:1],
                            in1=mask_ln[32 * i:32 * (i + 1), 32 * i:32 * (i + 1)],
                            op0=ALU.mult, op1=ALU.mult)
                    for bi, (b, b2) in enumerate(pairs_off):
                        V.tensor_copy(
                            out=AoffBD[32 * i:32 * (i + 1),
                                       bi * 128 + 32 * i: bi * 128 + 32 * (i + 1)],
                            in_=psQ[32 * b:32 * (b + 1),
                                    i * 128 + 32 * b2: i * 128 + 32 * (b2 + 1)])
                T.w(V.mark())
                for i, h in enumerate(hs):
                    T.matmul(psR[:, i * 128:(i + 1) * 128], S_b[h][:],
                             k_hh[:, h * 512 + cs0: h * 512 + cs0 + 128],
                             start=True, stop=True)
                V.w(T.mark())
                for i, h in enumerate(hs):
                    V.scalar_tensor_tensor(out=RpreG[:, i * 128:(i + 1) * 128],
                                           in0=psR[:, i * 128:(i + 1) * 128],
                                           scalar=-1.0,
                                           in1=v_h[:, h * 512 + cs0: h * 512 + cs0 + 128],
                                           op0=ALU.mult, op1=ALU.add)
                T.w(V.mark())
                for i in range(4):
                    T.transpose(psBt[:, i * 128:(i + 1) * 128],
                                RpreG[:, i * 128:(i + 1) * 128], identb[:])
                    T.transpose(psBt[:, 512 + i * 128:512 + (i + 1) * 128],
                                k_hh[:, hs[i] * 512 + cs0: hs[i] * 512 + cs0 + 128],
                                identb[:])
                V.w(T.mark())
                for i in range(4):
                    for b in range(4):
                        j16 = c_loc * 4 + b
                        V.tensor_scalar(
                            out=rbS[b][32 * i:32 * (i + 1), :],
                            in0=psBt[32 * b:32 * (b + 1), i * 128:(i + 1) * 128],
                            scalar1=betaT4[:, j16 * 16 + hs[i]: j16 * 16 + hs[i] + 1],
                            scalar2=None, op0=ALU.mult)
                    V.tensor_copy(out=KtmG[:, i * 128:(i + 1) * 128],
                                  in_=psBt[:, 512 + i * 128:512 + (i + 1) * 128])
                for b in range(4):
                    V.tensor_copy(out=W32[0][:], in_=Nbd[b][:])
                    T.w(V.mark())
                    T.transpose(psBt[:, 0:128], W32[0][:], identb[:])
                    V.w(T.mark())
                    V.tensor_copy(out=Z32[0][:], in_=psBt[:, 0:128])
                    for j in range(1, 5):
                        T.w(V.mark())
                        T.matmul(psC[:, 0:128], Z32[j - 1][:], W32[j - 1][:],
                                 start=True, stop=True)
                        T.matmul(psC[:, 128:256], W32[j - 1][:], Z32[j - 1][:],
                                 start=True, stop=True)
                        V.w(T.mark())
                        V.tensor_copy(out=W32[j][:], in_=psC[:, 0:128])
                        V.tensor_copy(out=Z32[j][:], in_=psC[:, 128:256])
                    V.tensor_tensor(out=Gg[0][:], in0=Z32[0][:], in1=identb[:],
                                    op=ALU.add)
                    for j in range(1, 5):
                        T.w(V.mark())
                        T.matmul(psC[:, 256:384], W32[j][:], Gg[(j - 1) % 2][:],
                                 start=True, stop=True)
                        V.w(T.mark())
                        V.tensor_tensor(out=Gg[j % 2][:], in0=psC[:, 256:384],
                                        in1=Gg[(j - 1) % 2][:], op=ALU.add)
                    V.tensor_copy(out=G4bd[b][:], in_=Gg[0][:])
                for b in range(4):
                    T.w(V.mark())
                    T.matmul(psR[:, 128:256], G4bd[b][:], rbS[b][:], start=True, stop=True)
                    V.w(T.mark())
                    V.tensor_copy(out=ubS[b][:], in_=psR[:, 128:256])
                    for b2 in range(b + 1, 4):
                        bi = pairs_off.index((b, b2))
                        T.w(V.mark())
                        T.matmul(psR[:, 256:384], AoffBD[:, bi * 128:(bi + 1) * 128],
                                 ubS[b][:], start=True, stop=True)
                        V.w(T.mark())
                        V.scalar_tensor_tensor(out=rbS[b2][:], in0=psR[:, 256:384],
                                               scalar=nbstk[b2][:, 0:1], in1=rbS[b2][:],
                                               op0=ALU.mult, op1=ALU.add)
                for i in range(4):
                    for b in range(4):
                        V.tensor_copy(out=UtmG[32 * b:32 * (b + 1), i * 128:(i + 1) * 128],
                                      in_=ubS[b][32 * i:32 * (i + 1), :])
                for i, h in enumerate(hs):
                    T.w(V.mark())
                    T.w(S.mark())
                    T.matmul(psC[:, 384:512], KtmG[:, i * 128:(i + 1) * 128],
                             UtmG[:, i * 128:(i + 1) * 128], start=True, stop=True)
                    if own:
                        T.matmul(psQ[:, 0:128],
                                 k_hh[:, h * 512 + cs0: h * 512 + cs0 + 128],
                                 q_h[:, h * 512 + cs0: h * 512 + cs0 + 128],
                                 start=True, stop=True)
                    V.w(T.mark())
                    V.tensor_tensor(out=S_f[h][:], in0=S_f[h][:], in1=psC[:, 384:512],
                                    op=ALU.add)
                    if own:
                        V.tensor_tensor(out=PTh[:], in0=psQ[:, 0:128], in1=mask_tu[:],
                                        op=ALU.mult)
                        T.w(V.mark())
                        T.matmul(psR[:, 384:512], S_b[h][:],
                                 q_h[:, h * 512 + cs0: h * 512 + cs0 + 128],
                                 start=True, stop=False)
                        T.matmul(psR[:, 384:512], UtmG[:, i * 128:(i + 1) * 128],
                                 PTh[:], start=False, stop=True)
                        T.matmul(psRow[:, 0:128], wg2T[:, h * 128:(h + 1) * 128],
                                 xg1_buf[:, cs0:cs0 + 128], start=True, stop=True)
                        S.w(T.mark())
                        S.activation(out=gate[:], in_=psRow[:, 0:128], func=AF.Sigmoid)
                        V.w(S.mark())
                        V.tensor_tensor(out=ogf[:], in0=psR[:, 384:512], in1=gate[:],
                                        op=ALU.mult)
                        V.tensor_tensor(out=ogsq[:], in0=ogf[:], in1=ogf[:], op=ALU.mult)
                        T.w(V.mark())
                        T.matmul(psRow[0:1, 128:256], ones_col[:], ogsq[:],
                                 start=True, stop=True)
                        V.w(T.mark())
                        V.tensor_scalar(out=rb_s[:], in0=psRow[0:1, 128:256],
                                        scalar1=_f32(1.0 / 128.0), scalar2=EPS,
                                        op0=ALU.mult, op1=ALU.add)
                        S.w(V.mark())
                        S.activation(out=rb_t[:], in_=rb_s[:], func=AF.Sqrt)
                        V.w(S.mark())
                        V.reciprocal(out=rb_u[:], in_=rb_t[:])
                        V.tensor_tensor(out=rb_v[:], in0=rb_u[:], in1=rb_u[:], op=ALU.mult)
                        V.tensor_tensor(out=rb_v[:], in0=rb_v[:], in1=rb_s[:], op=ALU.mult)
                        V.tensor_scalar(out=rb_v[:], in0=rb_v[:], scalar1=-0.5,
                                        scalar2=1.5, op0=ALU.mult, op1=ALU.add)
                        V.tensor_tensor(out=rb_u[:], in0=rb_u[:], in1=rb_v[:], op=ALU.mult)
                        T.w(V.mark())
                        T.matmul(psRow[:, 256:384], ones_row[:], rb_u[0:1, :],
                                 start=True, stop=True)
                        V.w(T.mark())
                        V.tensor_tensor(out=ogf[:], in0=ogf[:], in1=psRow[:, 256:384],
                                        op=ALU.mult)
                        V.tensor_scalar(out=og_all[:, h * 512 + cs0: h * 512 + cs0 + 128],
                                        in0=ogf[:], scalar1=gnc[:, h:h + 1],
                                        scalar2=None, op0=ALU.mult)
                    V.tensor_copy(out=S_b[h][:], in_=S_f[h][:])

        def ln_tm(tile):
            V.tensor_reduce(out=colm[:], in_=tile[:], axis=AX.X, op=ALU.add)
            S.w(V.mark())
            S.activation(out=sq_t[:], in_=tile[:], func=AF.Square)
            V.w(S.mark())
            V.tensor_reduce(out=colq[:], in_=sq_t[:], axis=AX.X, op=ALU.add)
            V.tensor_scalar(out=colm[:], in0=colm[:], scalar1=_f32(1.0 / 2048.0),
                            scalar2=None, op0=ALU.mult)
            V.tensor_scalar(out=colq[:], in0=colq[:], scalar1=_f32(1.0 / 2048.0),
                            scalar2=None, op0=ALU.mult)
            V.tensor_tensor(out=colt[:], in0=colm[:], in1=colm[:], op=ALU.mult)
            V.tensor_tensor(out=colq[:], in0=colq[:], in1=colt[:], op=ALU.subtract)
            V.tensor_scalar(out=colq[:], in0=colq[:], scalar1=EPS_LN, scalar2=None,
                            op0=ALU.add)
            S.w(V.mark())
            S.activation(out=colt[:], in_=colq[:], func=AF.Sqrt)
            V.w(S.mark())
            V.reciprocal(out=colr[:], in_=colt[:])
            V.tensor_tensor(out=colt[:], in0=colr[:], in1=colr[:], op=ALU.mult)
            V.tensor_tensor(out=colt[:], in0=colt[:], in1=colq[:], op=ALU.mult)
            V.tensor_scalar(out=colt[:], in0=colt[:], scalar1=-0.5, scalar2=1.5,
                            op0=ALU.mult, op1=ALU.add)
            V.tensor_tensor(out=colr[:], in0=colr[:], in1=colt[:], op=ALU.mult)
            V.tensor_scalar(out=tile[:], in0=tile[:], scalar1=colm[:, 0:1],
                            scalar2=None, op0=ALU.subtract)
            V.tensor_scalar(out=tile[:], in0=tile[:], scalar1=colr[:, 0:1],
                            scalar2=None, op0=ALU.mult)

        def tail_strip(si):
            t0 = si * 512
            banks = [psX, psQ, psC, psR]
            for mc in range(4):
                for kk in range(16):
                    mld = SY.dma(worhs[kk % 2][:],
                                 woT_d[kk * 128:(kk + 1) * 128, mc * 512:(mc + 1) * 512],
                                 wait=T.mark())
                    T.w(mld)
                    if kk == 0:
                        T.w(V.mark())
                        T.w(S.mark())
                    for tt in range(4):
                        T.matmul(banks[tt][:, 0:512],
                                 og_all[:, kk * 512 + tt * 128: kk * 512 + tt * 128 + 128],
                                 worhs[kk % 2][:], start=(kk == 0), stop=(kk == 15))
                V.w(T.mark())
                for tt in range(4):
                    V.tensor_copy(out=y1[tt][:, mc * 512:(mc + 1) * 512],
                                  in_=banks[tt][:, 0:512])
            for tt in range(4):
                mld = SY.dma(x_tm[tt % 2][:],
                             x_own_d[t0 + tt * 128: t0 + (tt + 1) * 128, :],
                             wait=V.mark())
                V.w(mld)
                V.tensor_tensor(out=y1[tt][:], in0=y1[tt][:], in1=x_tm[tt % 2][:],
                                op=ALU.add)
                ln_tm(y1[tt])
                T.w(V.mark())
                for g in range(4):
                    if g > 0:
                        T.w(V.mark())
                    for k4 in range(4):
                        kk = g * 4 + k4
                        T.transpose(psX[:, k4 * 128:(k4 + 1) * 128],
                                    y1[tt][:, kk * 128:(kk + 1) * 128], ident_s[:])
                    V.w(T.mark())
                    for k4 in range(4):
                        kk = g * 4 + k4
                        V.tensor_copy(out=x1b[:, kk * 512 + tt * 128: kk * 512 + tt * 128 + 128],
                                      in_=psX[:, k4 * 128:(k4 + 1) * 128])
            NH = NF // 2
            for half in range(2):
                f0 = half * NH
                for mg in range(NH // 2):
                    mgl = f0 // 2 + mg
                    src = updq_d[:, mgl * 256:(mgl + 1) * 256].rearrange("(a p) c -> p a c", p=128)
                    mld = SY.dma(upw[mg % 2][:].rearrange("p (a c) -> p a c", a=16), src,
                                 wait=T.mark())
                    T.w(mld)
                    T.w(S.mark())
                    if mg == 0:
                        T.w(V.mark())
                    for m2 in range(2):
                        pq = psQ if m2 == 0 else psC
                        for kk in range(16):
                            T.matmul(pq[:, 0:512],
                                     upw[mg % 2][:, kk * 256 + m2 * 128: kk * 256 + m2 * 128 + 128],
                                     x1b[:, kk * 512:(kk + 1) * 512],
                                     start=(kk == 0), stop=(kk == 15))
                    S.w(T.mark())
                    for m2 in range(2):
                        m = mg * 2 + m2
                        pq = psQ if m2 == 0 else psC
                        S.activation(out=h_sb[:, m * 512:(m + 1) * 512], in_=pq[:, 0:512],
                                     func=AF.Gelu_apprx_tanh, bias=ub[:, f0 + m:f0 + m + 1])
                for mc in range(4):
                    T.w(V.mark())
                    T.w(S.mark())
                    if half == 0:
                        for tt in range(4):
                            T.matmul(banks[tt][:, 0:512], ones_row[:],
                                     dbrow[0:1, mc * 512:(mc + 1) * 512],
                                     start=True, stop=False)
                    for kf in range(NH):
                        mld = SY.dma(dnw[kf % 2][:],
                                     dndq_d[(f0 + kf) * 128:(f0 + kf + 1) * 128,
                                            mc * 512:(mc + 1) * 512],
                                     wait=T.mark())
                        T.w(mld)
                        for tt in range(4):
                            T.matmul(banks[tt][:, 0:512],
                                     h_sb[:, kf * 512 + tt * 128: kf * 512 + tt * 128 + 128],
                                     dnw[kf % 2][:],
                                     start=(half == 1 and kf == 0),
                                     stop=(kf == NH - 1))
                    V.w(T.mark())
                    for tt in range(4):
                        V.tensor_tensor(out=y1[tt][:, mc * 512:(mc + 1) * 512],
                                        in0=y1[tt][:, mc * 512:(mc + 1) * 512],
                                        in1=banks[tt][:, 0:512], op=ALU.add)
            for tt in range(4):
                ln_tm(y1[tt])
                GP.dma(y_d[t0 + tt * 128: t0 + (tt + 1) * 128, :], y1[tt][:],
                       wait=V.mark())
            V.w(GP.mark())

        for si in range(NS_PRE):
            proj_strip(si, x_pre_d, own=False)
            for c_loc in range(4):
                delta_chunk(c_loc, own=False)
        for si in range(NS_OWN):
            proj_strip(si, x_own_d, own=True)
            for c_loc in range(4):
                delta_chunk(c_loc, own=True)
            tail_strip(si)
        GP.drain()
        SY.drain()

    run_passes(nc, sems, prog)
    return nc


import ml_dtypes

_prog_cache = {}


def _program():
    if "nc" not in _prog_cache:
        _prog_cache["nc"] = build_fused(n_own=1024, n_pre=3072, FF=8192)
    return _prog_cache["nc"]


def kernel(**ins):
    from concourse.bass_utils import run_bass_kernel_spmd
    x = np.asarray(ins["L0_attn_x"], dtype=np.float32)
    ii, jj = np.indices((128, 128))
    shared = {
        "Wq": np.asarray(ins["L0_attn_Wq"], np.float32),
        "Wk": np.asarray(ins["L0_attn_Wk"], np.float32),
        "Wv": np.asarray(ins["L0_attn_Wv"], np.float32),
        "Wbeta": np.asarray(ins["L0_attn_Wbeta"], np.float32),
        "Wg1": np.asarray(ins["L0_attn_Wg1"], np.float32),
        "Wg2": np.asarray(ins["L0_attn_Wg2"], np.float32),
        "gnorm_cols": np.ascontiguousarray(
            np.asarray(ins["L0_attn_g_norm"], np.float32).reshape(16, 128).T),
        "Wo": np.asarray(ins["L0_attn_Wo"], np.float32),
        "up_w": np.asarray(ins["L0_ffn_up_kernel"], np.float32),
        "dn_w": np.asarray(ins["L0_ffn_down_kernel"], np.float32),
        "ub_cols": np.ascontiguousarray(
            np.asarray(ins["L0_ffn_up_bias"], np.float32).reshape(64, 128).T),
        "db_row": np.ascontiguousarray(
            np.asarray(ins["L0_ffn_down_bias"], np.float32).reshape(1, 2048)),
        "ident": np.eye(128, dtype=np.float32),
        "mask_lowneg": np.where(ii > jj, -1.0, 0.0).astype(np.float32),
        "mask_triu": np.where(ii <= jj, 1.0, 0.0).astype(ml_dtypes.bfloat16),
    }
    nc = _program()
    in_maps = []
    for c in range(8):
        b, p = c // 4, c % 4
        xp = np.zeros((3072, 2048), np.float32)
        if p:
            xp[3072 - 1024 * p:] = x[b, :1024 * p]
        m = dict(shared)
        m["x_pre"] = xp
        m["x_own"] = np.ascontiguousarray(x[b, 1024 * p:1024 * (p + 1)])
        in_maps.append(m)
    res = run_bass_kernel_spmd(nc, in_maps, core_ids=list(range(8)))
    out = np.empty((2, 4096, 2048), np.float32)
    for c in range(8):
        b, p = c // 4, c % 4
        out[b, 1024 * p:1024 * (p + 1)] = res.results[c]["y_out"]
    return out


# version 04
# speedup vs baseline: 1.1083x; 1.1083x over previous
import numpy as np
from contextlib import ExitStack
import concourse.bass as bass
import concourse.mybir as mybir

F32 = mybir.dt.float32
BF16 = mybir.dt.bfloat16
AF = mybir.ActivationFunctionType
ALU = mybir.AluOpType
AX = mybir.AxisListType

_STAGES = [16] * 7 + [8, 4, 2, 1]


def _f32(x):
    return float(np.float32(x))


class SE:

    def __init__(self, eng, sem, self_chain=True):
        self.e = eng
        self.sem = sem
        self.cnt = 0
        self.self_chain = self_chain

    def w(self, sem_cnt):
        sem, val = sem_cnt
        if val > 0:
            self.e.wait_ge(sem, val)

    def mark(self):
        return (self.sem, self.cnt)

    def __getattr__(self, name):
        fn = getattr(self.e, name)
        if not callable(fn):
            return fn

        def wrapped(*a, **k):
            inst = fn(*a, **k)
            if self.self_chain and self.cnt > 0:
                inst._wait_ge(self.sem, self.cnt)
            self.cnt += 1
            inst.then_inc(self.sem, 1)
            return inst

        return wrapped


QUANT_VEC_OPS = 1 + 1 + 1 + 4 * len(_STAGES) + 1 + 1 + 4 + 2 + 1 + 2 + 1 + 1


def emit_quant_vec(V, qb, out_ap):
    aa, sg = qb["aa"], qb["sg"]
    amax, rr, ss, mm, tt = qb["amax"], qb["rr"], qb["ss"], qb["mm"], qb["tt"]
    xh, acc, bb, cc, qq = qb["xh"], qb["acc"], qb["bb"], qb["cc"], qb["qq"]
    NB = 16
    c0 = V.cnt
    V.tensor_reduce(out=amax[:], in_=aa[:].rearrange("p (a b) -> p a b", b=32),
                    axis=AX.X, op=ALU.max)
    V.tensor_scalar(out=mm[:], in0=amax[:], scalar1=_f32(1e-30),
                    scalar2=_f32(2.0 ** 100), op0=ALU.max, op1=ALU.mult)
    V.memset(rr[:], _f32(2.0 ** 102))
    for E in _STAGES:
        V.tensor_scalar(out=tt[:], in0=mm[:], scalar1=_f32(2.0 ** E),
                        scalar2=None, op0=ALU.is_lt)
        V.tensor_scalar(out=tt[:], in0=tt[:], scalar1=_f32(1.0 - 2.0 ** (-E)),
                        scalar2=_f32(2.0 ** (-E)), op0=ALU.mult, op1=ALU.add)
        V.tensor_tensor(out=mm[:], in0=mm[:], in1=tt[:], op=ALU.mult)
        V.tensor_tensor(out=rr[:], in0=rr[:], in1=tt[:], op=ALU.mult)
    V.reciprocal(out=ss[:], in_=rr[:])
    rbc = rr[:].unsqueeze(2).broadcast_to([128, NB, 32])
    V.tensor_tensor(out=xh[:].rearrange("p (a b) -> p a b", b=32),
                    in0=aa[:].rearrange("p (a b) -> p a b", b=32),
                    in1=rbc, op=ALU.mult)
    V.tensor_scalar(out=acc[:], in0=xh[:], scalar1=0.25, scalar2=None, op0=ALU.is_ge)
    for m in (0.75, 1.25, 1.75):
        V.scalar_tensor_tensor(out=acc[:], in0=xh[:], scalar=m, in1=acc[:],
                               op0=ALU.is_ge, op1=ALU.add)
    V.tensor_scalar(out=bb[:], in0=xh[:], scalar1=2.5, scalar2=None, op0=ALU.is_ge)
    V.scalar_tensor_tensor(out=bb[:], in0=xh[:], scalar=3.5, in1=bb[:],
                           op0=ALU.is_ge, op1=ALU.add)
    V.tensor_scalar(out=cc[:], in0=xh[:], scalar1=5.0, scalar2=None, op0=ALU.is_ge)
    V.scalar_tensor_tensor(out=qq[:], in0=acc[:], scalar=0.5, in1=bb[:],
                           op0=ALU.mult, op1=ALU.add)
    V.scalar_tensor_tensor(out=qq[:], in0=cc[:], scalar=2.0, in1=qq[:],
                           op0=ALU.mult, op1=ALU.add)
    sbc = ss[:].unsqueeze(2).broadcast_to([128, NB, 32])
    V.tensor_tensor(out=qq[:].rearrange("p (a b) -> p a b", b=32),
                    in0=qq[:].rearrange("p (a b) -> p a b", b=32),
                    in1=sbc, op=ALU.mult)
    V.tensor_tensor(out=out_ap, in0=qq[:], in1=sg[:], op=ALU.mult)
    assert V.cnt - c0 == QUANT_VEC_OPS, (V.cnt - c0, QUANT_VEC_OPS)


def quant_bufs(nc, tag=""):
    sb = nc.alloc_sbuf_tensor
    return {
        "aa": sb("q_aa" + tag, [128, 512], F32),
        "sg": sb("q_sg" + tag, [128, 512], F32),
        "amax": sb("q_amax" + tag, [128, 16], F32),
        "rr": sb("q_rr" + tag, [128, 16], F32),
        "ss": sb("q_ss" + tag, [128, 16], F32),
        "mm": sb("q_mm" + tag, [128, 16], F32),
        "tt": sb("q_tt" + tag, [128, 16], F32),
        "xh": sb("q_xh" + tag, [128, 512], F32),
        "acc": sb("q_acc" + tag, [128, 512], F32),
        "bb": sb("q_bb" + tag, [128, 512], F32),
        "cc": sb("q_cc" + tag, [128, 512], F32),
        "qq": sb("q_qq" + tag, [128, 512], F32),
    }



def build_dequant(shapes):
    nc = bass.Bass("TRN2", target_bir_lowering=False)
    ident_d = nc.dram_tensor("ident", [128, 128], F32, kind="ExternalInput").ap()
    ins, outs, strips = {}, {}, []
    for name, R, C in shapes:
        assert R % 128 == 0 and C % 512 == 0
        ins[name] = nc.dram_tensor(name, [R, C], F32, kind="ExternalInput").ap()
        outs[name] = nc.dram_tensor(name + "_dq", [R, C], BF16, kind="ExternalOutput").ap()
        for rt in range(R // 128):
            for cs in range(C // 512):
                strips.append((name, rt, cs))
    nst = len(strips)
    VC = QUANT_VEC_OPS + 1
    SC = 3
    TC = 8

    sb = nc.alloc_sbuf_tensor
    pst = nc.alloc_psum_tensor
    es = ExitStack()
    sem = lambda n: es.enter_context(nc.semaphore(n))

    ident_s = sb("ident_s", [128, 128], F32)
    identb = sb("identb", [128, 128], BF16)
    ldb = [sb("ld0", [128, 512], F32), sb("ld1", [128, 512], F32)]
    wt = sb("wt", [128, 512], F32)
    dqt = sb("dqt", [128, 512], BF16)
    stb = [sb("st0", [128, 512], BF16), sb("st1", [128, 512], BF16)]
    qb = quant_bufs(nc)
    psA = pst("psA", [128, 512], F32)
    psB = pst("psB", [128, 512], BF16)

    smV, smS, smT = sem("smV"), sem("smS"), sem("smT")
    idm = sem("idm")
    lds = [sem("lda"), sem("ldb")]
    sts = [sem("sta"), sem("stb")]

    def v_after_dqt(i):
        return 1 + i * VC + QUANT_VEC_OPS

    def v_after_staging(i):
        return 1 + (i + 1) * VC

    with nc.Block() as block:
        @block.sync
        def _(sync):
            sync.dma_start(out=ident_s[:], in_=ident_d[:]).then_inc(idm, 16)
            for i, (name, rt, cs) in enumerate(strips):
                inst = sync.dma_start(
                    out=ldb[i % 2][:],
                    in_=ins[name][rt * 128:(rt + 1) * 128, cs * 512:(cs + 1) * 512])
                if i >= 2:
                    inst._wait_ge(smT, TC * (i - 2) + 4)
                inst.then_inc(lds[i % 2], 16)

        @block.gpsimd
        def _(g):
            for i, (name, rt, cs) in enumerate(strips):
                g.wait_ge(smV, v_after_staging(i))
                g.dma_start(out=outs[name][rt * 128:(rt + 1) * 128, cs * 512:(cs + 1) * 512],
                            in_=stb[i % 2][:]).then_inc(sts[i % 2], 16)
            for par in range(2):
                n = len([1 for i in range(nst) if i % 2 == par])
                if n:
                    g.wait_ge(sts[par], 16 * n)

        @block.scalar
        def _(s):
            S = SE(s, smS)
            for i in range(nst):
                S.w((smT, TC * i + 4))
                if i > 0:
                    S.w((smV, v_after_dqt(i - 1)))
                S.copy(out=wt[:], in_=psA[:])
                S.activation(out=qb["aa"][:], in_=wt[:], func=AF.Abs)
                S.activation(out=qb["sg"][:], in_=wt[:], func=AF.Sign)
            assert S.cnt == SC * nst

        @block.vector
        def _(v):
            V = SE(v, smV)
            V.w((idm, 16))
            V.tensor_copy(out=identb[:], in_=ident_s[:])
            for i in range(nst):
                V.w((smS, SC * i + SC))
                emit_quant_vec(V, qb, dqt[:])
                V.w((smT, TC * i + TC))
                if i >= 2:
                    V.w((sts[i % 2], 16 * (i // 2)))
                V.tensor_copy(out=stb[i % 2][:], in_=psB[:])
            assert V.cnt == 1 + VC * nst

        @block.tensor
        def _(t):
            T = SE(t, smT, self_chain=False)
            T.w((idm, 16))
            T.w((smV, 1))
            for i in range(nst):
                T.w((lds[i % 2], 16 * (i // 2 + 1)))
                if i > 0:
                    T.w((smS, SC * (i - 1) + 1))
                buf = ldb[i % 2]
                for kk in range(4):
                    T.transpose(psA[:, kk * 128:(kk + 1) * 128],
                                buf[:, kk * 128:(kk + 1) * 128], ident_s[:])
                T.w((smV, v_after_dqt(i)))
                if i > 0:
                    T.w((smV, v_after_staging(i - 1)))
                for kk in range(4):
                    T.transpose(psB[:, kk * 128:(kk + 1) * 128],
                                dqt[:, kk * 128:(kk + 1) * 128], identb[:])
            assert T.cnt == TC * nst

    return nc, [n for n, _, _ in shapes]


def ref_dequant(w):
    FP4_GRID = np.array([0.0, 0.5, 1.0, 1.5, 2.0, 3.0, 4.0, 6.0], dtype=w.dtype)
    FP4_MIDS = np.array([0.25, 0.75, 1.25, 1.75, 2.5, 3.5, 5.0], dtype=w.dtype)
    K, N = w.shape
    wb = w.reshape(K // 32, 32, N)
    amax = np.max(np.abs(wb), axis=1, keepdims=True)
    expo = np.floor(np.log2(np.maximum(amax, 1e-30))) - 2.0
    scales = np.exp2(expo)
    x = wb / scales
    idx = np.digitize(np.abs(x), FP4_MIDS)
    q = np.sign(x) * FP4_GRID[idx]
    return (q * scales).reshape(K, N)




class PEng:
    def __init__(self, name, sems, eng=None, live=False, self_chain=True):
        self.name, self.sems, self.e = name, sems, eng
        self.live, self.self_chain, self.cnt = live, self_chain, 0

    def mark(self):
        return (self.name, self.cnt)

    def w(self, dep):
        if dep is None:
            return
        nm, val = dep
        if nm == self.name or val <= 0:
            return
        if self.live:
            self.e.wait_ge(self.sems[nm], val)

    def __getattr__(self, op):
        def f(*a, **k):
            self.cnt += 1
            if self.live:
                inst = getattr(self.e, op)(*a, **k)
                if self.self_chain and self.cnt > 1:
                    inst._wait_ge(self.sems[self.name], self.cnt - 1)
                inst.then_inc(self.sems[self.name], 1)
                return inst
            return None
        return f


class PDma:
    def __init__(self, name, sems, eng=None, live=False):
        self.name, self.sems, self.e = name, sems, eng
        self.live, self.cnt = live, 0

    def dma(self, out=None, in_=None, wait=None):
        self.cnt += 1
        if self.live:
            if self.cnt > 1:
                self.e.wait_ge(self.sems[self.name], 16 * (self.cnt - 1))
            inst = self.e.dma_start(out=out, in_=in_)
            if wait is not None:
                nm, val = wait
                if val > 0:
                    inst._wait_ge(self.sems[nm], val)
            inst.then_inc(self.sems[self.name], 16)
        return (self.name, self.cnt * 16)

    def mark(self):
        return (self.name, self.cnt * 16)

    def w(self, dep):
        if dep is None:
            return
        nm, val = dep
        if nm == self.name or val <= 0:
            return
        if self.live:
            self.e.wait_ge(self.sems[nm], val)

    def drain(self):
        if self.live and self.cnt:
            self.e.wait_ge(self.sems[self.name], self.cnt * 16)


def run_passes(nc, sems, prog):
    with nc.Block() as block:
        @block.sync
        def _(sync):
            prog(PEng("T", sems), PEng("V", sems), PEng("S", sems),
                 PDma("SY", sems, sync, True), PDma("GP", sems))

        @block.gpsimd
        def _(g):
            prog(PEng("T", sems), PEng("V", sems), PEng("S", sems),
                 PDma("SY", sems), PDma("GP", sems, g, True))

        @block.scalar
        def _(s):
            prog(PEng("T", sems), PEng("V", sems), PEng("S", sems, s, True),
                 PDma("SY", sems), PDma("GP", sems))

        @block.vector
        def _(v):
            prog(PEng("T", sems), PEng("V", sems, v, True), PEng("S", sems),
                 PDma("SY", sems), PDma("GP", sems))

        @block.tensor
        def _(t):
            prog(PEng("T", sems, t, True, self_chain=False), PEng("V", sems),
                 PEng("S", sems), PDma("SY", sems), PDma("GP", sems))



def build_attn(n_tok, dbg_at=None):
    assert n_tok % 512 == 0
    NS = n_tok // 512
    nc = bass.Bass("TRN2", target_bir_lowering=False)
    dr = lambda n, s, dt=F32, k="ExternalInput": nc.dram_tensor(n, s, dt, kind=k).ap()
    xb = dr("xb", [n_tok, 2048])
    Wq_s = dr("Wq_s", [512, 2048])
    Wk_s = dr("Wk_s", [512, 2048])
    Wv_s = dr("Wv_s", [512, 2048])
    Wbeta_s = dr("Wbeta_s", [4, 2048])
    Wg1 = dr("Wg1", [128, 2048])
    Wg2_s = dr("Wg2_s", [512, 128])
    gnorm_cols = dr("gnorm_cols", [128, 4])
    ident_d = dr("ident", [128, 128])
    mask_ln_d = dr("mask_lowneg", [128, 128])
    mask_tu_d = dr("mask_triu", [128, 128], BF16)
    og_out = dr("og_out", [512, n_tok], F32, k="ExternalOutput")
    dbg_out = dr("dbg_out", [128, 512], F32, k="ExternalOutput") if dbg_at is not None else None

    sb = nc.alloc_sbuf_tensor
    pst = nc.alloc_psum_tensor
    es = ExitStack()

    ident_s = sb("ident_s", [128, 128], F32)
    identb = sb("identb", [128, 128], BF16)
    mask_ln = sb("mask_ln", [128, 128], F32)
    mask_tu = sb("mask_tu", [128, 128], BF16)
    gnc = sb("gnc", [128, 4], F32)
    wqT = sb("wqT", [128, 16 * 512], BF16)
    wkT = sb("wkT", [128, 16 * 512], BF16)
    wvT = sb("wvT", [128, 16 * 512], BF16)
    wbT = sb("wbT", [128, 16 * 4], BF16)
    wg1T = sb("wg1T", [128, 16 * 128], BF16)
    wg2T = sb("wg2T", [128, 4 * 128], BF16)
    wld = [sb("wld0", [128, 2048], F32), sb("wld1", [128, 2048], F32)]
    x_tm = [sb("x_tm0", [128, 2048], F32), sb("x_tm1", [128, 2048], F32)]
    x_cm = sb("x_cm", [128, 16 * 512], BF16)
    q_h = [sb(f"q_{h}", [128, 512], BF16) for h in range(4)]
    kr_h = [sb(f"kr_{h}", [128, 512], BF16) for h in range(4)]
    k_h = [sb(f"k_{h}", [128, 512], BF16) for h in range(4)]
    v_h = [sb(f"v_{h}", [128, 512], BF16) for h in range(4)]
    ksq = sb("ksq", [128, 512], F32)
    xg1 = sb("xg1", [128, 512], BF16)
    beta_sb = sb("beta_sb", [4, 512], F32)
    rowb = [sb(f"rowb{i}", [1, 512], F32) for i in range(3)]
    ones_col = sb("ones_col", [128, 1], F32)
    ones_row = sb("ones_row", [1, 128], F32)
    Ablk = sb("Ablk", [32, 10 * 32], F32)
    Ablk_bf = sb("Ablk_bf", [32, 6 * 32], BF16)
    W32 = [sb(f"W32_{j}", [32, 32], F32) for j in range(5)]
    Z32 = [sb(f"Z32_{j}", [32, 32], F32) for j in range(5)]
    Gg = [sb("Gg_a", [32, 32], F32), sb("Gg_c", [32, 32], F32)]
    PTb4 = sb("PTb4", [32, 4 * 128], BF16)
    Rpre = sb("Rpre", [128, 128], BF16)
    rb4 = sb("rb4", [32, 4 * 128], F32)
    Ub4 = sb("Ub4", [32, 4 * 128], BF16)
    Ktb4 = sb("Ktb4", [32, 4 * 128], BF16)
    betaT4 = sb("betaT4", [32, 16 * 4], F32)
    nbetaT4 = sb("nbetaT4", [32, 16 * 4], F32)
    S_f = [sb(f"Sf_{h}", [128, 128], F32) for h in range(4)]
    S_b = [sb(f"Sb_{h}", [128, 128], BF16) for h in range(4)]
    ogf = sb("ogf", [128, 128], F32)
    ogsq = sb("ogsq", [128, 128], F32)
    gate = sb("gate", [128, 128], BF16)
    og_st = [sb(f"og_st{h}", [128, 512], F32) for h in range(4)]
    rb_s = sb("rb_s", [1, 128], F32)
    rb_t = sb("rb_t", [1, 128], F32)
    rb_u = sb("rb_u", [1, 128], F32)
    rb_v = sb("rb_v", [1, 128], F32)

    psX = pst("psX", [128, 512], F32)
    psQ = pst("psQ", [128, 512], F32)
    psC = pst("psC", [128, 512], F32)
    psR = pst("psR", [128, 512], F32)
    psBt = pst("psBt", [128, 1024], BF16)
    psRow = pst("psRow", [128, 512], F32)

    sems = {}
    for nm in ("T", "V", "S", "SY", "GP"):
        sems[nm] = es.enter_context(nc.semaphore("sem" + nm))

    EPS = _f32(1e-6)

    def prog(T, V, S, SY, GP):
        SY.dma(ident_s[:], ident_d)
        SY.dma(mask_ln[:], mask_ln_d)
        SY.dma(mask_tu[:], mask_tu_d)
        m_cst = SY.dma(gnc[:], gnorm_cols)
        V.w(m_cst)
        V.tensor_copy(out=identb[:], in_=ident_s[:])
        V.memset(ones_col[:], 1.0)
        V.memset(ones_row[:], 1.0)
        for h in range(4):
            V.memset(S_f[h][:], 0.0)
            V.memset(S_b[h][:], 0.0)

        loads = []
        for mat in range(3):
            for m in range(4):
                loads.append(("qkv", mat, m))
        loads.append(("wg1", 0, 0))
        loads.append(("wbeta", 0, 0))
        for t4 in range(4):
            loads.append(("wg2", 0, t4))
        tmark_after_load = [None] * len(loads)
        vmark_after_load = [None] * len(loads)
        ld_marks = []
        for i, (kind, mat, m) in enumerate(loads):
            src = None
            if kind == "qkv":
                src = (Wq_s, Wk_s, Wv_s)[mat][m * 128:(m + 1) * 128, :]
                dst = wld[i % 2][:, :]
            elif kind == "wg1":
                src = Wg1
                dst = wld[i % 2][:, :]
            elif kind == "wbeta":
                src = Wbeta_s
                dst = wld[i % 2][0:4, :]
            else:
                src = Wg2_s[m * 128:(m + 1) * 128, :]
                dst = wld[i % 2][:, 0:128]
            wait = tmark_after_load[i - 2] if i >= 2 else None
            ld_marks.append(SY.dma(dst, src, wait=wait))

            T.w(ld_marks[i])
            T.w(V.mark())
            if kind == "qkv":
                dstT = (wqT, wkT, wvT)[mat]
                for g in range(4):
                    if g > 0:
                        T.w(V.mark())
                    for kk4 in range(4):
                        kk = g * 4 + kk4
                        T.transpose(psX[:, kk4 * 128:(kk4 + 1) * 128],
                                    wld[i % 2][:, kk * 128:(kk + 1) * 128], ident_s[:])
                    mt = T.mark()
                    V.w(mt)
                    for kk4 in range(4):
                        kk = g * 4 + kk4
                        V.tensor_copy(out=dstT[:, kk * 512 + m * 128: kk * 512 + m * 128 + 128],
                                      in_=psX[:, kk4 * 128:(kk4 + 1) * 128])
            elif kind == "wg1":
                for g in range(4):
                    if g > 0:
                        T.w(V.mark())
                    for kk4 in range(4):
                        kk = g * 4 + kk4
                        T.transpose(psX[:, kk4 * 128:(kk4 + 1) * 128],
                                    wld[i % 2][:, kk * 128:(kk + 1) * 128], ident_s[:])
                    mt = T.mark()
                    V.w(mt)
                    for kk4 in range(4):
                        kk = g * 4 + kk4
                        V.tensor_copy(out=wg1T[:, kk * 128:(kk + 1) * 128],
                                      in_=psX[:, kk4 * 128:(kk4 + 1) * 128])
            elif kind == "wbeta":
                for g in range(4):
                    if g > 0:
                        T.w(V.mark())
                    for kk4 in range(4):
                        kk = g * 4 + kk4
                        T.transpose(psX[:, kk4 * 128:kk4 * 128 + 4],
                                    wld[i % 2][0:4, kk * 128:(kk + 1) * 128], ident_s[0:4, 0:4])
                    mt = T.mark()
                    V.w(mt)
                    for kk4 in range(4):
                        kk = g * 4 + kk4
                        V.tensor_copy(out=wbT[:, kk * 4:(kk + 1) * 4],
                                      in_=psX[:, kk4 * 128:kk4 * 128 + 4])
            else:
                T.transpose(psX[:, 0:128], wld[i % 2][:, 0:128], ident_s[:])
                mt = T.mark()
                V.w(mt)
                V.tensor_copy(out=wg2T[:, m * 128:(m + 1) * 128], in_=psX[:, 0:128])
            tmark_after_load[i] = T.mark()
            vmark_after_load[i] = V.mark()

        xld_marks = {}
        t_after_x = {}
        for si in range(NS):
            for tt in range(4):
                gi = si * 4 + tt
                wait = t_after_x.get(gi - 2)
                xld_marks[gi] = SY.dma(x_tm[gi % 2][:],
                                       xb[gi * 128:(gi + 1) * 128, :], wait=wait)
                T.w(xld_marks[gi])
                T.w(V.mark())
                for g in range(4):
                    if g > 0:
                        T.w(V.mark())
                    for kk4 in range(4):
                        kk = g * 4 + kk4
                        T.transpose(psX[:, kk4 * 128:(kk4 + 1) * 128],
                                    x_tm[gi % 2][:, kk * 128:(kk + 1) * 128], ident_s[:])
                    mt = T.mark()
                    V.w(mt)
                    for kk4 in range(4):
                        kk = g * 4 + kk4
                        V.tensor_copy(out=x_cm[:, kk * 512 + tt * 128: kk * 512 + tt * 128 + 128],
                                      in_=psX[:, kk4 * 128:(kk4 + 1) * 128])
                t_after_x[gi] = T.mark()

            m_xcm = V.mark()

            T.w(m_xcm)
            for kk in range(16):
                T.matmul(psRow[0:4, :], wbT[:, kk * 4:(kk + 1) * 4],
                         x_cm[:, kk * 512:(kk + 1) * 512],
                         start=(kk == 0), stop=(kk == 15))
            S.w(T.mark())
            S.activation(out=beta_sb[:], in_=psRow[0:4, :], func=AF.Sigmoid)
            V.w(S.mark())
            V.tensor_scalar(out=beta_sb[:], in0=beta_sb[:], scalar1=2.0,
                            scalar2=None, op0=ALU.mult)
            T.w(V.mark())
            for j16 in range(16):
                T.transpose(psX[0:32, j16 * 4:j16 * 4 + 4],
                            beta_sb[0:4, j16 * 32:(j16 + 1) * 32], ident_s[0:4, 0:4])
            V.w(T.mark())
            V.tensor_copy(out=betaT4[:, :], in_=psX[0:32, 0:64])
            V.tensor_scalar(out=nbetaT4[:, :], in0=betaT4[:, :], scalar1=-1.0,
                            scalar2=None, op0=ALU.mult)

            for h in range(4):
                for mat, (wT, dst) in enumerate(((wqT, q_h[h]), (wkT, kr_h[h]), (wvT, v_h[h]))):
                    T.w(V.mark())
                    T.w(S.mark())
                    for kk in range(16):
                        T.matmul(psQ[:, :], wT[:, kk * 512 + h * 128: kk * 512 + h * 128 + 128],
                                 x_cm[:, kk * 512:(kk + 1) * 512],
                                 start=(kk == 0), stop=(kk == 15))
                    mt = T.mark()
                    S.w(mt)
                    if mat == 0:
                        S.activation(out=dst[:], in_=psQ[:, :], func=AF.Silu)
                    elif mat == 1:
                        S.activation(out=dst[:], in_=psQ[:, :], func=AF.Silu)
                    else:
                        S.copy(out=dst[:], in_=psQ[:, :])
                V.w(S.mark())
                V.tensor_tensor(out=ksq[:], in0=kr_h[h][:], in1=kr_h[h][:], op=ALU.mult)
                T.w(V.mark())
                T.matmul(psRow[0:1, :], ones_col[:], ksq[:], start=True, stop=True)
                V.w(T.mark())
                V.tensor_scalar(out=rowb[0][:], in0=psRow[0:1, :], scalar1=EPS,
                                scalar2=None, op0=ALU.add)
                S.w(V.mark())
                S.activation(out=rowb[1][:], in_=rowb[0][:], func=AF.Sqrt)
                V.w(S.mark())
                V.reciprocal(out=rowb[2][:], in_=rowb[1][:])
                V.tensor_tensor(out=rowb[1][:], in0=rowb[2][:], in1=rowb[2][:], op=ALU.mult)
                V.tensor_tensor(out=rowb[1][:], in0=rowb[1][:], in1=rowb[0][:], op=ALU.mult)
                V.tensor_scalar(out=rowb[1][:], in0=rowb[1][:], scalar1=-0.5,
                                scalar2=1.5, op0=ALU.mult, op1=ALU.add)
                V.tensor_tensor(out=rowb[2][:], in0=rowb[2][:], in1=rowb[1][:], op=ALU.mult)
                T.w(V.mark())
                T.matmul(psX[:, 0:512], ones_row[:], rowb[2][:], start=True, stop=True)
                V.w(T.mark())
                V.tensor_tensor(out=k_h[h][:], in0=kr_h[h][:], in1=psX[:, :], op=ALU.mult)

            T.w(V.mark())
            for kk in range(16):
                T.matmul(psQ[:, :], wg1T[:, kk * 128:(kk + 1) * 128],
                         x_cm[:, kk * 512:(kk + 1) * 512],
                         start=(kk == 0), stop=(kk == 15))
            V.w(T.mark())
            V.tensor_copy(out=xg1[:], in_=psQ[:, :])

            for c_loc in range(4):
                for h in range(4):
                    cs = slice(c_loc * 128, (c_loc + 1) * 128)
                    Kc = k_h[h][:, cs]
                    Qc = q_h[h][:, cs]
                    Vc = v_h[h][:, cs]

                    def sb32(b):
                        return slice((c_loc * 4 + b) * 32, (c_loc * 4 + b) * 32 + 32)

                    def bcol(b, neg=False):
                        t = nbetaT4 if neg else betaT4
                        j = (c_loc * 4 + b) * 4 + h
                        return t[:, j:j + 1]

                    pairs = [(b, b) for b in range(4)] + \
                            [(b, b2) for b in range(4) for b2 in range(b + 1, 4)]
                    T.w(V.mark())
                    for idx, (b, b2) in enumerate(pairs):
                        T.matmul(psQ[0:32, idx * 32:(idx + 1) * 32],
                                 Kc[:, b * 32:(b + 1) * 32], Kc[:, b2 * 32:(b2 + 1) * 32],
                                 start=True, stop=True)
                    V.w(T.mark())
                    V.tensor_copy(out=Ablk[:, :], in_=psQ[0:32, 0:320])
                    V.tensor_copy(out=Ablk_bf[:, :], in_=psQ[0:32, 128:320])
                    T.w(V.mark())
                    T.matmul(psR[:, 0:128], S_b[h][:], Kc, start=True, stop=True)
                    V.w(T.mark())
                    V.scalar_tensor_tensor(out=Rpre[:], in0=psR[:, 0:128], scalar=-1.0,
                                           in1=Vc, op0=ALU.mult, op1=ALU.add)
                    T.w(V.mark())
                    for b in range(4):
                        T.transpose(psBt[0:32, b * 128:(b + 1) * 128],
                                    Rpre[:, b * 32:(b + 1) * 32], identb[:])
                        T.transpose(psBt[0:32, 512 + b * 128:512 + (b + 1) * 128],
                                    Kc[:, b * 32:(b + 1) * 32], identb[:])
                    V.w(T.mark())
                    for b in range(4):
                        V.tensor_scalar(out=rb4[:, b * 128:(b + 1) * 128],
                                        in0=psBt[0:32, b * 128:(b + 1) * 128],
                                        scalar1=bcol(b), scalar2=None, op0=ALU.mult)
                        V.tensor_copy(out=Ktb4[:, b * 128:(b + 1) * 128],
                                      in_=psBt[0:32, 512 + b * 128:512 + (b + 1) * 128])
                    T.w(V.mark())
                    for b in range(4):
                        T.matmul(psQ[0:32, 320 + 0:320 + 128] if False else
                                 psC[0:32, b * 128:(b + 1) * 128],
                                 Kc[:, b * 32:(b + 1) * 32], Qc, start=True, stop=True)
                    V.w(T.mark())
                    for b in range(4):
                        V.tensor_tensor(out=PTb4[:, b * 128:(b + 1) * 128],
                                        in0=psC[0:32, b * 128:(b + 1) * 128],
                                        in1=mask_tu[b * 32:b * 32 + 32, :], op=ALU.mult)
                    if dbg_at == ("PT", si, c_loc, h):
                        V.tensor_copy(out=x_tm[0][0:32, 0:512], in_=PTb4[:, 0:512])
                        SY.dma(dbg_out[0:32, :], x_tm[0][0:32, 0:512], wait=V.mark())
                    for b in range(4):
                        didx = b
                        V.scalar_tensor_tensor(out=W32[0][:], in0=Ablk[:, didx * 32:(didx + 1) * 32],
                                               scalar=bcol(b), in1=mask_ln[0:32, 0:32],
                                               op0=ALU.mult, op1=ALU.mult)
                        T.w(V.mark())
                        T.transpose(psC[0:32, 448:480], W32[0][:], ident_s[0:32, 0:32])
                        V.w(T.mark())
                        V.tensor_copy(out=Z32[0][:], in_=psC[0:32, 448:480])
                        for j in range(1, 5):
                            T.w(V.mark())
                            T.matmul(psC[0:32, 448:480], Z32[j - 1][:], W32[j - 1][:], start=True, stop=True)
                            T.matmul(psC[0:32, 480:512], W32[j - 1][:], Z32[j - 1][:], start=True, stop=True)
                            V.w(T.mark())
                            V.tensor_copy(out=W32[j][:], in_=psC[0:32, 448:480])
                            V.tensor_copy(out=Z32[j][:], in_=psC[0:32, 480:512])
                        V.tensor_tensor(out=Gg[0][:], in0=Z32[0][:], in1=ident_s[0:32, 0:32], op=ALU.add)
                        for j in range(1, 5):
                            T.w(V.mark())
                            T.matmul(psC[0:32, 448:480], W32[j][:], Gg[(j - 1) % 2][:], start=True, stop=True)
                            V.w(T.mark())
                            V.tensor_tensor(out=Gg[j % 2][:], in0=psC[0:32, 448:480],
                                            in1=Gg[(j - 1) % 2][:], op=ALU.add)
                        G4 = Gg[0]
                        T.w(V.mark())
                        T.matmul(psR[0:32, 128:256], G4[:], rb4[:, b * 128:(b + 1) * 128],
                                 start=True, stop=True)
                        V.w(T.mark())
                        V.tensor_copy(out=Ub4[:, b * 128:(b + 1) * 128], in_=psR[0:32, 128:256])
                        for b2 in range(b + 1, 4):
                            pidx = 4 + [(bb, bb2) for bb in range(4) for bb2 in range(bb + 1, 4)].index((b, b2))
                            T.w(V.mark())
                            T.matmul(psR[0:32, 256:384],
                                     Ablk_bf[:, (pidx - 4) * 32:(pidx - 3) * 32],
                                     Ub4[:, b * 128:(b + 1) * 128], start=True, stop=True)
                            V.w(T.mark())
                            V.scalar_tensor_tensor(out=rb4[:, b2 * 128:(b2 + 1) * 128],
                                                   in0=psR[0:32, 256:384], scalar=bcol(b2, neg=True),
                                                   in1=rb4[:, b2 * 128:(b2 + 1) * 128],
                                                   op0=ALU.mult, op1=ALU.add)
                    if dbg_at == ("PT2", si, c_loc, h):
                        V.tensor_copy(out=x_tm[0][0:32, 0:512], in_=PTb4[:, 0:512])
                        SY.dma(dbg_out[0:32, :], x_tm[0][0:32, 0:512], wait=V.mark())
                    T.w(V.mark())
                    T.matmul(psR[:, 384:512], S_b[h][:], Qc, start=True, stop=False)
                    for b in range(4):
                        T.matmul(psR[:, 384:512], Ub4[:, b * 128:(b + 1) * 128],
                                 PTb4[:, b * 128:(b + 1) * 128],
                                 start=False, stop=(b == 3))
                    if dbg_at == (si, c_loc, h):
                        V.w(T.mark())
                        V.tensor_copy(out=x_tm[0][:, 0:128], in_=psR[:, 0:128])
                        V.tensor_copy(out=x_tm[0][:, 384:512], in_=psR[:, 384:512])
                        mdbg = V.mark()
                        SY.dma(dbg_out, x_tm[0][:, 0:512], wait=mdbg)
                    for b in range(4):
                        T.matmul(psQ[:, 384:512], Ktb4[:, b * 128:(b + 1) * 128],
                                 Ub4[:, b * 128:(b + 1) * 128],
                                 start=(b == 0), stop=(b == 3))
                    V.w(T.mark())
                    V.tensor_tensor(out=S_f[h][:], in0=S_f[h][:], in1=psQ[:, 384:512], op=ALU.add)
                    V.tensor_copy(out=S_b[h][:], in_=S_f[h][:])
                    T.matmul(psRow[:, 0:128], wg2T[:, h * 128:(h + 1) * 128],
                             xg1[:, cs], start=True, stop=True)
                    S.w(T.mark())
                    S.activation(out=gate[:], in_=psRow[:, 0:128], func=AF.Sigmoid)
                    V.w(S.mark())
                    V.tensor_tensor(out=ogf[:], in0=psR[:, 384:512], in1=gate[:], op=ALU.mult)
                    V.tensor_tensor(out=ogsq[:], in0=ogf[:], in1=ogf[:], op=ALU.mult)
                    T.w(V.mark())
                    T.matmul(psRow[0:1, 128:256], ones_col[:], ogsq[:], start=True, stop=True)
                    V.w(T.mark())
                    V.tensor_scalar(out=rb_s[:], in0=psRow[0:1, 128:256],
                                    scalar1=_f32(1.0 / 128.0), scalar2=EPS,
                                    op0=ALU.mult, op1=ALU.add)
                    S.w(V.mark())
                    S.activation(out=rb_t[:], in_=rb_s[:], func=AF.Sqrt)
                    V.w(S.mark())
                    V.reciprocal(out=rb_u[:], in_=rb_t[:])
                    V.tensor_tensor(out=rb_v[:], in0=rb_u[:], in1=rb_u[:], op=ALU.mult)
                    V.tensor_tensor(out=rb_v[:], in0=rb_v[:], in1=rb_s[:], op=ALU.mult)
                    V.tensor_scalar(out=rb_v[:], in0=rb_v[:], scalar1=-0.5,
                                    scalar2=1.5, op0=ALU.mult, op1=ALU.add)
                    V.tensor_tensor(out=rb_u[:], in0=rb_u[:], in1=rb_v[:], op=ALU.mult)
                    T.w(V.mark())
                    T.matmul(psRow[:, 256:384], ones_row[:], rb_u[0:1, :], start=True, stop=True)
                    V.w(T.mark())
                    V.tensor_tensor(out=ogf[:], in0=ogf[:], in1=psRow[:, 256:384], op=ALU.mult)
                    if si > 0 and c_loc == 0 and h == 0:
                        V.w(("GP", 16 * 4 * si))
                    V.tensor_scalar(out=og_st[h][:, cs], in0=ogf[:],
                                    scalar1=gnc[:, h:h + 1], scalar2=None, op0=ALU.mult)

            mv = V.mark()
            for h in range(4):
                GP.dma(og_out[h * 128:(h + 1) * 128, si * 512:(si + 1) * 512],
                       og_st[h][:], wait=mv)
        GP.drain()
        SY.drain()

    run_passes(nc, sems, prog)
    return nc



def build_tail(n_tok):
    assert n_tok % 512 == 0
    NH = n_tok // 512
    nc = bass.Bass("TRN2", target_bir_lowering=False)
    dr = lambda n, s, dt=F32, k="ExternalInput": nc.dram_tensor(n, s, dt, kind=k).ap()
    og_d = dr("og_full", [2048, n_tok])
    x_d = dr("x_slice", [n_tok, 2048])
    Wo_d = dr("Wo", [2048, 2048])
    up_d = dr("up_dq", [2048, 8192], BF16)
    dn_d = dr("dn_dq", [8192, 2048], BF16)
    ub_d = dr("ub_cols", [128, 64])
    db_d = dr("db_cols", [128, 16])
    ident_d = dr("ident", [128, 128])
    y_d = dr("y_out", [n_tok, 2048], F32, k="ExternalOutput")

    sb = nc.alloc_sbuf_tensor
    pst = nc.alloc_psum_tensor
    es = ExitStack()

    ident_s = sb("ident_s", [128, 128], F32)
    ub = sb("ub", [128, 64], F32)
    db = sb("db", [128, 16], F32)
    _obase = nc.bump_sbuf(65536)[0]
    h_sb = nc.alloc_sbuf_tensor_at("h_sb", [128, 64 * 512], BF16, offset=_obase)
    x_cm = nc.alloc_sbuf_tensor_at("x_cm", [128, 16 * 512], F32, offset=_obase)
    og_sb = nc.alloc_sbuf_tensor_at("og_sb", [128, 16 * 512], BF16, offset=_obase + 32768)
    x_tm = [nc.alloc_sbuf_tensor_at("x_tm0", [128, 2048], F32, offset=_obase + 49152),
            nc.alloc_sbuf_tensor_at("x_tm1", [128, 2048], F32, offset=_obase + 57344)]
    wld = x_tm
    woT = sb("woT", [128, 16 * 128], BF16)
    y_f = sb("y_f", [128, 16 * 512], F32)
    _xbase = nc.bump_sbuf(16384)[0]
    x1b = nc.alloc_sbuf_tensor_at("x1b", [128, 16 * 512], BF16, offset=_xbase)
    outb = nc.alloc_sbuf_tensor_at("outb", [128, 2048], F32, offset=_xbase + 8192)
    sqb = sb("sqb", [128, 512], F32)
    upw = [sb("upw0", [128, 16 * 256], BF16), sb("upw1", [128, 16 * 256], BF16)]
    dnw = [sb("dnw0", [128, 64 * 128], BF16), sb("dnw1", [128, 64 * 128], BF16)]
    z_f = sb("z_f", [128, 16 * 512], F32)
    rowm = sb("rowm", [1, 512], F32)
    rowq = sb("rowq", [1, 512], F32)
    rowr = sb("rowr", [1, 512], F32)
    rowt = sb("rowt", [1, 512], F32)
    ones_col = sb("ones_col", [128, 1], F32)
    ones_row = sb("ones_row", [1, 128], F32)

    psA = pst("psA", [128, 512], F32)
    psB_ = pst("psB", [128, 512], F32)
    psT = pst("psT", [128, 512], F32)
    psRow = pst("psRow", [128, 512], F32)

    sems = {}
    for nm in ("T", "V", "S", "SY", "GP"):
        sems[nm] = es.enter_context(nc.semaphore("sem" + nm))

    def ln_rows(T, V, S, src_tiles, eps):
        T.w(V.mark())
        for k in range(16):
            T.matmul(psRow[0:1, 0:512], ones_col[:], src_tiles[k],
                     start=(k == 0), stop=(k == 15))
        V.w(T.mark())
        V.tensor_scalar(out=rowm[:], in0=psRow[0:1, 0:512], scalar1=_f32(1.0 / 2048.0),
                        scalar2=None, op0=ALU.mult)
        for k in range(16):
            V.tensor_tensor(out=sqb[:], in0=src_tiles[k], in1=src_tiles[k], op=ALU.mult)
            T.w(V.mark())
            T.matmul(psRow[0:1, 0:512], ones_col[:], sqb[:],
                     start=(k == 0), stop=(k == 15))
            if k < 15:
                V.w(T.mark())
        V.w(T.mark())
        V.tensor_scalar(out=rowq[:], in0=psRow[0:1, 0:512], scalar1=_f32(1.0 / 2048.0),
                        scalar2=None, op0=ALU.mult)
        V.tensor_tensor(out=rowt[:], in0=rowm[:], in1=rowm[:], op=ALU.mult)
        V.tensor_tensor(out=rowq[:], in0=rowq[:], in1=rowt[:], op=ALU.subtract)
        V.tensor_scalar(out=rowq[:], in0=rowq[:], scalar1=_f32(eps), scalar2=None, op0=ALU.add)
        S.w(V.mark())
        S.activation(out=rowt[:], in_=rowq[:], func=AF.Sqrt)
        V.w(S.mark())
        V.reciprocal(out=rowr[:], in_=rowt[:])
        V.tensor_tensor(out=rowt[:], in0=rowr[:], in1=rowr[:], op=ALU.mult)
        V.tensor_tensor(out=rowt[:], in0=rowt[:], in1=rowq[:], op=ALU.mult)
        V.tensor_scalar(out=rowt[:], in0=rowt[:], scalar1=-0.5, scalar2=1.5,
                        op0=ALU.mult, op1=ALU.add)
        V.tensor_tensor(out=rowr[:], in0=rowr[:], in1=rowt[:], op=ALU.mult)
        T.w(V.mark())
        T.matmul(psRow[:, 0:512], ones_row[:], rowm[0:1, 0:128], start=True, stop=False)
        T.matmul(psRow[:, 0:512], ones_row[:], rowm[0:1, 0:128], start=False, stop=True)

    EPS_LN = 1e-5

    def prog(T, V, S, SY, GP):
        m_c = SY.dma(ident_s[:], ident_d)
        SY.dma(ub[:], ub_d)
        m_c2 = SY.dma(db[:], db_d)
        V.w(m_c2)
        V.memset(ones_col[:], 1.0)
        V.memset(ones_row[:], 1.0)

        for half in range(NH):
            t0 = half * 512
            for tt in range(4):
                mld = SY.dma(x_tm[tt % 2][:], x_d[t0 + tt * 128: t0 + (tt + 1) * 128, :],
                             wait=T.mark())
                T.w(mld)
                T.w(V.mark())
                for g in range(4):
                    if g > 0:
                        T.w(V.mark())
                    for kk4 in range(4):
                        kk = g * 4 + kk4
                        T.transpose(psT[:, kk4 * 128:(kk4 + 1) * 128],
                                    x_tm[tt % 2][:, kk * 128:(kk + 1) * 128], ident_s[:])
                    V.w(T.mark())
                    for kk4 in range(4):
                        kk = g * 4 + kk4
                        V.tensor_copy(out=x_cm[:, kk * 512 + tt * 128: kk * 512 + tt * 128 + 128],
                                      in_=psT[:, kk4 * 128:(kk4 + 1) * 128])
            m_og = GP.dma(og_sb[:].rearrange("p (a c) -> p a c", a=16),
                          og_d[:, t0:t0 + 512].rearrange("(a p) c -> p a c", p=128),
                          wait=T.mark())
            for m in range(16):
                mld = SY.dma(wld[m % 2][:], Wo_d[m * 128:(m + 1) * 128, :],
                             wait=T.mark())
                T.w(mld)
                T.w(V.mark())
                for g in range(4):
                    if g > 0:
                        T.w(V.mark())
                    for kk4 in range(4):
                        kk = g * 4 + kk4
                        T.transpose(psT[:, kk4 * 128:(kk4 + 1) * 128],
                                    wld[m % 2][:, kk * 128:(kk + 1) * 128], ident_s[:])
                    V.w(T.mark())
                    for kk4 in range(4):
                        kk = g * 4 + kk4
                        V.tensor_copy(out=woT[:, kk * 128:(kk + 1) * 128],
                                      in_=psT[:, kk4 * 128:(kk4 + 1) * 128])
                T.w(V.mark())
                T.w(m_og)
                for k in range(16):
                    T.matmul(psA[:, 0:512], woT[:, k * 128:(k + 1) * 128],
                             og_sb[:, k * 512:(k + 1) * 512],
                             start=(k == 0), stop=(k == 15))
                V.w(T.mark())
                V.tensor_tensor(out=y_f[:, m * 512:(m + 1) * 512], in0=psA[:, 0:512],
                                in1=x_cm[:, m * 512:(m + 1) * 512], op=ALU.add)
            ytiles = [y_f[:, k * 512:(k + 1) * 512] for k in range(16)]
            T.w(V.mark())
            for k in range(16):
                T.matmul(psRow[0:1, 0:512], ones_col[:], ytiles[k],
                         start=(k == 0), stop=(k == 15))
            V.w(T.mark())
            V.tensor_scalar(out=rowm[:], in0=psRow[0:1, 0:512],
                            scalar1=_f32(1.0 / 2048.0), scalar2=None, op0=ALU.mult)
            for k in range(16):
                V.tensor_tensor(out=sqb[:], in0=ytiles[k], in1=ytiles[k], op=ALU.mult)
                T.w(V.mark())
                T.matmul(psRow[0:1, 0:512], ones_col[:], sqb[:],
                         start=(k == 0), stop=(k == 15))
                V.w(T.mark())
            V.tensor_scalar(out=rowq[:], in0=psRow[0:1, 0:512],
                            scalar1=_f32(1.0 / 2048.0), scalar2=None, op0=ALU.mult)
            V.tensor_tensor(out=rowt[:], in0=rowm[:], in1=rowm[:], op=ALU.mult)
            V.tensor_tensor(out=rowq[:], in0=rowq[:], in1=rowt[:], op=ALU.subtract)
            V.tensor_scalar(out=rowq[:], in0=rowq[:], scalar1=_f32(EPS_LN),
                            scalar2=None, op0=ALU.add)
            S.w(V.mark())
            S.activation(out=rowt[:], in_=rowq[:], func=AF.Sqrt)
            V.w(S.mark())
            V.reciprocal(out=rowr[:], in_=rowt[:])
            V.tensor_tensor(out=rowt[:], in0=rowr[:], in1=rowr[:], op=ALU.mult)
            V.tensor_tensor(out=rowt[:], in0=rowt[:], in1=rowq[:], op=ALU.mult)
            V.tensor_scalar(out=rowt[:], in0=rowt[:], scalar1=-0.5, scalar2=1.5,
                            op0=ALU.mult, op1=ALU.add)
            V.tensor_tensor(out=rowr[:], in0=rowr[:], in1=rowt[:], op=ALU.mult)
            T.w(V.mark())
            T.matmul(psA[:, 0:512], ones_row[:], rowm[0:1, 0:512], start=True, stop=True)
            T.matmul(psB_[:, 0:512], ones_row[:], rowr[0:1, 0:512], start=True, stop=True)
            V.w(T.mark())
            for k in range(16):
                V.tensor_tensor(out=y_f[:, k * 512:(k + 1) * 512], in0=ytiles[k],
                                in1=psA[:, 0:512], op=ALU.subtract)
                V.tensor_tensor(out=y_f[:, k * 512:(k + 1) * 512], in0=ytiles[k],
                                in1=psB_[:, 0:512], op=ALU.mult)
                V.tensor_copy(out=x1b[:, k * 512:(k + 1) * 512], in_=ytiles[k])
            for mg in range(32):
                src = up_d[:, mg * 256:(mg + 1) * 256].rearrange("(a p) c -> p a c", p=128)
                mld = SY.dma(upw[mg % 2][:].rearrange("p (a c) -> p a c", a=16), src,
                             wait=T.mark())
                T.w(mld)
                T.w(S.mark())
                if mg == 0:
                    T.w(V.mark())
                for m2 in range(2):
                    m = mg * 2 + m2
                    for k in range(16):
                        T.matmul(psA[:, 0:512] if m2 == 0 else psB_[:, 0:512],
                                 upw[mg % 2][:, k * 256 + m2 * 128: k * 256 + m2 * 128 + 128],
                                 x1b[:, k * 512:(k + 1) * 512],
                                 start=(k == 0), stop=(k == 15))
                S.w(T.mark())
                for m2 in range(2):
                    m = mg * 2 + m2
                    S.activation(out=h_sb[:, m * 512:(m + 1) * 512],
                                 in_=psA[:, 0:512] if m2 == 0 else psB_[:, 0:512],
                                 func=AF.Gelu_apprx_tanh, bias=ub[:, m:m + 1])
            for m in range(16):
                src = dn_d[:, m * 128:(m + 1) * 128].rearrange("(a p) c -> p a c", p=128)
                mld = SY.dma(dnw[m % 2][:].rearrange("p (a c) -> p a c", a=64), src,
                             wait=T.mark())
                T.w(mld)
                T.w(S.mark())
                T.w(V.mark())
                for kf in range(64):
                    T.matmul(psA[:, 0:512], dnw[m % 2][:, kf * 128:(kf + 1) * 128],
                             h_sb[:, kf * 512:(kf + 1) * 512],
                             start=(kf == 0), stop=(kf == 63))
                V.w(T.mark())
                V.tensor_scalar(out=z_f[:, m * 512:(m + 1) * 512], in0=psA[:, 0:512],
                                scalar1=db[:, m:m + 1], scalar2=None, op0=ALU.add)
                V.tensor_tensor(out=z_f[:, m * 512:(m + 1) * 512],
                                in0=z_f[:, m * 512:(m + 1) * 512],
                                in1=y_f[:, m * 512:(m + 1) * 512], op=ALU.add)
            ztiles = [z_f[:, k * 512:(k + 1) * 512] for k in range(16)]
            T.w(V.mark())
            for k in range(16):
                T.matmul(psRow[0:1, 0:512], ones_col[:], ztiles[k],
                         start=(k == 0), stop=(k == 15))
            V.w(T.mark())
            V.tensor_scalar(out=rowm[:], in0=psRow[0:1, 0:512],
                            scalar1=_f32(1.0 / 2048.0), scalar2=None, op0=ALU.mult)
            for k in range(16):
                V.tensor_tensor(out=sqb[:], in0=ztiles[k], in1=ztiles[k], op=ALU.mult)
                T.w(V.mark())
                T.matmul(psRow[0:1, 0:512], ones_col[:], sqb[:],
                         start=(k == 0), stop=(k == 15))
                V.w(T.mark())
            V.tensor_scalar(out=rowq[:], in0=psRow[0:1, 0:512],
                            scalar1=_f32(1.0 / 2048.0), scalar2=None, op0=ALU.mult)
            V.tensor_tensor(out=rowt[:], in0=rowm[:], in1=rowm[:], op=ALU.mult)
            V.tensor_tensor(out=rowq[:], in0=rowq[:], in1=rowt[:], op=ALU.subtract)
            V.tensor_scalar(out=rowq[:], in0=rowq[:], scalar1=_f32(EPS_LN),
                            scalar2=None, op0=ALU.add)
            S.w(V.mark())
            S.activation(out=rowt[:], in_=rowq[:], func=AF.Sqrt)
            V.w(S.mark())
            V.reciprocal(out=rowr[:], in_=rowt[:])
            V.tensor_tensor(out=rowt[:], in0=rowr[:], in1=rowr[:], op=ALU.mult)
            V.tensor_tensor(out=rowt[:], in0=rowt[:], in1=rowq[:], op=ALU.mult)
            V.tensor_scalar(out=rowt[:], in0=rowt[:], scalar1=-0.5, scalar2=1.5,
                            op0=ALU.mult, op1=ALU.add)
            V.tensor_tensor(out=rowr[:], in0=rowr[:], in1=rowt[:], op=ALU.mult)
            T.w(V.mark())
            T.matmul(psA[:, 0:512], ones_row[:], rowm[0:1, 0:512], start=True, stop=True)
            T.matmul(psB_[:, 0:512], ones_row[:], rowr[0:1, 0:512], start=True, stop=True)
            V.w(T.mark())
            for k in range(16):
                V.tensor_tensor(out=z_f[:, k * 512:(k + 1) * 512], in0=ztiles[k],
                                in1=psA[:, 0:512], op=ALU.subtract)
                V.tensor_tensor(out=z_f[:, k * 512:(k + 1) * 512], in0=ztiles[k],
                                in1=psB_[:, 0:512], op=ALU.mult)
            for tt in range(4):
                T.w(V.mark())
                for k in range(16):
                    if k > 0 and k % 4 == 0:
                        T.w(V.mark())
                    T.transpose(psT[:, (k % 4) * 128:(k % 4 + 1) * 128],
                                z_f[:, k * 512 + tt * 128: k * 512 + tt * 128 + 128],
                                ident_s[:])
                    if k % 4 == 3:
                        V.w(T.mark())
                        for k4 in range(k - 3, k + 1):
                            V.tensor_copy(out=outb[:, k4 * 128:(k4 + 1) * 128],
                                          in_=psT[:, (k4 % 4) * 128:(k4 % 4 + 1) * 128])
                mo = V.mark()
                GP.dma(y_d[t0 + tt * 128: t0 + (tt + 1) * 128, :], outb[:], wait=mo)
                V.w(GP.mark())
        GP.drain()
        SY.drain()

    run_passes(nc, sems, prog)
    return nc




def build_fused(n_own=1024, n_pre=3072, FF=8192):
    assert n_own % 512 == 0 and n_pre % 512 == 0 and FF % 512 == 0
    NS_OWN = n_own // 512
    NS_PRE = n_pre // 512
    NF = FF // 128
    nc = bass.Bass("TRN2", target_bir_lowering=False)
    dr = lambda n, s, dt=F32, k="ExternalInput": nc.dram_tensor(n, s, dt, kind=k).ap()
    x_pre_d = dr("x_pre", [max(n_pre, 512), 2048])
    x_own_d = dr("x_own", [n_own, 2048])
    Wq_d = dr("Wq", [2048, 2048])
    Wk_d = dr("Wk", [2048, 2048])
    Wv_d = dr("Wv", [2048, 2048])
    Wb_d = dr("Wbeta", [16, 2048])
    Wg1_d = dr("Wg1", [128, 2048])
    Wg2_d = dr("Wg2", [2048, 128])
    gnc_d = dr("gnorm_cols", [128, 16])
    Wo_d = dr("Wo", [2048, 2048])
    up_d = dr("up_w", [2048, FF])
    dn_d = dr("dn_w", [FF, 2048])
    ub_d = dr("ub_cols", [128, FF // 128])
    db_d = dr("db_row", [1, 2048])
    ident_d = dr("ident", [128, 128])
    mask_ln_d = dr("mask_lowneg", [128, 128])
    mask_tu_d = dr("mask_triu", [128, 128], BF16)
    y_d = dr("y_out", [n_own, 2048], F32, k="ExternalOutput")
    di = lambda n, s, dt: nc.dram_tensor(n, s, dt, kind="Internal").ap()
    wqT_d = di("s_wqT", [2048, 2048], BF16)
    wkT_d = di("s_wkT", [2048, 2048], BF16)
    wvT_d = di("s_wvT", [2048, 2048], BF16)
    woT_d = di("s_woT", [2048, 2048], BF16)
    updq_d = di("s_updq", [2048, FF], BF16)
    dndq_d = di("s_dndq", [FF, 2048], BF16)

    sb = nc.alloc_sbuf_tensor
    sba = nc.alloc_sbuf_tensor_at
    pst = nc.alloc_psum_tensor
    es = ExitStack()

    ident_s = sb("ident_s", [128, 128], F32)
    identb = sb("identb", [128, 128], BF16)
    mask_ln = sb("mask_ln", [128, 128], F32)
    mask_tu = sb("mask_tu", [128, 128], BF16)
    gnc = sb("gnc", [128, 16], F32)
    ub = sb("ub", [128, FF // 128], F32)
    ones_col = sb("ones_col", [128, 1], F32)
    ones_row = sb("ones_row", [1, 128], F32)
    wbT = sb("wbT", [128, 16 * 16], BF16)
    wg1T = sb("wg1T", [128, 16 * 128], BF16)
    wg2T = sb("wg2T", [128, 16 * 128], BF16)
    S_f = [sb(f"Sf_{h}", [128, 128], F32) for h in range(16)]
    S_b = [sb(f"Sb_{h}", [128, 128], BF16) for h in range(16)]
    bstk = [sb(f"bstk{b}", [128, 1], F32) for b in range(4)]
    nbstk = [sb(f"nbstk{b}", [128, 1], F32) for b in range(4)]
    colm = sb("colm", [128, 1], F32)
    colq = sb("colq", [128, 1], F32)
    colr = sb("colr", [128, 1], F32)
    colt = sb("colt", [128, 1], F32)
    og_all = sb("og_all", [128, 16 * 512], BF16)

    baseA = nc.bump_sbuf(65536)[0]
    h_sb = sba("h_sb", [128, (NF // 2) * 512], BF16, offset=baseA)
    y1 = [sba(f"y1_{t}", [128, 2048], F32, offset=baseA + 32768 + t * 8192)
          for t in range(4)]
    q_h = sba("q_h", [128, 16 * 512], BF16, offset=baseA)
    kr_h = sba("kr_h", [128, 16 * 512], BF16, offset=baseA + 16384)
    k_hh = sba("k_hh", [128, 16 * 512], BF16, offset=baseA + 32768)
    v_h = sba("v_h", [128, 16 * 512], BF16, offset=baseA + 49152)
    _dw = baseA + 16384
    Nbd = [sba(f"Nbd{b}", [128, 128], F32, offset=_dw + b * 512) for b in range(4)]
    AoffBD = sba("AoffBD", [128, 6 * 128], BF16, offset=_dw + 2048)
    cW = [[sba(f"cW{b}_{j}", [128, 128], BF16, offset=_dw + 3584 + (b * 2 + j) * 256)
           for j in range(2)] for b in range(4)]
    cZ = [[sba(f"cZ{b}_{j}", [128, 128], BF16, offset=_dw + 5632 + (b * 2 + j) * 256)
           for j in range(2)] for b in range(4)]
    cG = [[sba(f"cG{b}_{j}", [128, 128], BF16, offset=_dw + 7680 + (b * 2 + j) * 256)
           for j in range(2)] for b in range(4)]
    rbS = [sba(f"rbS{b}", [128, 128], BF16, offset=_dw + 9728 + b * 256) for b in range(4)]
    ubS = [sba(f"ubS{b}", [128, 128], BF16, offset=_dw + 10752 + b * 256) for b in range(4)]
    RpreG = sba("RpreG", [128, 4 * 128], BF16, offset=_dw + 11776)
    UtmG = sba("UtmG", [128, 4 * 128], BF16, offset=_dw + 12800)
    KtmG = sba("KtmG", [128, 4 * 128], BF16, offset=_dw + 13824)
    PTh4 = sba("PTh4", [128, 4 * 128], BF16, offset=_dw + 3584)
    gate4 = sba("gate4", [128, 4 * 128], BF16, offset=_dw + 4608)
    ogf4 = sba("ogf4", [128, 4 * 128], F32, offset=_dw + 5632)

    baseB = nc.bump_sbuf(65536)[0]
    x_tm = [sba("x_tm0", [128, 2048], F32, offset=baseB),
            sba("x_tm1", [128, 2048], F32, offset=baseB + 8192)]
    _b2 = baseB + 16384
    qw_aa = sba("qw_aa", [128, 2048], F32, offset=_b2)
    qw_sg = sba("qw_sg", [128, 2048], BF16, offset=_b2 + 8192)
    qw_ac = sba("qw_ac", [128, 2048], BF16, offset=_b2 + 12288)
    qw_bb = sba("qw_bb", [128, 2048], BF16, offset=_b2 + 16384)
    qw_cc = sba("qw_cc", [128, 2048], BF16, offset=_b2 + 20480)
    qw_st = [sba("qw_st0", [128, 2048], BF16, offset=_b2 + 24576),
             sba("qw_st1", [128, 2048], BF16, offset=_b2 + 28672)]
    qc_am = sba("qc_am", [128, 64], F32, offset=_b2 + 32768)
    qc_rr = sba("qc_rr", [128, 64], F32, offset=_b2 + 33024)
    qc_ss = sba("qc_ss", [128, 64], F32, offset=_b2 + 33280)
    qc_mm = sba("qc_mm", [128, 64], F32, offset=_b2 + 33536)
    qc_tt = sba("qc_tt", [128, 64], F32, offset=_b2 + 33792)
    wstr = [sba("wstr0", [128, 16 * 128], BF16, offset=_b2),
            sba("wstr1", [128, 16 * 128], BF16, offset=_b2 + 4096)]
    rowm = sba("rowm", [1, 512], F32, offset=_b2 + 8192)
    rowq = sba("rowq", [1, 512], F32, offset=_b2 + 10240)
    rowr = sba("rowr", [1, 512], F32, offset=_b2 + 12288)
    ksq = sba("ksq", [128, 512], F32, offset=_b2 + 14336)
    worhs = [sba("worhs0", [128, 512], BF16, offset=_b2),
             sba("worhs1", [128, 512], BF16, offset=_b2 + 1024)]
    dnw = [sba("dnw0", [128, 512], BF16, offset=_b2 + 2048),
           sba("dnw1", [128, 512], BF16, offset=_b2 + 3072)]
    upw = [sba("upw0", [128, 16 * 256], BF16, offset=_b2 + 4096),
           sba("upw1", [128, 16 * 256], BF16, offset=_b2 + 12288)]
    sq_t = sba("sq_t", [128, 2048], F32, offset=_b2 + 20480)
    dbrow = sba("dbrow", [1, 2048], F32, offset=_b2 + 34304)
    beta_sb = sba("beta_sb", [16, 512], F32, offset=_b2 + 42496)
    betaT4 = sba("betaT4", [32, 16 * 16], F32, offset=_b2 + 44544)
    rb_s = sba("rb_s", [1, 128], F32, offset=_b2 + 45568)
    rb_t = sba("rb_t", [1, 128], F32, offset=_b2 + 46080)
    rb_u = sba("rb_u", [1, 128], F32, offset=_b2 + 46592)
    rb_v = sba("rb_v", [1, 128], F32, offset=_b2 + 47104)
    xg1_buf = sba("xg1_buf", [128, 512], BF16, offset=_b2 + 47616)

    baseC = nc.bump_sbuf(16384)[0]
    x_cm = sba("x_cm", [128, 16 * 512], BF16, offset=baseC)
    x1b = sba("x1b", [128, 16 * 512], BF16, offset=baseC)

    psX = pst("psX", [128, 512], F32)
    psQ = pst("psQ", [128, 512], F32)
    psC = pst("psC", [128, 512], F32)
    psR = pst("psR", [128, 512], F32)
    psBt = pst("psBt", [128, 1024], BF16)
    psRow = pst("psRow", [128, 512], F32)

    sems = {}
    for nm in ("T", "V", "S", "SY", "GP"):
        sems[nm] = es.enter_context(nc.semaphore("sem" + nm))

    EPS = _f32(1e-6)
    EPS_LN = _f32(1e-5)
    pairs_off = [(b, b2) for b in range(4) for b2 in range(b + 1, 4)]

    def prog(T, V, S, SY, GP):
        SY.dma(ident_s[:], ident_d)
        SY.dma(mask_ln[:], mask_ln_d)
        SY.dma(mask_tu[:], mask_tu_d)
        SY.dma(gnc[:], gnc_d)
        SY.dma(ub[:], ub_d)
        m_c = SY.dma(dbrow[:], db_d)
        V.w(m_c)
        V.tensor_copy(out=identb[:], in_=ident_s[:])
        V.memset(ones_col[:], 1.0)
        V.memset(ones_row[:], 1.0)
        for h in range(16):
            V.memset(S_f[h][:], 0.0)
            V.memset(S_b[h][:], 0.0)

        mld = SY.dma(x_tm[0][0:16, :], Wb_d, wait=T.mark())
        T.w(mld)
        for kk in range(16):
            T.transpose(psX[:, (kk % 4) * 128:(kk % 4) * 128 + 16],
                        x_tm[0][0:16, kk * 128:(kk + 1) * 128],
                        ident_s[0:16, 0:16])
            if kk % 4 == 3:
                V.w(T.mark())
                for k2 in range(kk - 3, kk + 1):
                    V.tensor_copy(out=wbT[:, k2 * 16:(k2 + 1) * 16],
                                  in_=psX[:, (k2 % 4) * 128:(k2 % 4) * 128 + 16])
                if kk < 15:
                    T.w(V.mark())
        mld = SY.dma(x_tm[1][:], Wg1_d, wait=T.mark())
        T.w(mld)
        T.w(V.mark())
        for g in range(4):
            if g > 0:
                T.w(V.mark())
            for k4 in range(4):
                kk = g * 4 + k4
                T.transpose(psX[:, k4 * 128:(k4 + 1) * 128],
                            x_tm[1][:, kk * 128:(kk + 1) * 128], ident_s[:])
            V.w(T.mark())
            for k4 in range(4):
                kk = g * 4 + k4
                V.tensor_copy(out=wg1T[:, kk * 128:(kk + 1) * 128],
                              in_=psX[:, k4 * 128:(k4 + 1) * 128])
        for hh in range(16):
            mld = SY.dma(x_tm[hh % 2][:, 0:128], Wg2_d[hh * 128:(hh + 1) * 128, :],
                         wait=T.mark())
            T.w(mld)
            T.w(V.mark())
            T.transpose(psX[:, 0:128], x_tm[hh % 2][:, 0:128], ident_s[:])
            V.w(T.mark())
            V.tensor_copy(out=wg2T[:, hh * 128:(hh + 1) * 128], in_=psX[:, 0:128])

        for wi, (src_d, dst_d) in enumerate(((Wq_d, wqT_d), (Wk_d, wkT_d),
                                             (Wv_d, wvT_d), (Wo_d, woT_d))):
            for m in range(16):
                i = wi * 16 + m
                mld = SY.dma(x_tm[i % 2][:], src_d[m * 128:(m + 1) * 128, :],
                             wait=T.mark())
                T.w(mld)
                T.w(V.mark())
                T.w(GP.mark())
                for g in range(4):
                    if g > 0:
                        T.w(V.mark())
                    for k4 in range(4):
                        kk = g * 4 + k4
                        T.transpose(psX[:, k4 * 128:(k4 + 1) * 128],
                                    x_tm[i % 2][:, kk * 128:(kk + 1) * 128],
                                    ident_s[:])
                    V.w(T.mark())
                    for k4 in range(4):
                        kk = g * 4 + k4
                        V.tensor_copy(out=wstr[i % 2][:, kk * 128:(kk + 1) * 128],
                                      in_=psX[:, k4 * 128:(k4 + 1) * 128])
                GP.dma(dst_d[:, m * 128:(m + 1) * 128].rearrange("(a p) c -> p a c", p=128),
                       wstr[i % 2][:].rearrange("p (a c) -> p a c", a=16),
                       wait=V.mark())

        NBQ = 64
        for src_d, dst_d, RT, CT in ((up_d, updq_d, 2048, FF),
                                     (dn_d, dndq_d, FF, 2048)):
            idx = 0
            for rt in range(RT // 128):
                for cs in range(CT // 2048):
                    par = idx % 2
                    mld = SY.dma(x_tm[par][:],
                                 src_d[rt * 128:(rt + 1) * 128, cs * 2048:(cs + 1) * 2048],
                                 wait=T.mark())
                    T.w(mld)
                    T.w(S.mark())
                    if idx == 0:
                        T.w(V.mark())
                    for g in range(4):
                        if g > 0:
                            T.w(S.mark())
                        for k4 in range(4):
                            kk = g * 4 + k4
                            T.transpose(psX[:, k4 * 128:(k4 + 1) * 128],
                                        x_tm[par][:, kk * 128:(kk + 1) * 128],
                                        ident_s[:])
                        S.w(T.mark())
                        if g == 0:
                            S.w(V.mark())
                            if idx == 0:
                                S.w(GP.mark())
                        S.activation(out=qw_aa[:, g * 512:(g + 1) * 512],
                                     in_=psX[:, 0:512], func=AF.Abs)
                        S.activation(out=qw_sg[:, g * 512:(g + 1) * 512],
                                     in_=psX[:, 0:512], func=AF.Sign)
                    V.w(S.mark())
                    V.tensor_reduce(out=qc_am[:],
                                    in_=qw_aa[:].rearrange("p (a b) -> p a b", b=32),
                                    axis=AX.X, op=ALU.max)
                    V.tensor_scalar(out=qc_mm[:], in0=qc_am[:], scalar1=_f32(1e-30),
                                    scalar2=_f32(2.0 ** 100), op0=ALU.max, op1=ALU.mult)
                    V.memset(qc_rr[:], _f32(2.0 ** 102))
                    for E in _STAGES:
                        V.tensor_scalar(out=qc_tt[:], in0=qc_mm[:], scalar1=_f32(2.0 ** E),
                                        scalar2=None, op0=ALU.is_lt)
                        V.tensor_scalar(out=qc_tt[:], in0=qc_tt[:],
                                        scalar1=_f32(1.0 - 2.0 ** (-E)),
                                        scalar2=_f32(2.0 ** (-E)), op0=ALU.mult, op1=ALU.add)
                        V.tensor_tensor(out=qc_mm[:], in0=qc_mm[:], in1=qc_tt[:], op=ALU.mult)
                        V.tensor_tensor(out=qc_rr[:], in0=qc_rr[:], in1=qc_tt[:], op=ALU.mult)
                    V.reciprocal(out=qc_ss[:], in_=qc_rr[:])
                    rbc = qc_rr[:].unsqueeze(2).broadcast_to([128, NBQ, 32])
                    V.tensor_tensor(out=qw_aa[:].rearrange("p (a b) -> p a b", b=32),
                                    in0=qw_aa[:].rearrange("p (a b) -> p a b", b=32),
                                    in1=rbc, op=ALU.mult)
                    xh = qw_aa
                    V.tensor_scalar(out=qw_ac[:], in0=xh[:], scalar1=0.25,
                                    scalar2=None, op0=ALU.is_ge)
                    for mmid in (0.75, 1.25, 1.75):
                        V.scalar_tensor_tensor(out=qw_ac[:], in0=xh[:], scalar=mmid,
                                               in1=qw_ac[:], op0=ALU.is_ge, op1=ALU.add)
                    V.tensor_scalar(out=qw_bb[:], in0=xh[:], scalar1=2.5,
                                    scalar2=None, op0=ALU.is_ge)
                    V.scalar_tensor_tensor(out=qw_bb[:], in0=xh[:], scalar=3.5,
                                           in1=qw_bb[:], op0=ALU.is_ge, op1=ALU.add)
                    V.tensor_scalar(out=qw_cc[:], in0=xh[:], scalar1=5.0,
                                    scalar2=None, op0=ALU.is_ge)
                    V.scalar_tensor_tensor(out=qw_ac[:], in0=qw_ac[:], scalar=0.5,
                                           in1=qw_bb[:], op0=ALU.mult, op1=ALU.add)
                    V.scalar_tensor_tensor(out=qw_ac[:], in0=qw_cc[:], scalar=2.0,
                                           in1=qw_ac[:], op0=ALU.mult, op1=ALU.add)
                    sbc = qc_ss[:].unsqueeze(2).broadcast_to([128, NBQ, 32])
                    V.tensor_tensor(out=qw_bb[:].rearrange("p (a b) -> p a b", b=32),
                                    in0=qw_ac[:].rearrange("p (a b) -> p a b", b=32),
                                    in1=sbc, op=ALU.mult)
                    V.tensor_tensor(out=qw_cc[:], in0=qw_bb[:], in1=qw_sg[:], op=ALU.mult)
                    T.w(V.mark())
                    for g in range(4):
                        if g > 0:
                            T.w(V.mark())
                        for k4 in range(4):
                            kk = g * 4 + k4
                            T.transpose(psBt[:, k4 * 128:(k4 + 1) * 128],
                                        qw_cc[:, kk * 128:(kk + 1) * 128], identb[:])
                        V.w(T.mark())
                        if g == 0:
                            V.w(GP.mark())
                        for k4 in range(4):
                            kk = g * 4 + k4
                            V.tensor_copy(out=qw_st[par][:, kk * 128:(kk + 1) * 128],
                                          in_=psBt[:, k4 * 128:(k4 + 1) * 128])
                    GP.dma(dst_d[rt * 128:(rt + 1) * 128, cs * 2048:(cs + 1) * 2048],
                           qw_st[par][:], wait=V.mark())
                    idx += 1

        def proj_strip(si, xsrc_d, own):
            S.w(GP.mark())
            V.w(GP.mark())
            for tt in range(4):
                mld = SY.dma(x_tm[tt % 2][:],
                             xsrc_d[si * 512 + tt * 128: si * 512 + (tt + 1) * 128, :],
                             wait=T.mark())
                T.w(mld)
                T.w(V.mark())
                for g in range(4):
                    if g > 0:
                        T.w(V.mark())
                    for k4 in range(4):
                        kk = g * 4 + k4
                        T.transpose(psX[:, k4 * 128:(k4 + 1) * 128],
                                    x_tm[tt % 2][:, kk * 128:(kk + 1) * 128],
                                    ident_s[:])
                    V.w(T.mark())
                    for k4 in range(4):
                        kk = g * 4 + k4
                        V.tensor_copy(out=x_cm[:, kk * 512 + tt * 128: kk * 512 + tt * 128 + 128],
                                      in_=psX[:, k4 * 128:(k4 + 1) * 128])
            T.w(V.mark())
            T.w(S.mark())
            for kk in range(16):
                T.matmul(psRow[0:16, :], wbT[:, kk * 16:(kk + 1) * 16],
                         x_cm[:, kk * 512:(kk + 1) * 512],
                         start=(kk == 0), stop=(kk == 15))
            S.w(T.mark())
            S.activation(out=beta_sb[:], in_=psRow[0:16, :], func=AF.Sigmoid)
            V.w(S.mark())
            V.tensor_scalar(out=beta_sb[:], in0=beta_sb[:], scalar1=2.0,
                            scalar2=None, op0=ALU.mult)
            T.w(V.mark())
            for j16 in range(16):
                T.transpose(psX[0:32, (j16 % 8) * 16:(j16 % 8) * 16 + 16],
                            beta_sb[0:16, j16 * 32:(j16 + 1) * 32],
                            ident_s[0:16, 0:16])
                if j16 % 8 == 7:
                    V.w(T.mark())
                    for j2 in range(j16 - 7, j16 + 1):
                        V.tensor_copy(out=betaT4[:, j2 * 16:(j2 + 1) * 16],
                                      in_=psX[0:32, (j2 % 8) * 16:(j2 % 8) * 16 + 16])
                    if j16 < 15:
                        T.w(V.mark())
            mats = [(wkT_d, kr_h, "silu"), (wvT_d, v_h, "copy")]
            if own:
                mats.append((wqT_d, q_h, "silu"))
            evac_marks = {}
            gi2 = 0
            for wT_d, dst, act in mats:
                for m in range(16):
                    mld = SY.dma(wstr[gi2 % 2][:].rearrange("p (a c) -> p a c", a=16),
                                 wT_d[:, m * 128:(m + 1) * 128].rearrange("(a p) c -> p a c", p=128),
                                 wait=T.mark())
                    T.w(mld)
                    if gi2 >= 2:
                        T.w(evac_marks[gi2 - 2])
                    pq = psQ if gi2 % 2 == 0 else psC
                    for kk in range(16):
                        T.matmul(pq[:, :], wstr[gi2 % 2][:, kk * 128:(kk + 1) * 128],
                                 x_cm[:, kk * 512:(kk + 1) * 512],
                                 start=(kk == 0), stop=(kk == 15))
                    S.w(T.mark())
                    if act == "copy":
                        S.copy(out=dst[:, m * 512:(m + 1) * 512], in_=pq[:, :])
                    else:
                        S.activation(out=dst[:, m * 512:(m + 1) * 512], in_=pq[:, :],
                                     func=AF.Silu)
                    evac_marks[gi2] = S.mark()
                    gi2 += 1
            if own:
                T.w(V.mark())
                T.w(S.mark())
                for kk in range(16):
                    T.matmul(psQ[:, :], wg1T[:, kk * 128:(kk + 1) * 128],
                             x_cm[:, kk * 512:(kk + 1) * 512],
                             start=(kk == 0), stop=(kk == 15))
                V.w(T.mark())
                V.tensor_copy(out=xg1_buf[:], in_=psQ[:, :])
            s0set = sq_t[0:1, 0:2048]
            s1set = x_tm[0][0:1, 0:2048]
            s2set = x_tm[1][0:1, 0:2048]
            rbanks = [psRow, psQ, psC, psR]
            for hg in range(4):
                hs4 = [hg * 4 + i for i in range(4)]
                V.w(S.mark())
                if hg == 0:
                    V.w(T.mark())
                for i, h in enumerate(hs4):
                    V.tensor_tensor(out=sq_t[:, i * 512:(i + 1) * 512],
                                    in0=kr_h[:, h * 512:(h + 1) * 512],
                                    in1=kr_h[:, h * 512:(h + 1) * 512], op=ALU.mult)
                T.w(V.mark())
                for i in range(4):
                    T.matmul(rbanks[i][0:1, 0:512], ones_col[:],
                             sq_t[:, i * 512:(i + 1) * 512], start=True, stop=True)
                V.w(T.mark())
                for i in range(4):
                    V.tensor_scalar(out=s0set[0:1, i * 512:(i + 1) * 512],
                                    in0=rbanks[i][0:1, 0:512], scalar1=EPS,
                                    scalar2=None, op0=ALU.add)
                S.w(V.mark())
                S.activation(out=s1set, in_=s0set, func=AF.Sqrt)
                V.w(S.mark())
                V.reciprocal(out=s2set, in_=s1set)
                V.tensor_tensor(out=s1set, in0=s2set, in1=s2set, op=ALU.mult)
                V.tensor_tensor(out=s1set, in0=s1set, in1=s0set, op=ALU.mult)
                V.tensor_scalar(out=s1set, in0=s1set, scalar1=-0.5, scalar2=1.5,
                                op0=ALU.mult, op1=ALU.add)
                V.tensor_tensor(out=s2set, in0=s2set, in1=s1set, op=ALU.mult)
                T.w(V.mark())
                for i in range(4):
                    T.matmul(rbanks[i][:, 0:512], ones_row[:],
                             s2set[0:1, i * 512:(i + 1) * 512], start=True, stop=True)
                V.w(T.mark())
                for i, h in enumerate(hs4):
                    V.tensor_tensor(out=k_hh[:, h * 512:(h + 1) * 512],
                                    in0=kr_h[:, h * 512:(h + 1) * 512],
                                    in1=rbanks[i][:, 0:512], op=ALU.mult)

        def delta_chunk(c_loc, own):
            cs0 = c_loc * 128
            if c_loc == 0:
                V.w(S.mark())
                V.w(T.mark())
                for b in range(4):
                    V.memset(Nbd[b][:], 0.0)
                V.memset(AoffBD[:], 0.0)
            for grp in range(4):
                hs = [grp * 4 + i for i in range(4)]
                T.w(V.mark())
                T.w(S.mark())
                for i, h in enumerate(hs):
                    kk_ap = k_hh[:, h * 512 + cs0: h * 512 + cs0 + 128]
                    T.matmul(psQ[:, i * 128:(i + 1) * 128], kk_ap, kk_ap,
                             start=True, stop=True)
                for i, h in enumerate(hs):
                    T.matmul(psR[:, i * 128:(i + 1) * 128], S_b[h][:],
                             k_hh[:, h * 512 + cs0: h * 512 + cs0 + 128],
                             start=True, stop=True)
                V.w(T.mark())
                for b in range(4):
                    j16 = c_loc * 4 + b
                    for i, h in enumerate(hs):
                        V.tensor_copy(out=bstk[b][32 * i:32 * (i + 1), 0:1],
                                      in_=betaT4[:, j16 * 16 + h: j16 * 16 + h + 1])
                    V.tensor_scalar(out=nbstk[b][:, 0:1], in0=bstk[b][:, 0:1],
                                    scalar1=-1.0, scalar2=None, op0=ALU.mult)
                for i in range(4):
                    for b in range(4):
                        V.scalar_tensor_tensor(
                            out=Nbd[b][32 * i:32 * (i + 1), 32 * i:32 * (i + 1)],
                            in0=psQ[32 * b:32 * (b + 1),
                                    i * 128 + 32 * b: i * 128 + 32 * (b + 1)],
                            scalar=bstk[b][32 * i:32 * (i + 1), 0:1],
                            in1=mask_ln[32 * i:32 * (i + 1), 32 * i:32 * (i + 1)],
                            op0=ALU.mult, op1=ALU.mult)
                    for bi, (b, b2) in enumerate(pairs_off):
                        V.tensor_copy(
                            out=AoffBD[32 * i:32 * (i + 1),
                                       bi * 128 + 32 * i: bi * 128 + 32 * (i + 1)],
                            in_=psQ[32 * b:32 * (b + 1),
                                    i * 128 + 32 * b2: i * 128 + 32 * (b2 + 1)])
                for i, h in enumerate(hs):
                    V.scalar_tensor_tensor(out=RpreG[:, i * 128:(i + 1) * 128],
                                           in0=psR[:, i * 128:(i + 1) * 128],
                                           scalar=-1.0,
                                           in1=v_h[:, h * 512 + cs0: h * 512 + cs0 + 128],
                                           op0=ALU.mult, op1=ALU.add)
                T.w(V.mark())
                for i in range(4):
                    T.transpose(psBt[:, i * 128:(i + 1) * 128],
                                RpreG[:, i * 128:(i + 1) * 128], identb[:])
                    T.transpose(psBt[:, 512 + i * 128:512 + (i + 1) * 128],
                                k_hh[:, hs[i] * 512 + cs0: hs[i] * 512 + cs0 + 128],
                                identb[:])
                V.w(T.mark())
                for i in range(4):
                    for b in range(4):
                        j16 = c_loc * 4 + b
                        V.tensor_scalar(
                            out=rbS[b][32 * i:32 * (i + 1), :],
                            in0=psBt[32 * b:32 * (b + 1), i * 128:(i + 1) * 128],
                            scalar1=betaT4[:, j16 * 16 + hs[i]: j16 * 16 + hs[i] + 1],
                            scalar2=None, op0=ALU.mult)
                    V.tensor_copy(out=KtmG[:, i * 128:(i + 1) * 128],
                                  in_=psBt[:, 512 + i * 128:512 + (i + 1) * 128])
                for b in range(4):
                    V.tensor_copy(out=cW[b][0][:], in_=Nbd[b][:])
                T.w(V.mark())
                for b in range(4):
                    T.transpose(psBt[:, b * 128:(b + 1) * 128], cW[b][0][:], identb[:])
                V.w(T.mark())
                for b in range(4):
                    V.tensor_copy(out=cZ[b][0][:], in_=psBt[:, b * 128:(b + 1) * 128])
                    V.tensor_tensor(out=cG[b][0][:], in0=cZ[b][0][:], in1=identb[:],
                                    op=ALU.add)
                for j in range(1, 5):
                    cur, nxt = (j - 1) % 2, j % 2
                    T.w(V.mark())
                    for b in range(4):
                        T.matmul(psC[:, b * 128:(b + 1) * 128], cZ[b][cur][:],
                                 cW[b][cur][:], start=True, stop=True)
                    if j < 4:
                        for b in range(4):
                            T.matmul(psR[:, b * 128:(b + 1) * 128], cW[b][cur][:],
                                     cZ[b][cur][:], start=True, stop=True)
                    V.w(T.mark())
                    for b in range(4):
                        V.tensor_copy(out=cW[b][nxt][:], in_=psC[:, b * 128:(b + 1) * 128])
                        if j < 4:
                            V.tensor_copy(out=cZ[b][nxt][:], in_=psR[:, b * 128:(b + 1) * 128])
                    T.w(V.mark())
                    for b in range(4):
                        T.matmul(psQ[:, b * 128:(b + 1) * 128], cW[b][nxt][:],
                                 cG[b][cur][:], start=True, stop=True)
                    V.w(T.mark())
                    for b in range(4):
                        V.tensor_tensor(out=cG[b][nxt][:], in0=psQ[:, b * 128:(b + 1) * 128],
                                        in1=cG[b][cur][:], op=ALU.add)
                for b in range(4):
                    T.w(V.mark())
                    T.matmul(psR[:, b * 128:(b + 1) * 128], cG[b][0][:], rbS[b][:],
                             start=True, stop=True)
                    V.w(T.mark())
                    V.tensor_copy(out=ubS[b][:], in_=psR[:, b * 128:(b + 1) * 128])
                    for i in range(4):
                        V.tensor_copy(out=UtmG[32 * b:32 * (b + 1), i * 128:(i + 1) * 128],
                                      in_=ubS[b][32 * i:32 * (i + 1), :])
                    if b < 3:
                        T.w(V.mark())
                        for b2 in range(b + 1, 4):
                            bi = pairs_off.index((b, b2))
                            T.matmul(psC[:, b2 * 128:(b2 + 1) * 128],
                                     AoffBD[:, bi * 128:(bi + 1) * 128],
                                     ubS[b][:], start=True, stop=True)
                        V.w(T.mark())
                        for b2 in range(b + 1, 4):
                            V.scalar_tensor_tensor(out=rbS[b2][:],
                                                   in0=psC[:, b2 * 128:(b2 + 1) * 128],
                                                   scalar=nbstk[b2][:, 0:1], in1=rbS[b2][:],
                                                   op0=ALU.mult, op1=ALU.add)
                T.w(V.mark())
                T.w(S.mark())
                for i, h in enumerate(hs):
                    T.matmul(psC[:, i * 128:(i + 1) * 128], KtmG[:, i * 128:(i + 1) * 128],
                             UtmG[:, i * 128:(i + 1) * 128], start=True, stop=True)
                    if own:
                        T.matmul(psQ[:, i * 128:(i + 1) * 128],
                                 k_hh[:, h * 512 + cs0: h * 512 + cs0 + 128],
                                 q_h[:, h * 512 + cs0: h * 512 + cs0 + 128],
                                 start=True, stop=True)
                V.w(T.mark())
                for i, h in enumerate(hs):
                    V.tensor_tensor(out=S_f[h][:], in0=S_f[h][:],
                                    in1=psC[:, i * 128:(i + 1) * 128], op=ALU.add)
                if own:
                    for i in range(4):
                        V.tensor_tensor(out=PTh4[:, i * 128:(i + 1) * 128],
                                        in0=psQ[:, i * 128:(i + 1) * 128],
                                        in1=mask_tu[:], op=ALU.mult)
                    T.w(V.mark())
                    for i, h in enumerate(hs):
                        T.matmul(psR[:, i * 128:(i + 1) * 128], S_b[h][:],
                                 q_h[:, h * 512 + cs0: h * 512 + cs0 + 128],
                                 start=True, stop=False)
                        T.matmul(psR[:, i * 128:(i + 1) * 128],
                                 UtmG[:, i * 128:(i + 1) * 128],
                                 PTh4[:, i * 128:(i + 1) * 128], start=False, stop=True)
                        T.matmul(psRow[:, i * 128:(i + 1) * 128],
                                 wg2T[:, h * 128:(h + 1) * 128],
                                 xg1_buf[:, cs0:cs0 + 128], start=True, stop=True)
                    S.w(T.mark())
                    for i in range(4):
                        S.activation(out=gate4[:, i * 128:(i + 1) * 128],
                                     in_=psRow[:, i * 128:(i + 1) * 128], func=AF.Sigmoid)
                    V.w(S.mark())
                    for i in range(4):
                        V.tensor_tensor(out=ogf4[:, i * 128:(i + 1) * 128],
                                        in0=psR[:, i * 128:(i + 1) * 128],
                                        in1=gate4[:, i * 128:(i + 1) * 128], op=ALU.mult)
                    V.tensor_tensor(out=ksq[:], in0=ogf4[:], in1=ogf4[:], op=ALU.mult)
                    T.w(V.mark())
                    for i in range(4):
                        T.matmul(psC[0:1, i * 128:(i + 1) * 128], ones_col[:],
                                 ksq[:, i * 128:(i + 1) * 128], start=True, stop=True)
                    V.w(T.mark())
                    V.tensor_scalar(out=rowm[:], in0=psC[0:1, 0:512],
                                    scalar1=_f32(1.0 / 128.0), scalar2=EPS,
                                    op0=ALU.mult, op1=ALU.add)
                    S.w(V.mark())
                    S.activation(out=rowq[:], in_=rowm[:], func=AF.Sqrt)
                    V.w(S.mark())
                    V.reciprocal(out=rowr[:], in_=rowq[:])
                    V.tensor_tensor(out=rowq[:], in0=rowr[:], in1=rowr[:], op=ALU.mult)
                    V.tensor_tensor(out=rowq[:], in0=rowq[:], in1=rowm[:], op=ALU.mult)
                    V.tensor_scalar(out=rowq[:], in0=rowq[:], scalar1=-0.5,
                                    scalar2=1.5, op0=ALU.mult, op1=ALU.add)
                    V.tensor_tensor(out=rowr[:], in0=rowr[:], in1=rowq[:], op=ALU.mult)
                    T.w(V.mark())
                    T.matmul(psX[:, 0:512], ones_row[:], rowr[0:1, :], start=True, stop=True)
                    V.w(T.mark())
                    V.tensor_tensor(out=ogf4[:], in0=ogf4[:], in1=psX[:, :], op=ALU.mult)
                    for i, h in enumerate(hs):
                        V.tensor_scalar(out=og_all[:, h * 512 + cs0: h * 512 + cs0 + 128],
                                        in0=ogf4[:, i * 128:(i + 1) * 128],
                                        scalar1=gnc[:, h:h + 1],
                                        scalar2=None, op0=ALU.mult)
                for i, h in enumerate(hs):
                    V.tensor_copy(out=S_b[h][:], in_=S_f[h][:])

        def ln_tm(tile):
            V.tensor_reduce(out=colm[:], in_=tile[:], axis=AX.X, op=ALU.add)
            S.w(V.mark())
            S.activation(out=sq_t[:], in_=tile[:], func=AF.Square)
            V.w(S.mark())
            V.tensor_reduce(out=colq[:], in_=sq_t[:], axis=AX.X, op=ALU.add)
            V.tensor_scalar(out=colm[:], in0=colm[:], scalar1=_f32(1.0 / 2048.0),
                            scalar2=None, op0=ALU.mult)
            V.tensor_scalar(out=colq[:], in0=colq[:], scalar1=_f32(1.0 / 2048.0),
                            scalar2=None, op0=ALU.mult)
            V.tensor_tensor(out=colt[:], in0=colm[:], in1=colm[:], op=ALU.mult)
            V.tensor_tensor(out=colq[:], in0=colq[:], in1=colt[:], op=ALU.subtract)
            V.tensor_scalar(out=colq[:], in0=colq[:], scalar1=EPS_LN, scalar2=None,
                            op0=ALU.add)
            S.w(V.mark())
            S.activation(out=colt[:], in_=colq[:], func=AF.Sqrt)
            V.w(S.mark())
            V.reciprocal(out=colr[:], in_=colt[:])
            V.tensor_tensor(out=colt[:], in0=colr[:], in1=colr[:], op=ALU.mult)
            V.tensor_tensor(out=colt[:], in0=colt[:], in1=colq[:], op=ALU.mult)
            V.tensor_scalar(out=colt[:], in0=colt[:], scalar1=-0.5, scalar2=1.5,
                            op0=ALU.mult, op1=ALU.add)
            V.tensor_tensor(out=colr[:], in0=colr[:], in1=colt[:], op=ALU.mult)
            V.tensor_scalar(out=tile[:], in0=tile[:], scalar1=colm[:, 0:1],
                            scalar2=None, op0=ALU.subtract)
            V.tensor_scalar(out=tile[:], in0=tile[:], scalar1=colr[:, 0:1],
                            scalar2=None, op0=ALU.mult)

        def tail_strip(si):
            t0 = si * 512
            banks = [psX, psQ, psC, psR]
            for mc in range(4):
                for kk in range(16):
                    mld = SY.dma(worhs[kk % 2][:],
                                 woT_d[kk * 128:(kk + 1) * 128, mc * 512:(mc + 1) * 512],
                                 wait=T.mark())
                    T.w(mld)
                    if kk == 0:
                        T.w(V.mark())
                        T.w(S.mark())
                    for tt in range(4):
                        T.matmul(banks[tt][:, 0:512],
                                 og_all[:, kk * 512 + tt * 128: kk * 512 + tt * 128 + 128],
                                 worhs[kk % 2][:], start=(kk == 0), stop=(kk == 15))
                V.w(T.mark())
                for tt in range(4):
                    V.tensor_copy(out=y1[tt][:, mc * 512:(mc + 1) * 512],
                                  in_=banks[tt][:, 0:512])
            for tt in range(4):
                mld = SY.dma(x_tm[tt % 2][:],
                             x_own_d[t0 + tt * 128: t0 + (tt + 1) * 128, :],
                             wait=V.mark())
                V.w(mld)
                V.tensor_tensor(out=y1[tt][:], in0=y1[tt][:], in1=x_tm[tt % 2][:],
                                op=ALU.add)
                ln_tm(y1[tt])
                T.w(V.mark())
                for g in range(4):
                    if g > 0:
                        T.w(V.mark())
                    for k4 in range(4):
                        kk = g * 4 + k4
                        T.transpose(psX[:, k4 * 128:(k4 + 1) * 128],
                                    y1[tt][:, kk * 128:(kk + 1) * 128], ident_s[:])
                    V.w(T.mark())
                    for k4 in range(4):
                        kk = g * 4 + k4
                        V.tensor_copy(out=x1b[:, kk * 512 + tt * 128: kk * 512 + tt * 128 + 128],
                                      in_=psX[:, k4 * 128:(k4 + 1) * 128])
            NH = NF // 2
            for half in range(2):
                f0 = half * NH
                gel_marks = {}
                for mg in range(NH // 2):
                    mgl = f0 // 2 + mg
                    src = updq_d[:, mgl * 256:(mgl + 1) * 256].rearrange("(a p) c -> p a c", p=128)
                    mld = SY.dma(upw[mg % 2][:].rearrange("p (a c) -> p a c", a=16), src,
                                 wait=T.mark())
                    T.w(mld)
                    if mg < 2:
                        T.w(S.mark())
                    else:
                        T.w(gel_marks[mg - 2])
                    if mg == 0:
                        T.w(V.mark())
                    bk = [banks[(2 * mg) % 4], banks[(2 * mg + 1) % 4]]
                    for m2 in range(2):
                        for kk in range(16):
                            T.matmul(bk[m2][:, 0:512],
                                     upw[mg % 2][:, kk * 256 + m2 * 128: kk * 256 + m2 * 128 + 128],
                                     x1b[:, kk * 512:(kk + 1) * 512],
                                     start=(kk == 0), stop=(kk == 15))
                    S.w(T.mark())
                    for m2 in range(2):
                        m = mg * 2 + m2
                        S.activation(out=h_sb[:, m * 512:(m + 1) * 512], in_=bk[m2][:, 0:512],
                                     func=AF.Gelu_apprx_tanh, bias=ub[:, f0 + m:f0 + m + 1])
                    gel_marks[mg] = S.mark()
                for mc in range(4):
                    T.w(V.mark())
                    T.w(S.mark())
                    if half == 0:
                        for tt in range(4):
                            T.matmul(banks[tt][:, 0:512], ones_row[:],
                                     dbrow[0:1, mc * 512:(mc + 1) * 512],
                                     start=True, stop=False)
                    for kf in range(NH):
                        mld = SY.dma(dnw[kf % 2][:],
                                     dndq_d[(f0 + kf) * 128:(f0 + kf + 1) * 128,
                                            mc * 512:(mc + 1) * 512],
                                     wait=T.mark())
                        T.w(mld)
                        for tt in range(4):
                            T.matmul(banks[tt][:, 0:512],
                                     h_sb[:, kf * 512 + tt * 128: kf * 512 + tt * 128 + 128],
                                     dnw[kf % 2][:],
                                     start=(half == 1 and kf == 0),
                                     stop=(kf == NH - 1))
                    V.w(T.mark())
                    for tt in range(4):
                        V.tensor_tensor(out=y1[tt][:, mc * 512:(mc + 1) * 512],
                                        in0=y1[tt][:, mc * 512:(mc + 1) * 512],
                                        in1=banks[tt][:, 0:512], op=ALU.add)
            for tt in range(4):
                ln_tm(y1[tt])
                GP.dma(y_d[t0 + tt * 128: t0 + (tt + 1) * 128, :], y1[tt][:],
                       wait=V.mark())
            V.w(GP.mark())

        for si in range(NS_PRE):
            proj_strip(si, x_pre_d, own=False)
            for c_loc in range(4):
                delta_chunk(c_loc, own=False)
        for si in range(NS_OWN):
            proj_strip(si, x_own_d, own=True)
            for c_loc in range(4):
                delta_chunk(c_loc, own=True)
            tail_strip(si)
        GP.drain()
        SY.drain()

    run_passes(nc, sems, prog)
    return nc


import ml_dtypes

_prog_cache = {}


def _program():
    if "nc" not in _prog_cache:
        _prog_cache["nc"] = build_fused(n_own=1024, n_pre=3072, FF=8192)
    return _prog_cache["nc"]


def kernel(**ins):
    from concourse.bass_utils import run_bass_kernel_spmd
    x = np.asarray(ins["L0_attn_x"], dtype=np.float32)
    ii, jj = np.indices((128, 128))
    shared = {
        "Wq": np.asarray(ins["L0_attn_Wq"], np.float32),
        "Wk": np.asarray(ins["L0_attn_Wk"], np.float32),
        "Wv": np.asarray(ins["L0_attn_Wv"], np.float32),
        "Wbeta": np.asarray(ins["L0_attn_Wbeta"], np.float32),
        "Wg1": np.asarray(ins["L0_attn_Wg1"], np.float32),
        "Wg2": np.asarray(ins["L0_attn_Wg2"], np.float32),
        "gnorm_cols": np.ascontiguousarray(
            np.asarray(ins["L0_attn_g_norm"], np.float32).reshape(16, 128).T),
        "Wo": np.asarray(ins["L0_attn_Wo"], np.float32),
        "up_w": np.asarray(ins["L0_ffn_up_kernel"], np.float32),
        "dn_w": np.asarray(ins["L0_ffn_down_kernel"], np.float32),
        "ub_cols": np.ascontiguousarray(
            np.asarray(ins["L0_ffn_up_bias"], np.float32).reshape(64, 128).T),
        "db_row": np.ascontiguousarray(
            np.asarray(ins["L0_ffn_down_bias"], np.float32).reshape(1, 2048)),
        "ident": np.eye(128, dtype=np.float32),
        "mask_lowneg": np.where(ii > jj, -1.0, 0.0).astype(np.float32),
        "mask_triu": np.where(ii <= jj, 1.0, 0.0).astype(ml_dtypes.bfloat16),
    }
    nc = _program()
    in_maps = []
    for c in range(8):
        b, p = c // 4, c % 4
        xp = np.zeros((3072, 2048), np.float32)
        if p:
            xp[3072 - 1024 * p:] = x[b, :1024 * p]
        m = dict(shared)
        m["x_pre"] = xp
        m["x_own"] = np.ascontiguousarray(x[b, 1024 * p:1024 * (p + 1)])
        in_maps.append(m)
    res = run_bass_kernel_spmd(nc, in_maps, core_ids=list(range(8)))
    out = np.empty((2, 4096, 2048), np.float32)
    for c in range(8):
        b, p = c // 4, c % 4
        out[b, 1024 * p:1024 * (p + 1)] = res.results[c]["y_out"]
    return out


# version 05
# speedup vs baseline: 1.3167x; 1.1881x over previous
import numpy as np
from contextlib import ExitStack
import concourse.bass as bass
import concourse.mybir as mybir

F32 = mybir.dt.float32
BF16 = mybir.dt.bfloat16
AF = mybir.ActivationFunctionType
ALU = mybir.AluOpType
AX = mybir.AxisListType

_STAGES = [16] * 7 + [8, 4, 2, 1]


def _f32(x):
    return float(np.float32(x))


class SE:

    def __init__(self, eng, sem, self_chain=True):
        self.e = eng
        self.sem = sem
        self.cnt = 0
        self.self_chain = self_chain

    def w(self, sem_cnt):
        sem, val = sem_cnt
        if val > 0:
            self.e.wait_ge(sem, val)

    def mark(self):
        return (self.sem, self.cnt)

    def __getattr__(self, name):
        fn = getattr(self.e, name)
        if not callable(fn):
            return fn

        def wrapped(*a, **k):
            inst = fn(*a, **k)
            if self.self_chain and self.cnt > 0:
                inst._wait_ge(self.sem, self.cnt)
            self.cnt += 1
            inst.then_inc(self.sem, 1)
            return inst

        return wrapped


QUANT_VEC_OPS = 1 + 1 + 1 + 4 * len(_STAGES) + 1 + 1 + 4 + 2 + 1 + 2 + 1 + 1


def emit_quant_vec(V, qb, out_ap):
    aa, sg = qb["aa"], qb["sg"]
    amax, rr, ss, mm, tt = qb["amax"], qb["rr"], qb["ss"], qb["mm"], qb["tt"]
    xh, acc, bb, cc, qq = qb["xh"], qb["acc"], qb["bb"], qb["cc"], qb["qq"]
    NB = 16
    c0 = V.cnt
    V.tensor_reduce(out=amax[:], in_=aa[:].rearrange("p (a b) -> p a b", b=32),
                    axis=AX.X, op=ALU.max)
    V.tensor_scalar(out=mm[:], in0=amax[:], scalar1=_f32(1e-30),
                    scalar2=_f32(2.0 ** 100), op0=ALU.max, op1=ALU.mult)
    V.memset(rr[:], _f32(2.0 ** 102))
    for E in _STAGES:
        V.tensor_scalar(out=tt[:], in0=mm[:], scalar1=_f32(2.0 ** E),
                        scalar2=None, op0=ALU.is_lt)
        V.tensor_scalar(out=tt[:], in0=tt[:], scalar1=_f32(1.0 - 2.0 ** (-E)),
                        scalar2=_f32(2.0 ** (-E)), op0=ALU.mult, op1=ALU.add)
        V.tensor_tensor(out=mm[:], in0=mm[:], in1=tt[:], op=ALU.mult)
        V.tensor_tensor(out=rr[:], in0=rr[:], in1=tt[:], op=ALU.mult)
    V.reciprocal(out=ss[:], in_=rr[:])
    rbc = rr[:].unsqueeze(2).broadcast_to([128, NB, 32])
    V.tensor_tensor(out=xh[:].rearrange("p (a b) -> p a b", b=32),
                    in0=aa[:].rearrange("p (a b) -> p a b", b=32),
                    in1=rbc, op=ALU.mult)
    V.tensor_scalar(out=acc[:], in0=xh[:], scalar1=0.25, scalar2=None, op0=ALU.is_ge)
    for m in (0.75, 1.25, 1.75):
        V.scalar_tensor_tensor(out=acc[:], in0=xh[:], scalar=m, in1=acc[:],
                               op0=ALU.is_ge, op1=ALU.add)
    V.tensor_scalar(out=bb[:], in0=xh[:], scalar1=2.5, scalar2=None, op0=ALU.is_ge)
    V.scalar_tensor_tensor(out=bb[:], in0=xh[:], scalar=3.5, in1=bb[:],
                           op0=ALU.is_ge, op1=ALU.add)
    V.tensor_scalar(out=cc[:], in0=xh[:], scalar1=5.0, scalar2=None, op0=ALU.is_ge)
    V.scalar_tensor_tensor(out=qq[:], in0=acc[:], scalar=0.5, in1=bb[:],
                           op0=ALU.mult, op1=ALU.add)
    V.scalar_tensor_tensor(out=qq[:], in0=cc[:], scalar=2.0, in1=qq[:],
                           op0=ALU.mult, op1=ALU.add)
    sbc = ss[:].unsqueeze(2).broadcast_to([128, NB, 32])
    V.tensor_tensor(out=qq[:].rearrange("p (a b) -> p a b", b=32),
                    in0=qq[:].rearrange("p (a b) -> p a b", b=32),
                    in1=sbc, op=ALU.mult)
    V.tensor_tensor(out=out_ap, in0=qq[:], in1=sg[:], op=ALU.mult)
    assert V.cnt - c0 == QUANT_VEC_OPS, (V.cnt - c0, QUANT_VEC_OPS)


def quant_bufs(nc, tag=""):
    sb = nc.alloc_sbuf_tensor
    return {
        "aa": sb("q_aa" + tag, [128, 512], F32),
        "sg": sb("q_sg" + tag, [128, 512], F32),
        "amax": sb("q_amax" + tag, [128, 16], F32),
        "rr": sb("q_rr" + tag, [128, 16], F32),
        "ss": sb("q_ss" + tag, [128, 16], F32),
        "mm": sb("q_mm" + tag, [128, 16], F32),
        "tt": sb("q_tt" + tag, [128, 16], F32),
        "xh": sb("q_xh" + tag, [128, 512], F32),
        "acc": sb("q_acc" + tag, [128, 512], F32),
        "bb": sb("q_bb" + tag, [128, 512], F32),
        "cc": sb("q_cc" + tag, [128, 512], F32),
        "qq": sb("q_qq" + tag, [128, 512], F32),
    }



def build_dequant(shapes):
    nc = bass.Bass("TRN2", target_bir_lowering=False)
    ident_d = nc.dram_tensor("ident", [128, 128], F32, kind="ExternalInput").ap()
    ins, outs, strips = {}, {}, []
    for name, R, C in shapes:
        assert R % 128 == 0 and C % 512 == 0
        ins[name] = nc.dram_tensor(name, [R, C], F32, kind="ExternalInput").ap()
        outs[name] = nc.dram_tensor(name + "_dq", [R, C], BF16, kind="ExternalOutput").ap()
        for rt in range(R // 128):
            for cs in range(C // 512):
                strips.append((name, rt, cs))
    nst = len(strips)
    VC = QUANT_VEC_OPS + 1
    SC = 3
    TC = 8

    sb = nc.alloc_sbuf_tensor
    pst = nc.alloc_psum_tensor
    es = ExitStack()
    sem = lambda n: es.enter_context(nc.semaphore(n))

    ident_s = sb("ident_s", [128, 128], F32)
    identb = sb("identb", [128, 128], BF16)
    ldb = [sb("ld0", [128, 512], F32), sb("ld1", [128, 512], F32)]
    wt = sb("wt", [128, 512], F32)
    dqt = sb("dqt", [128, 512], BF16)
    stb = [sb("st0", [128, 512], BF16), sb("st1", [128, 512], BF16)]
    qb = quant_bufs(nc)
    psA = pst("psA", [128, 512], F32)
    psB = pst("psB", [128, 512], BF16)

    smV, smS, smT = sem("smV"), sem("smS"), sem("smT")
    idm = sem("idm")
    lds = [sem("lda"), sem("ldb")]
    sts = [sem("sta"), sem("stb")]

    def v_after_dqt(i):
        return 1 + i * VC + QUANT_VEC_OPS

    def v_after_staging(i):
        return 1 + (i + 1) * VC

    with nc.Block() as block:
        @block.sync
        def _(sync):
            sync.dma_start(out=ident_s[:], in_=ident_d[:]).then_inc(idm, 16)
            for i, (name, rt, cs) in enumerate(strips):
                inst = sync.dma_start(
                    out=ldb[i % 2][:],
                    in_=ins[name][rt * 128:(rt + 1) * 128, cs * 512:(cs + 1) * 512])
                if i >= 2:
                    inst._wait_ge(smT, TC * (i - 2) + 4)
                inst.then_inc(lds[i % 2], 16)

        @block.gpsimd
        def _(g):
            for i, (name, rt, cs) in enumerate(strips):
                g.wait_ge(smV, v_after_staging(i))
                g.dma_start(out=outs[name][rt * 128:(rt + 1) * 128, cs * 512:(cs + 1) * 512],
                            in_=stb[i % 2][:]).then_inc(sts[i % 2], 16)
            for par in range(2):
                n = len([1 for i in range(nst) if i % 2 == par])
                if n:
                    g.wait_ge(sts[par], 16 * n)

        @block.scalar
        def _(s):
            S = SE(s, smS)
            for i in range(nst):
                S.w((smT, TC * i + 4))
                if i > 0:
                    S.w((smV, v_after_dqt(i - 1)))
                S.copy(out=wt[:], in_=psA[:])
                S.activation(out=qb["aa"][:], in_=wt[:], func=AF.Abs)
                S.activation(out=qb["sg"][:], in_=wt[:], func=AF.Sign)
            assert S.cnt == SC * nst

        @block.vector
        def _(v):
            V = SE(v, smV)
            V.w((idm, 16))
            V.tensor_copy(out=identb[:], in_=ident_s[:])
            for i in range(nst):
                V.w((smS, SC * i + SC))
                emit_quant_vec(V, qb, dqt[:])
                V.w((smT, TC * i + TC))
                if i >= 2:
                    V.w((sts[i % 2], 16 * (i // 2)))
                V.tensor_copy(out=stb[i % 2][:], in_=psB[:])
            assert V.cnt == 1 + VC * nst

        @block.tensor
        def _(t):
            T = SE(t, smT, self_chain=False)
            T.w((idm, 16))
            T.w((smV, 1))
            for i in range(nst):
                T.w((lds[i % 2], 16 * (i // 2 + 1)))
                if i > 0:
                    T.w((smS, SC * (i - 1) + 1))
                buf = ldb[i % 2]
                for kk in range(4):
                    T.transpose(psA[:, kk * 128:(kk + 1) * 128],
                                buf[:, kk * 128:(kk + 1) * 128], ident_s[:])
                T.w((smV, v_after_dqt(i)))
                if i > 0:
                    T.w((smV, v_after_staging(i - 1)))
                for kk in range(4):
                    T.transpose(psB[:, kk * 128:(kk + 1) * 128],
                                dqt[:, kk * 128:(kk + 1) * 128], identb[:])
            assert T.cnt == TC * nst

    return nc, [n for n, _, _ in shapes]


def ref_dequant(w):
    FP4_GRID = np.array([0.0, 0.5, 1.0, 1.5, 2.0, 3.0, 4.0, 6.0], dtype=w.dtype)
    FP4_MIDS = np.array([0.25, 0.75, 1.25, 1.75, 2.5, 3.5, 5.0], dtype=w.dtype)
    K, N = w.shape
    wb = w.reshape(K // 32, 32, N)
    amax = np.max(np.abs(wb), axis=1, keepdims=True)
    expo = np.floor(np.log2(np.maximum(amax, 1e-30))) - 2.0
    scales = np.exp2(expo)
    x = wb / scales
    idx = np.digitize(np.abs(x), FP4_MIDS)
    q = np.sign(x) * FP4_GRID[idx]
    return (q * scales).reshape(K, N)




class PEng:
    def __init__(self, name, sems, eng=None, live=False, self_chain=True):
        self.name, self.sems, self.e = name, sems, eng
        self.live, self.self_chain, self.cnt = live, self_chain, 0

    def mark(self):
        return (self.name, self.cnt)

    def w(self, dep):
        if dep is None:
            return
        nm, val = dep
        if nm == self.name or val <= 0:
            return
        if self.live:
            self.e.wait_ge(self.sems[nm], val)

    def __getattr__(self, op):
        import os
        nochain = bool(os.environ.get("MK_NO_SELF_CHAIN"))
        def f(*a, **k):
            self.cnt += 1
            if self.live:
                inst = getattr(self.e, op)(*a, **k)
                if self.self_chain and self.cnt > 1 and not nochain:
                    inst._wait_ge(self.sems[self.name], self.cnt - 1)
                inst.then_inc(self.sems[self.name], 1)
                return inst
            return None
        return f


class PDma:
    def __init__(self, name, sems, eng=None, live=False):
        self.name, self.sems, self.e = name, sems, eng
        self.live, self.cnt = live, 0

    def dma(self, out=None, in_=None, wait=None):
        self.cnt += 1
        if self.live:
            if self.cnt > 1:
                self.e.wait_ge(self.sems[self.name], 16 * (self.cnt - 1))
            inst = self.e.dma_start(out=out, in_=in_)
            if wait is not None:
                nm, val = wait
                if val > 0:
                    inst._wait_ge(self.sems[nm], val)
            inst.then_inc(self.sems[self.name], 16)
        return (self.name, self.cnt * 16)

    def mark(self):
        return (self.name, self.cnt * 16)

    def w(self, dep):
        if dep is None:
            return
        nm, val = dep
        if nm == self.name or val <= 0:
            return
        if self.live:
            self.e.wait_ge(self.sems[nm], val)

    def drain(self):
        if self.live and self.cnt:
            self.e.wait_ge(self.sems[self.name], self.cnt * 16)


def run_passes(nc, sems, prog):
    with nc.Block() as block:
        @block.sync
        def _(sync):
            prog(PEng("T", sems), PEng("V", sems), PEng("S", sems),
                 PDma("SY", sems, sync, True), PDma("GP", sems))

        @block.gpsimd
        def _(g):
            prog(PEng("T", sems), PEng("V", sems), PEng("S", sems),
                 PDma("SY", sems), PDma("GP", sems, g, True))

        @block.scalar
        def _(s):
            prog(PEng("T", sems), PEng("V", sems), PEng("S", sems, s, True),
                 PDma("SY", sems), PDma("GP", sems))

        @block.vector
        def _(v):
            prog(PEng("T", sems), PEng("V", sems, v, True), PEng("S", sems),
                 PDma("SY", sems), PDma("GP", sems))

        @block.tensor
        def _(t):
            prog(PEng("T", sems, t, True, self_chain=False), PEng("V", sems),
                 PEng("S", sems), PDma("SY", sems), PDma("GP", sems))



def build_attn(n_tok, dbg_at=None):
    assert n_tok % 512 == 0
    NS = n_tok // 512
    nc = bass.Bass("TRN2", target_bir_lowering=False)
    dr = lambda n, s, dt=F32, k="ExternalInput": nc.dram_tensor(n, s, dt, kind=k).ap()
    xb = dr("xb", [n_tok, 2048])
    Wq_s = dr("Wq_s", [512, 2048])
    Wk_s = dr("Wk_s", [512, 2048])
    Wv_s = dr("Wv_s", [512, 2048])
    Wbeta_s = dr("Wbeta_s", [4, 2048])
    Wg1 = dr("Wg1", [128, 2048])
    Wg2_s = dr("Wg2_s", [512, 128])
    gnorm_cols = dr("gnorm_cols", [128, 4])
    ident_d = dr("ident", [128, 128])
    mask_ln_d = dr("mask_lowneg", [128, 128])
    mask_tu_d = dr("mask_triu", [128, 128], BF16)
    og_out = dr("og_out", [512, n_tok], F32, k="ExternalOutput")
    dbg_out = dr("dbg_out", [128, 512], F32, k="ExternalOutput") if dbg_at is not None else None

    sb = nc.alloc_sbuf_tensor
    pst = nc.alloc_psum_tensor
    es = ExitStack()

    ident_s = sb("ident_s", [128, 128], F32)
    identb = sb("identb", [128, 128], BF16)
    mask_ln = sb("mask_ln", [128, 128], F32)
    mask_tu = sb("mask_tu", [128, 128], BF16)
    gnc = sb("gnc", [128, 4], F32)
    wqT = sb("wqT", [128, 16 * 512], BF16)
    wkT = sb("wkT", [128, 16 * 512], BF16)
    wvT = sb("wvT", [128, 16 * 512], BF16)
    wbT = sb("wbT", [128, 16 * 4], BF16)
    wg1T = sb("wg1T", [128, 16 * 128], BF16)
    wg2T = sb("wg2T", [128, 4 * 128], BF16)
    wld = [sb("wld0", [128, 2048], F32), sb("wld1", [128, 2048], F32)]
    x_tm = [sb("x_tm0", [128, 2048], F32), sb("x_tm1", [128, 2048], F32)]
    x_cm = sb("x_cm", [128, 16 * 512], BF16)
    q_h = [sb(f"q_{h}", [128, 512], BF16) for h in range(4)]
    kr_h = [sb(f"kr_{h}", [128, 512], BF16) for h in range(4)]
    k_h = [sb(f"k_{h}", [128, 512], BF16) for h in range(4)]
    v_h = [sb(f"v_{h}", [128, 512], BF16) for h in range(4)]
    ksq = sb("ksq", [128, 512], F32)
    xg1 = sb("xg1", [128, 512], BF16)
    beta_sb = sb("beta_sb", [4, 512], F32)
    rowb = [sb(f"rowb{i}", [1, 512], F32) for i in range(3)]
    ones_col = sb("ones_col", [128, 1], F32)
    ones_row = sb("ones_row", [1, 128], F32)
    Ablk = sb("Ablk", [32, 10 * 32], F32)
    Ablk_bf = sb("Ablk_bf", [32, 6 * 32], BF16)
    W32 = [sb(f"W32_{j}", [32, 32], F32) for j in range(5)]
    Z32 = [sb(f"Z32_{j}", [32, 32], F32) for j in range(5)]
    Gg = [sb("Gg_a", [32, 32], F32), sb("Gg_c", [32, 32], F32)]
    PTb4 = sb("PTb4", [32, 4 * 128], BF16)
    Rpre = sb("Rpre", [128, 128], BF16)
    rb4 = sb("rb4", [32, 4 * 128], F32)
    Ub4 = sb("Ub4", [32, 4 * 128], BF16)
    Ktb4 = sb("Ktb4", [32, 4 * 128], BF16)
    betaT4 = sb("betaT4", [32, 16 * 4], F32)
    nbetaT4 = sb("nbetaT4", [32, 16 * 4], F32)
    S_f = [sb(f"Sf_{h}", [128, 128], F32) for h in range(4)]
    S_b = [sb(f"Sb_{h}", [128, 128], BF16) for h in range(4)]
    ogf = sb("ogf", [128, 128], F32)
    ogsq = sb("ogsq", [128, 128], F32)
    gate = sb("gate", [128, 128], BF16)
    og_st = [sb(f"og_st{h}", [128, 512], F32) for h in range(4)]
    rb_s = sb("rb_s", [1, 128], F32)
    rb_t = sb("rb_t", [1, 128], F32)
    rb_u = sb("rb_u", [1, 128], F32)
    rb_v = sb("rb_v", [1, 128], F32)

    psX = pst("psX", [128, 512], F32)
    psQ = pst("psQ", [128, 512], F32)
    psC = pst("psC", [128, 512], F32)
    psR = pst("psR", [128, 512], F32)
    psBt = pst("psBt", [128, 1024], BF16)
    psRow = pst("psRow", [128, 512], F32)

    sems = {}
    for nm in ("T", "V", "S", "SY", "GP"):
        sems[nm] = es.enter_context(nc.semaphore("sem" + nm))

    EPS = _f32(1e-6)

    def prog(T, V, S, SY, GP):
        SY.dma(ident_s[:], ident_d)
        SY.dma(mask_ln[:], mask_ln_d)
        SY.dma(mask_tu[:], mask_tu_d)
        m_cst = SY.dma(gnc[:], gnorm_cols)
        V.w(m_cst)
        V.tensor_copy(out=identb[:], in_=ident_s[:])
        V.memset(ones_col[:], 1.0)
        V.memset(ones_row[:], 1.0)
        for h in range(4):
            V.memset(S_f[h][:], 0.0)
            V.memset(S_b[h][:], 0.0)

        loads = []
        for mat in range(3):
            for m in range(4):
                loads.append(("qkv", mat, m))
        loads.append(("wg1", 0, 0))
        loads.append(("wbeta", 0, 0))
        for t4 in range(4):
            loads.append(("wg2", 0, t4))
        tmark_after_load = [None] * len(loads)
        vmark_after_load = [None] * len(loads)
        ld_marks = []
        for i, (kind, mat, m) in enumerate(loads):
            src = None
            if kind == "qkv":
                src = (Wq_s, Wk_s, Wv_s)[mat][m * 128:(m + 1) * 128, :]
                dst = wld[i % 2][:, :]
            elif kind == "wg1":
                src = Wg1
                dst = wld[i % 2][:, :]
            elif kind == "wbeta":
                src = Wbeta_s
                dst = wld[i % 2][0:4, :]
            else:
                src = Wg2_s[m * 128:(m + 1) * 128, :]
                dst = wld[i % 2][:, 0:128]
            wait = tmark_after_load[i - 2] if i >= 2 else None
            ld_marks.append(SY.dma(dst, src, wait=wait))

            T.w(ld_marks[i])
            T.w(V.mark())
            if kind == "qkv":
                dstT = (wqT, wkT, wvT)[mat]
                for g in range(4):
                    if g > 0:
                        T.w(V.mark())
                    for kk4 in range(4):
                        kk = g * 4 + kk4
                        T.transpose(psX[:, kk4 * 128:(kk4 + 1) * 128],
                                    wld[i % 2][:, kk * 128:(kk + 1) * 128], ident_s[:])
                    mt = T.mark()
                    V.w(mt)
                    for kk4 in range(4):
                        kk = g * 4 + kk4
                        V.tensor_copy(out=dstT[:, kk * 512 + m * 128: kk * 512 + m * 128 + 128],
                                      in_=psX[:, kk4 * 128:(kk4 + 1) * 128])
            elif kind == "wg1":
                for g in range(4):
                    if g > 0:
                        T.w(V.mark())
                    for kk4 in range(4):
                        kk = g * 4 + kk4
                        T.transpose(psX[:, kk4 * 128:(kk4 + 1) * 128],
                                    wld[i % 2][:, kk * 128:(kk + 1) * 128], ident_s[:])
                    mt = T.mark()
                    V.w(mt)
                    for kk4 in range(4):
                        kk = g * 4 + kk4
                        V.tensor_copy(out=wg1T[:, kk * 128:(kk + 1) * 128],
                                      in_=psX[:, kk4 * 128:(kk4 + 1) * 128])
            elif kind == "wbeta":
                for g in range(4):
                    if g > 0:
                        T.w(V.mark())
                    for kk4 in range(4):
                        kk = g * 4 + kk4
                        T.transpose(psX[:, kk4 * 128:kk4 * 128 + 4],
                                    wld[i % 2][0:4, kk * 128:(kk + 1) * 128], ident_s[0:4, 0:4])
                    mt = T.mark()
                    V.w(mt)
                    for kk4 in range(4):
                        kk = g * 4 + kk4
                        V.tensor_copy(out=wbT[:, kk * 4:(kk + 1) * 4],
                                      in_=psX[:, kk4 * 128:kk4 * 128 + 4])
            else:
                T.transpose(psX[:, 0:128], wld[i % 2][:, 0:128], ident_s[:])
                mt = T.mark()
                V.w(mt)
                V.tensor_copy(out=wg2T[:, m * 128:(m + 1) * 128], in_=psX[:, 0:128])
            tmark_after_load[i] = T.mark()
            vmark_after_load[i] = V.mark()

        xld_marks = {}
        t_after_x = {}
        for si in range(NS):
            for tt in range(4):
                gi = si * 4 + tt
                wait = t_after_x.get(gi - 2)
                xld_marks[gi] = SY.dma(x_tm[gi % 2][:],
                                       xb[gi * 128:(gi + 1) * 128, :], wait=wait)
                T.w(xld_marks[gi])
                T.w(V.mark())
                for g in range(4):
                    if g > 0:
                        T.w(V.mark())
                    for kk4 in range(4):
                        kk = g * 4 + kk4
                        T.transpose(psX[:, kk4 * 128:(kk4 + 1) * 128],
                                    x_tm[gi % 2][:, kk * 128:(kk + 1) * 128], ident_s[:])
                    mt = T.mark()
                    V.w(mt)
                    for kk4 in range(4):
                        kk = g * 4 + kk4
                        V.tensor_copy(out=x_cm[:, kk * 512 + tt * 128: kk * 512 + tt * 128 + 128],
                                      in_=psX[:, kk4 * 128:(kk4 + 1) * 128])
                t_after_x[gi] = T.mark()

            m_xcm = V.mark()

            T.w(m_xcm)
            for kk in range(16):
                T.matmul(psRow[0:4, :], wbT[:, kk * 4:(kk + 1) * 4],
                         x_cm[:, kk * 512:(kk + 1) * 512],
                         start=(kk == 0), stop=(kk == 15))
            S.w(T.mark())
            S.activation(out=beta_sb[:], in_=psRow[0:4, :], func=AF.Sigmoid)
            V.w(S.mark())
            V.tensor_scalar(out=beta_sb[:], in0=beta_sb[:], scalar1=2.0,
                            scalar2=None, op0=ALU.mult)
            T.w(V.mark())
            for j16 in range(16):
                T.transpose(psX[0:32, j16 * 4:j16 * 4 + 4],
                            beta_sb[0:4, j16 * 32:(j16 + 1) * 32], ident_s[0:4, 0:4])
            V.w(T.mark())
            V.tensor_copy(out=betaT4[:, :], in_=psX[0:32, 0:64])
            V.tensor_scalar(out=nbetaT4[:, :], in0=betaT4[:, :], scalar1=-1.0,
                            scalar2=None, op0=ALU.mult)

            for h in range(4):
                for mat, (wT, dst) in enumerate(((wqT, q_h[h]), (wkT, kr_h[h]), (wvT, v_h[h]))):
                    T.w(V.mark())
                    T.w(S.mark())
                    for kk in range(16):
                        T.matmul(psQ[:, :], wT[:, kk * 512 + h * 128: kk * 512 + h * 128 + 128],
                                 x_cm[:, kk * 512:(kk + 1) * 512],
                                 start=(kk == 0), stop=(kk == 15))
                    mt = T.mark()
                    S.w(mt)
                    if mat == 0:
                        S.activation(out=dst[:], in_=psQ[:, :], func=AF.Silu)
                    elif mat == 1:
                        S.activation(out=dst[:], in_=psQ[:, :], func=AF.Silu)
                    else:
                        S.copy(out=dst[:], in_=psQ[:, :])
                V.w(S.mark())
                V.tensor_tensor(out=ksq[:], in0=kr_h[h][:], in1=kr_h[h][:], op=ALU.mult)
                T.w(V.mark())
                T.matmul(psRow[0:1, :], ones_col[:], ksq[:], start=True, stop=True)
                V.w(T.mark())
                V.tensor_scalar(out=rowb[0][:], in0=psRow[0:1, :], scalar1=EPS,
                                scalar2=None, op0=ALU.add)
                S.w(V.mark())
                S.activation(out=rowb[1][:], in_=rowb[0][:], func=AF.Sqrt)
                V.w(S.mark())
                V.reciprocal(out=rowb[2][:], in_=rowb[1][:])
                V.tensor_tensor(out=rowb[1][:], in0=rowb[2][:], in1=rowb[2][:], op=ALU.mult)
                V.tensor_tensor(out=rowb[1][:], in0=rowb[1][:], in1=rowb[0][:], op=ALU.mult)
                V.tensor_scalar(out=rowb[1][:], in0=rowb[1][:], scalar1=-0.5,
                                scalar2=1.5, op0=ALU.mult, op1=ALU.add)
                V.tensor_tensor(out=rowb[2][:], in0=rowb[2][:], in1=rowb[1][:], op=ALU.mult)
                T.w(V.mark())
                T.matmul(psX[:, 0:512], ones_row[:], rowb[2][:], start=True, stop=True)
                V.w(T.mark())
                V.tensor_tensor(out=k_h[h][:], in0=kr_h[h][:], in1=psX[:, :], op=ALU.mult)

            T.w(V.mark())
            for kk in range(16):
                T.matmul(psQ[:, :], wg1T[:, kk * 128:(kk + 1) * 128],
                         x_cm[:, kk * 512:(kk + 1) * 512],
                         start=(kk == 0), stop=(kk == 15))
            V.w(T.mark())
            V.tensor_copy(out=xg1[:], in_=psQ[:, :])

            for c_loc in range(4):
                for h in range(4):
                    cs = slice(c_loc * 128, (c_loc + 1) * 128)
                    Kc = k_h[h][:, cs]
                    Qc = q_h[h][:, cs]
                    Vc = v_h[h][:, cs]

                    def sb32(b):
                        return slice((c_loc * 4 + b) * 32, (c_loc * 4 + b) * 32 + 32)

                    def bcol(b, neg=False):
                        t = nbetaT4 if neg else betaT4
                        j = (c_loc * 4 + b) * 4 + h
                        return t[:, j:j + 1]

                    pairs = [(b, b) for b in range(4)] + \
                            [(b, b2) for b in range(4) for b2 in range(b + 1, 4)]
                    T.w(V.mark())
                    for idx, (b, b2) in enumerate(pairs):
                        T.matmul(psQ[0:32, idx * 32:(idx + 1) * 32],
                                 Kc[:, b * 32:(b + 1) * 32], Kc[:, b2 * 32:(b2 + 1) * 32],
                                 start=True, stop=True)
                    V.w(T.mark())
                    V.tensor_copy(out=Ablk[:, :], in_=psQ[0:32, 0:320])
                    V.tensor_copy(out=Ablk_bf[:, :], in_=psQ[0:32, 128:320])
                    T.w(V.mark())
                    T.matmul(psR[:, 0:128], S_b[h][:], Kc, start=True, stop=True)
                    V.w(T.mark())
                    V.scalar_tensor_tensor(out=Rpre[:], in0=psR[:, 0:128], scalar=-1.0,
                                           in1=Vc, op0=ALU.mult, op1=ALU.add)
                    T.w(V.mark())
                    for b in range(4):
                        T.transpose(psBt[0:32, b * 128:(b + 1) * 128],
                                    Rpre[:, b * 32:(b + 1) * 32], identb[:])
                        T.transpose(psBt[0:32, 512 + b * 128:512 + (b + 1) * 128],
                                    Kc[:, b * 32:(b + 1) * 32], identb[:])
                    V.w(T.mark())
                    for b in range(4):
                        V.tensor_scalar(out=rb4[:, b * 128:(b + 1) * 128],
                                        in0=psBt[0:32, b * 128:(b + 1) * 128],
                                        scalar1=bcol(b), scalar2=None, op0=ALU.mult)
                        V.tensor_copy(out=Ktb4[:, b * 128:(b + 1) * 128],
                                      in_=psBt[0:32, 512 + b * 128:512 + (b + 1) * 128])
                    T.w(V.mark())
                    for b in range(4):
                        T.matmul(psQ[0:32, 320 + 0:320 + 128] if False else
                                 psC[0:32, b * 128:(b + 1) * 128],
                                 Kc[:, b * 32:(b + 1) * 32], Qc, start=True, stop=True)
                    V.w(T.mark())
                    for b in range(4):
                        V.tensor_tensor(out=PTb4[:, b * 128:(b + 1) * 128],
                                        in0=psC[0:32, b * 128:(b + 1) * 128],
                                        in1=mask_tu[b * 32:b * 32 + 32, :], op=ALU.mult)
                    if dbg_at == ("PT", si, c_loc, h):
                        V.tensor_copy(out=x_tm[0][0:32, 0:512], in_=PTb4[:, 0:512])
                        SY.dma(dbg_out[0:32, :], x_tm[0][0:32, 0:512], wait=V.mark())
                    for b in range(4):
                        didx = b
                        V.scalar_tensor_tensor(out=W32[0][:], in0=Ablk[:, didx * 32:(didx + 1) * 32],
                                               scalar=bcol(b), in1=mask_ln[0:32, 0:32],
                                               op0=ALU.mult, op1=ALU.mult)
                        T.w(V.mark())
                        T.transpose(psC[0:32, 448:480], W32[0][:], ident_s[0:32, 0:32])
                        V.w(T.mark())
                        V.tensor_copy(out=Z32[0][:], in_=psC[0:32, 448:480])
                        for j in range(1, 5):
                            T.w(V.mark())
                            T.matmul(psC[0:32, 448:480], Z32[j - 1][:], W32[j - 1][:], start=True, stop=True)
                            T.matmul(psC[0:32, 480:512], W32[j - 1][:], Z32[j - 1][:], start=True, stop=True)
                            V.w(T.mark())
                            V.tensor_copy(out=W32[j][:], in_=psC[0:32, 448:480])
                            V.tensor_copy(out=Z32[j][:], in_=psC[0:32, 480:512])
                        V.tensor_tensor(out=Gg[0][:], in0=Z32[0][:], in1=ident_s[0:32, 0:32], op=ALU.add)
                        for j in range(1, 5):
                            T.w(V.mark())
                            T.matmul(psC[0:32, 448:480], W32[j][:], Gg[(j - 1) % 2][:], start=True, stop=True)
                            V.w(T.mark())
                            V.tensor_tensor(out=Gg[j % 2][:], in0=psC[0:32, 448:480],
                                            in1=Gg[(j - 1) % 2][:], op=ALU.add)
                        G4 = Gg[0]
                        T.w(V.mark())
                        T.matmul(psR[0:32, 128:256], G4[:], rb4[:, b * 128:(b + 1) * 128],
                                 start=True, stop=True)
                        V.w(T.mark())
                        V.tensor_copy(out=Ub4[:, b * 128:(b + 1) * 128], in_=psR[0:32, 128:256])
                        for b2 in range(b + 1, 4):
                            pidx = 4 + [(bb, bb2) for bb in range(4) for bb2 in range(bb + 1, 4)].index((b, b2))
                            T.w(V.mark())
                            T.matmul(psR[0:32, 256:384],
                                     Ablk_bf[:, (pidx - 4) * 32:(pidx - 3) * 32],
                                     Ub4[:, b * 128:(b + 1) * 128], start=True, stop=True)
                            V.w(T.mark())
                            V.scalar_tensor_tensor(out=rb4[:, b2 * 128:(b2 + 1) * 128],
                                                   in0=psR[0:32, 256:384], scalar=bcol(b2, neg=True),
                                                   in1=rb4[:, b2 * 128:(b2 + 1) * 128],
                                                   op0=ALU.mult, op1=ALU.add)
                    if dbg_at == ("PT2", si, c_loc, h):
                        V.tensor_copy(out=x_tm[0][0:32, 0:512], in_=PTb4[:, 0:512])
                        SY.dma(dbg_out[0:32, :], x_tm[0][0:32, 0:512], wait=V.mark())
                    T.w(V.mark())
                    T.matmul(psR[:, 384:512], S_b[h][:], Qc, start=True, stop=False)
                    for b in range(4):
                        T.matmul(psR[:, 384:512], Ub4[:, b * 128:(b + 1) * 128],
                                 PTb4[:, b * 128:(b + 1) * 128],
                                 start=False, stop=(b == 3))
                    if dbg_at == (si, c_loc, h):
                        V.w(T.mark())
                        V.tensor_copy(out=x_tm[0][:, 0:128], in_=psR[:, 0:128])
                        V.tensor_copy(out=x_tm[0][:, 384:512], in_=psR[:, 384:512])
                        mdbg = V.mark()
                        SY.dma(dbg_out, x_tm[0][:, 0:512], wait=mdbg)
                    for b in range(4):
                        T.matmul(psQ[:, 384:512], Ktb4[:, b * 128:(b + 1) * 128],
                                 Ub4[:, b * 128:(b + 1) * 128],
                                 start=(b == 0), stop=(b == 3))
                    V.w(T.mark())
                    V.tensor_tensor(out=S_f[h][:], in0=S_f[h][:], in1=psQ[:, 384:512], op=ALU.add)
                    V.tensor_copy(out=S_b[h][:], in_=S_f[h][:])
                    T.matmul(psRow[:, 0:128], wg2T[:, h * 128:(h + 1) * 128],
                             xg1[:, cs], start=True, stop=True)
                    S.w(T.mark())
                    S.activation(out=gate[:], in_=psRow[:, 0:128], func=AF.Sigmoid)
                    V.w(S.mark())
                    V.tensor_tensor(out=ogf[:], in0=psR[:, 384:512], in1=gate[:], op=ALU.mult)
                    V.tensor_tensor(out=ogsq[:], in0=ogf[:], in1=ogf[:], op=ALU.mult)
                    T.w(V.mark())
                    T.matmul(psRow[0:1, 128:256], ones_col[:], ogsq[:], start=True, stop=True)
                    V.w(T.mark())
                    V.tensor_scalar(out=rb_s[:], in0=psRow[0:1, 128:256],
                                    scalar1=_f32(1.0 / 128.0), scalar2=EPS,
                                    op0=ALU.mult, op1=ALU.add)
                    S.w(V.mark())
                    S.activation(out=rb_t[:], in_=rb_s[:], func=AF.Sqrt)
                    V.w(S.mark())
                    V.reciprocal(out=rb_u[:], in_=rb_t[:])
                    V.tensor_tensor(out=rb_v[:], in0=rb_u[:], in1=rb_u[:], op=ALU.mult)
                    V.tensor_tensor(out=rb_v[:], in0=rb_v[:], in1=rb_s[:], op=ALU.mult)
                    V.tensor_scalar(out=rb_v[:], in0=rb_v[:], scalar1=-0.5,
                                    scalar2=1.5, op0=ALU.mult, op1=ALU.add)
                    V.tensor_tensor(out=rb_u[:], in0=rb_u[:], in1=rb_v[:], op=ALU.mult)
                    T.w(V.mark())
                    T.matmul(psRow[:, 256:384], ones_row[:], rb_u[0:1, :], start=True, stop=True)
                    V.w(T.mark())
                    V.tensor_tensor(out=ogf[:], in0=ogf[:], in1=psRow[:, 256:384], op=ALU.mult)
                    if si > 0 and c_loc == 0 and h == 0:
                        V.w(("GP", 16 * 4 * si))
                    V.tensor_scalar(out=og_st[h][:, cs], in0=ogf[:],
                                    scalar1=gnc[:, h:h + 1], scalar2=None, op0=ALU.mult)

            mv = V.mark()
            for h in range(4):
                GP.dma(og_out[h * 128:(h + 1) * 128, si * 512:(si + 1) * 512],
                       og_st[h][:], wait=mv)
        GP.drain()
        SY.drain()

    run_passes(nc, sems, prog)
    return nc



def build_tail(n_tok):
    assert n_tok % 512 == 0
    NH = n_tok // 512
    nc = bass.Bass("TRN2", target_bir_lowering=False)
    dr = lambda n, s, dt=F32, k="ExternalInput": nc.dram_tensor(n, s, dt, kind=k).ap()
    og_d = dr("og_full", [2048, n_tok])
    x_d = dr("x_slice", [n_tok, 2048])
    Wo_d = dr("Wo", [2048, 2048])
    up_d = dr("up_dq", [2048, 8192], BF16)
    dn_d = dr("dn_dq", [8192, 2048], BF16)
    ub_d = dr("ub_cols", [128, 64])
    db_d = dr("db_cols", [128, 16])
    ident_d = dr("ident", [128, 128])
    y_d = dr("y_out", [n_tok, 2048], F32, k="ExternalOutput")

    sb = nc.alloc_sbuf_tensor
    pst = nc.alloc_psum_tensor
    es = ExitStack()

    ident_s = sb("ident_s", [128, 128], F32)
    ub = sb("ub", [128, 64], F32)
    db = sb("db", [128, 16], F32)
    _obase = nc.bump_sbuf(65536)[0]
    h_sb = nc.alloc_sbuf_tensor_at("h_sb", [128, 64 * 512], BF16, offset=_obase)
    x_cm = nc.alloc_sbuf_tensor_at("x_cm", [128, 16 * 512], F32, offset=_obase)
    og_sb = nc.alloc_sbuf_tensor_at("og_sb", [128, 16 * 512], BF16, offset=_obase + 32768)
    x_tm = [nc.alloc_sbuf_tensor_at("x_tm0", [128, 2048], F32, offset=_obase + 49152),
            nc.alloc_sbuf_tensor_at("x_tm1", [128, 2048], F32, offset=_obase + 57344)]
    wld = x_tm
    woT = sb("woT", [128, 16 * 128], BF16)
    y_f = sb("y_f", [128, 16 * 512], F32)
    _xbase = nc.bump_sbuf(16384)[0]
    x1b = nc.alloc_sbuf_tensor_at("x1b", [128, 16 * 512], BF16, offset=_xbase)
    outb = nc.alloc_sbuf_tensor_at("outb", [128, 2048], F32, offset=_xbase + 8192)
    sqb = sb("sqb", [128, 512], F32)
    upw = [sb("upw0", [128, 16 * 256], BF16), sb("upw1", [128, 16 * 256], BF16)]
    dnw = [sb("dnw0", [128, 64 * 128], BF16), sb("dnw1", [128, 64 * 128], BF16)]
    z_f = sb("z_f", [128, 16 * 512], F32)
    rowm = sb("rowm", [1, 512], F32)
    rowq = sb("rowq", [1, 512], F32)
    rowr = sb("rowr", [1, 512], F32)
    rowt = sb("rowt", [1, 512], F32)
    ones_col = sb("ones_col", [128, 1], F32)
    ones_row = sb("ones_row", [1, 128], F32)

    psA = pst("psA", [128, 512], F32)
    psB_ = pst("psB", [128, 512], F32)
    psT = pst("psT", [128, 512], F32)
    psRow = pst("psRow", [128, 512], F32)

    sems = {}
    for nm in ("T", "V", "S", "SY", "GP"):
        sems[nm] = es.enter_context(nc.semaphore("sem" + nm))

    def ln_rows(T, V, S, src_tiles, eps):
        T.w(V.mark())
        for k in range(16):
            T.matmul(psRow[0:1, 0:512], ones_col[:], src_tiles[k],
                     start=(k == 0), stop=(k == 15))
        V.w(T.mark())
        V.tensor_scalar(out=rowm[:], in0=psRow[0:1, 0:512], scalar1=_f32(1.0 / 2048.0),
                        scalar2=None, op0=ALU.mult)
        for k in range(16):
            V.tensor_tensor(out=sqb[:], in0=src_tiles[k], in1=src_tiles[k], op=ALU.mult)
            T.w(V.mark())
            T.matmul(psRow[0:1, 0:512], ones_col[:], sqb[:],
                     start=(k == 0), stop=(k == 15))
            if k < 15:
                V.w(T.mark())
        V.w(T.mark())
        V.tensor_scalar(out=rowq[:], in0=psRow[0:1, 0:512], scalar1=_f32(1.0 / 2048.0),
                        scalar2=None, op0=ALU.mult)
        V.tensor_tensor(out=rowt[:], in0=rowm[:], in1=rowm[:], op=ALU.mult)
        V.tensor_tensor(out=rowq[:], in0=rowq[:], in1=rowt[:], op=ALU.subtract)
        V.tensor_scalar(out=rowq[:], in0=rowq[:], scalar1=_f32(eps), scalar2=None, op0=ALU.add)
        S.w(V.mark())
        S.activation(out=rowt[:], in_=rowq[:], func=AF.Sqrt)
        V.w(S.mark())
        V.reciprocal(out=rowr[:], in_=rowt[:])
        V.tensor_tensor(out=rowt[:], in0=rowr[:], in1=rowr[:], op=ALU.mult)
        V.tensor_tensor(out=rowt[:], in0=rowt[:], in1=rowq[:], op=ALU.mult)
        V.tensor_scalar(out=rowt[:], in0=rowt[:], scalar1=-0.5, scalar2=1.5,
                        op0=ALU.mult, op1=ALU.add)
        V.tensor_tensor(out=rowr[:], in0=rowr[:], in1=rowt[:], op=ALU.mult)
        T.w(V.mark())
        T.matmul(psRow[:, 0:512], ones_row[:], rowm[0:1, 0:128], start=True, stop=False)
        T.matmul(psRow[:, 0:512], ones_row[:], rowm[0:1, 0:128], start=False, stop=True)

    EPS_LN = 1e-5

    def prog(T, V, S, SY, GP):
        m_c = SY.dma(ident_s[:], ident_d)
        SY.dma(ub[:], ub_d)
        m_c2 = SY.dma(db[:], db_d)
        V.w(m_c2)
        V.memset(ones_col[:], 1.0)
        V.memset(ones_row[:], 1.0)

        for half in range(NH):
            t0 = half * 512
            for tt in range(4):
                mld = SY.dma(x_tm[tt % 2][:], x_d[t0 + tt * 128: t0 + (tt + 1) * 128, :],
                             wait=T.mark())
                T.w(mld)
                T.w(V.mark())
                for g in range(4):
                    if g > 0:
                        T.w(V.mark())
                    for kk4 in range(4):
                        kk = g * 4 + kk4
                        T.transpose(psT[:, kk4 * 128:(kk4 + 1) * 128],
                                    x_tm[tt % 2][:, kk * 128:(kk + 1) * 128], ident_s[:])
                    V.w(T.mark())
                    for kk4 in range(4):
                        kk = g * 4 + kk4
                        V.tensor_copy(out=x_cm[:, kk * 512 + tt * 128: kk * 512 + tt * 128 + 128],
                                      in_=psT[:, kk4 * 128:(kk4 + 1) * 128])
            m_og = GP.dma(og_sb[:].rearrange("p (a c) -> p a c", a=16),
                          og_d[:, t0:t0 + 512].rearrange("(a p) c -> p a c", p=128),
                          wait=T.mark())
            for m in range(16):
                mld = SY.dma(wld[m % 2][:], Wo_d[m * 128:(m + 1) * 128, :],
                             wait=T.mark())
                T.w(mld)
                T.w(V.mark())
                for g in range(4):
                    if g > 0:
                        T.w(V.mark())
                    for kk4 in range(4):
                        kk = g * 4 + kk4
                        T.transpose(psT[:, kk4 * 128:(kk4 + 1) * 128],
                                    wld[m % 2][:, kk * 128:(kk + 1) * 128], ident_s[:])
                    V.w(T.mark())
                    for kk4 in range(4):
                        kk = g * 4 + kk4
                        V.tensor_copy(out=woT[:, kk * 128:(kk + 1) * 128],
                                      in_=psT[:, kk4 * 128:(kk4 + 1) * 128])
                T.w(V.mark())
                T.w(m_og)
                for k in range(16):
                    T.matmul(psA[:, 0:512], woT[:, k * 128:(k + 1) * 128],
                             og_sb[:, k * 512:(k + 1) * 512],
                             start=(k == 0), stop=(k == 15))
                V.w(T.mark())
                V.tensor_tensor(out=y_f[:, m * 512:(m + 1) * 512], in0=psA[:, 0:512],
                                in1=x_cm[:, m * 512:(m + 1) * 512], op=ALU.add)
            ytiles = [y_f[:, k * 512:(k + 1) * 512] for k in range(16)]
            T.w(V.mark())
            for k in range(16):
                T.matmul(psRow[0:1, 0:512], ones_col[:], ytiles[k],
                         start=(k == 0), stop=(k == 15))
            V.w(T.mark())
            V.tensor_scalar(out=rowm[:], in0=psRow[0:1, 0:512],
                            scalar1=_f32(1.0 / 2048.0), scalar2=None, op0=ALU.mult)
            for k in range(16):
                V.tensor_tensor(out=sqb[:], in0=ytiles[k], in1=ytiles[k], op=ALU.mult)
                T.w(V.mark())
                T.matmul(psRow[0:1, 0:512], ones_col[:], sqb[:],
                         start=(k == 0), stop=(k == 15))
                V.w(T.mark())
            V.tensor_scalar(out=rowq[:], in0=psRow[0:1, 0:512],
                            scalar1=_f32(1.0 / 2048.0), scalar2=None, op0=ALU.mult)
            V.tensor_tensor(out=rowt[:], in0=rowm[:], in1=rowm[:], op=ALU.mult)
            V.tensor_tensor(out=rowq[:], in0=rowq[:], in1=rowt[:], op=ALU.subtract)
            V.tensor_scalar(out=rowq[:], in0=rowq[:], scalar1=_f32(EPS_LN),
                            scalar2=None, op0=ALU.add)
            S.w(V.mark())
            S.activation(out=rowt[:], in_=rowq[:], func=AF.Sqrt)
            V.w(S.mark())
            V.reciprocal(out=rowr[:], in_=rowt[:])
            V.tensor_tensor(out=rowt[:], in0=rowr[:], in1=rowr[:], op=ALU.mult)
            V.tensor_tensor(out=rowt[:], in0=rowt[:], in1=rowq[:], op=ALU.mult)
            V.tensor_scalar(out=rowt[:], in0=rowt[:], scalar1=-0.5, scalar2=1.5,
                            op0=ALU.mult, op1=ALU.add)
            V.tensor_tensor(out=rowr[:], in0=rowr[:], in1=rowt[:], op=ALU.mult)
            T.w(V.mark())
            T.matmul(psA[:, 0:512], ones_row[:], rowm[0:1, 0:512], start=True, stop=True)
            T.matmul(psB_[:, 0:512], ones_row[:], rowr[0:1, 0:512], start=True, stop=True)
            V.w(T.mark())
            for k in range(16):
                V.tensor_tensor(out=y_f[:, k * 512:(k + 1) * 512], in0=ytiles[k],
                                in1=psA[:, 0:512], op=ALU.subtract)
                V.tensor_tensor(out=y_f[:, k * 512:(k + 1) * 512], in0=ytiles[k],
                                in1=psB_[:, 0:512], op=ALU.mult)
                V.tensor_copy(out=x1b[:, k * 512:(k + 1) * 512], in_=ytiles[k])
            for mg in range(32):
                src = up_d[:, mg * 256:(mg + 1) * 256].rearrange("(a p) c -> p a c", p=128)
                mld = SY.dma(upw[mg % 2][:].rearrange("p (a c) -> p a c", a=16), src,
                             wait=T.mark())
                T.w(mld)
                T.w(S.mark())
                if mg == 0:
                    T.w(V.mark())
                for m2 in range(2):
                    m = mg * 2 + m2
                    for k in range(16):
                        T.matmul(psA[:, 0:512] if m2 == 0 else psB_[:, 0:512],
                                 upw[mg % 2][:, k * 256 + m2 * 128: k * 256 + m2 * 128 + 128],
                                 x1b[:, k * 512:(k + 1) * 512],
                                 start=(k == 0), stop=(k == 15))
                S.w(T.mark())
                for m2 in range(2):
                    m = mg * 2 + m2
                    S.activation(out=h_sb[:, m * 512:(m + 1) * 512],
                                 in_=psA[:, 0:512] if m2 == 0 else psB_[:, 0:512],
                                 func=AF.Gelu_apprx_tanh, bias=ub[:, m:m + 1])
            for m in range(16):
                src = dn_d[:, m * 128:(m + 1) * 128].rearrange("(a p) c -> p a c", p=128)
                mld = SY.dma(dnw[m % 2][:].rearrange("p (a c) -> p a c", a=64), src,
                             wait=T.mark())
                T.w(mld)
                T.w(S.mark())
                T.w(V.mark())
                for kf in range(64):
                    T.matmul(psA[:, 0:512], dnw[m % 2][:, kf * 128:(kf + 1) * 128],
                             h_sb[:, kf * 512:(kf + 1) * 512],
                             start=(kf == 0), stop=(kf == 63))
                V.w(T.mark())
                V.tensor_scalar(out=z_f[:, m * 512:(m + 1) * 512], in0=psA[:, 0:512],
                                scalar1=db[:, m:m + 1], scalar2=None, op0=ALU.add)
                V.tensor_tensor(out=z_f[:, m * 512:(m + 1) * 512],
                                in0=z_f[:, m * 512:(m + 1) * 512],
                                in1=y_f[:, m * 512:(m + 1) * 512], op=ALU.add)
            ztiles = [z_f[:, k * 512:(k + 1) * 512] for k in range(16)]
            T.w(V.mark())
            for k in range(16):
                T.matmul(psRow[0:1, 0:512], ones_col[:], ztiles[k],
                         start=(k == 0), stop=(k == 15))
            V.w(T.mark())
            V.tensor_scalar(out=rowm[:], in0=psRow[0:1, 0:512],
                            scalar1=_f32(1.0 / 2048.0), scalar2=None, op0=ALU.mult)
            for k in range(16):
                V.tensor_tensor(out=sqb[:], in0=ztiles[k], in1=ztiles[k], op=ALU.mult)
                T.w(V.mark())
                T.matmul(psRow[0:1, 0:512], ones_col[:], sqb[:],
                         start=(k == 0), stop=(k == 15))
                V.w(T.mark())
            V.tensor_scalar(out=rowq[:], in0=psRow[0:1, 0:512],
                            scalar1=_f32(1.0 / 2048.0), scalar2=None, op0=ALU.mult)
            V.tensor_tensor(out=rowt[:], in0=rowm[:], in1=rowm[:], op=ALU.mult)
            V.tensor_tensor(out=rowq[:], in0=rowq[:], in1=rowt[:], op=ALU.subtract)
            V.tensor_scalar(out=rowq[:], in0=rowq[:], scalar1=_f32(EPS_LN),
                            scalar2=None, op0=ALU.add)
            S.w(V.mark())
            S.activation(out=rowt[:], in_=rowq[:], func=AF.Sqrt)
            V.w(S.mark())
            V.reciprocal(out=rowr[:], in_=rowt[:])
            V.tensor_tensor(out=rowt[:], in0=rowr[:], in1=rowr[:], op=ALU.mult)
            V.tensor_tensor(out=rowt[:], in0=rowt[:], in1=rowq[:], op=ALU.mult)
            V.tensor_scalar(out=rowt[:], in0=rowt[:], scalar1=-0.5, scalar2=1.5,
                            op0=ALU.mult, op1=ALU.add)
            V.tensor_tensor(out=rowr[:], in0=rowr[:], in1=rowt[:], op=ALU.mult)
            T.w(V.mark())
            T.matmul(psA[:, 0:512], ones_row[:], rowm[0:1, 0:512], start=True, stop=True)
            T.matmul(psB_[:, 0:512], ones_row[:], rowr[0:1, 0:512], start=True, stop=True)
            V.w(T.mark())
            for k in range(16):
                V.tensor_tensor(out=z_f[:, k * 512:(k + 1) * 512], in0=ztiles[k],
                                in1=psA[:, 0:512], op=ALU.subtract)
                V.tensor_tensor(out=z_f[:, k * 512:(k + 1) * 512], in0=ztiles[k],
                                in1=psB_[:, 0:512], op=ALU.mult)
            for tt in range(4):
                T.w(V.mark())
                for k in range(16):
                    if k > 0 and k % 4 == 0:
                        T.w(V.mark())
                    T.transpose(psT[:, (k % 4) * 128:(k % 4 + 1) * 128],
                                z_f[:, k * 512 + tt * 128: k * 512 + tt * 128 + 128],
                                ident_s[:])
                    if k % 4 == 3:
                        V.w(T.mark())
                        for k4 in range(k - 3, k + 1):
                            V.tensor_copy(out=outb[:, k4 * 128:(k4 + 1) * 128],
                                          in_=psT[:, (k4 % 4) * 128:(k4 % 4 + 1) * 128])
                mo = V.mark()
                GP.dma(y_d[t0 + tt * 128: t0 + (tt + 1) * 128, :], outb[:], wait=mo)
                V.w(GP.mark())
        GP.drain()
        SY.drain()

    run_passes(nc, sems, prog)
    return nc




def build_fused(n_own=1024, n_pre=3072, FF=8192):
    assert n_own % 512 == 0 and n_pre % 512 == 0 and FF % 512 == 0
    NS_OWN = n_own // 512
    NS_PRE = n_pre // 512
    NF = FF // 128
    nc = bass.Bass("TRN2", target_bir_lowering=False)
    dr = lambda n, s, dt=F32, k="ExternalInput": nc.dram_tensor(n, s, dt, kind=k).ap()
    x_pre_d = dr("x_pre", [max(n_pre, 512), 2048])
    x_own_d = dr("x_own", [n_own, 2048])
    Wq_d = dr("Wq", [2048, 2048])
    Wk_d = dr("Wk", [2048, 2048])
    Wv_d = dr("Wv", [2048, 2048])
    Wb_d = dr("Wbeta", [16, 2048])
    Wg1_d = dr("Wg1", [128, 2048])
    Wg2_d = dr("Wg2", [2048, 128])
    gnc_d = dr("gnorm_cols", [128, 16])
    Wo_d = dr("Wo", [2048, 2048])
    up_d = dr("up_w", [2048, FF])
    dn_d = dr("dn_w", [FF, 2048])
    ub_d = dr("ub_cols", [128, FF // 128])
    db_d = dr("db_row", [1, 2048])
    ident_d = dr("ident", [128, 128])
    mask_ln_d = dr("mask_lowneg", [128, 128])
    mask_tu_d = dr("mask_triu", [128, 128], BF16)
    y_d = dr("y_out", [n_own, 2048], F32, k="ExternalOutput")
    di = lambda n, s, dt: nc.dram_tensor(n, s, dt, kind="Internal").ap()
    wqT_d = di("s_wqT", [2048, 2048], BF16)
    wkT_d = di("s_wkT", [2048, 2048], BF16)
    wvT_d = di("s_wvT", [2048, 2048], BF16)
    woT_d = di("s_woT", [2048, 2048], BF16)
    upimg_d = di("s_upimg", [(FF // 256) * 128, 4096], BF16)
    dndq_d = di("s_dndq", [FF, 2048], BF16)

    sb = nc.alloc_sbuf_tensor
    sba = nc.alloc_sbuf_tensor_at
    pst = nc.alloc_psum_tensor
    es = ExitStack()

    ident_s = sb("ident_s", [128, 128], F32)
    identb = sb("identb", [128, 128], BF16)
    mask_ln = sb("mask_ln", [128, 128], F32)
    mask_tu = sb("mask_tu", [128, 128], BF16)
    gnc = sb("gnc", [128, 16], F32)
    ub = sb("ub", [128, FF // 128], F32)
    ones_col = sb("ones_col", [128, 1], F32)
    ones_row = sb("ones_row", [1, 128], F32)
    wbT = sb("wbT", [128, 16 * 16], BF16)
    wg1T = sb("wg1T", [128, 16 * 128], BF16)
    wg2T = sb("wg2T", [128, 16 * 128], BF16)
    S_f = [sb(f"Sf_{h}", [128, 128], F32) for h in range(16)]
    S_b = [sb(f"Sb_{h}", [128, 128], BF16) for h in range(16)]
    bstk = [sb(f"bstk{b}", [128, 1], F32) for b in range(4)]
    nbstk = [sb(f"nbstk{b}", [128, 1], F32) for b in range(4)]
    colm = sb("colm", [128, 1], F32)
    colq = sb("colq", [128, 1], F32)
    colr = sb("colr", [128, 1], F32)
    colt = sb("colt", [128, 1], F32)
    og_all = sb("og_all", [128, 16 * 512], BF16)

    baseA = nc.bump_sbuf(65536)[0]
    h_sb = sba("h_sb", [128, (NF // 2) * 512], BF16, offset=baseA)
    y1 = [sba(f"y1_{t}", [128, 2048], F32, offset=baseA + 32768 + t * 8192)
          for t in range(4)]
    q_h = sba("q_h", [128, 16 * 512], BF16, offset=baseA)
    kr_h = sba("kr_h", [128, 16 * 512], BF16, offset=baseA + 16384)
    k_hh = sba("k_hh", [128, 16 * 512], BF16, offset=baseA + 32768)
    v_h = sba("v_h", [128, 16 * 512], BF16, offset=baseA + 49152)
    _dw = baseA + 16384
    Nbd = [sba(f"Nbd{b}", [128, 128], F32, offset=_dw + b * 512) for b in range(4)]
    AoffBD = sba("AoffBD", [128, 6 * 128], BF16, offset=_dw + 2048)
    cW = [[sba(f"cW{b}_{j}", [128, 128], BF16, offset=_dw + 3584 + (b * 2 + j) * 256)
           for j in range(2)] for b in range(4)]
    cZ = [[sba(f"cZ{b}_{j}", [128, 128], BF16, offset=_dw + 5632 + (b * 2 + j) * 256)
           for j in range(2)] for b in range(4)]
    cG = [[sba(f"cG{b}_{j}", [128, 128], BF16, offset=_dw + 7680 + (b * 2 + j) * 256)
           for j in range(2)] for b in range(4)]
    rbS = [sba(f"rbS{b}", [128, 128], BF16, offset=_dw + 9728 + b * 256) for b in range(4)]
    ubS = [sba(f"ubS{b}", [128, 128], BF16, offset=_dw + 10752 + b * 256) for b in range(4)]
    RpreG = sba("RpreG", [128, 4 * 128], BF16, offset=_dw + 11776)
    UtmG = sba("UtmG", [128, 4 * 128], BF16, offset=_dw + 12800)
    KtmG = sba("KtmG", [128, 4 * 128], BF16, offset=_dw + 13824)
    PTh4 = sba("PTh4", [128, 4 * 128], BF16, offset=_dw + 3584)
    gate4 = sba("gate4", [128, 4 * 128], BF16, offset=_dw + 4608)
    ogf4 = sba("ogf4", [128, 4 * 128], F32, offset=_dw + 5632)

    baseB = nc.bump_sbuf(65536)[0]
    x_tm = [sba("x_tm0", [128, 2048], F32, offset=baseB),
            sba("x_tm1", [128, 2048], F32, offset=baseB + 8192)]
    _b2 = baseB + 16384
    qw_aa = sba("qw_aa", [128, 2048], F32, offset=_b2)
    qw_sg = sba("qw_sg", [128, 2048], BF16, offset=_b2 + 8192)
    qw_ac = sba("qw_ac", [128, 2048], BF16, offset=_b2 + 12288)
    qw_bb = sba("qw_bb", [128, 2048], BF16, offset=_b2 + 16384)
    qw_cc = sba("qw_cc", [128, 2048], BF16, offset=_b2 + 20480)
    qw_st = [sba("qw_st0", [128, 2048], BF16, offset=_b2 + 24576),
             sba("qw_st1", [128, 2048], BF16, offset=_b2 + 28672)]
    qc_am = sba("qc_am", [128, 64], F32, offset=_b2 + 32768)
    qc_rr = sba("qc_rr", [128, 64], F32, offset=_b2 + 33024)
    qc_ss = sba("qc_ss", [128, 64], F32, offset=_b2 + 33280)
    qc_mm = sba("qc_mm", [128, 64], F32, offset=_b2 + 33536)
    qc_tt = sba("qc_tt", [128, 64], F32, offset=_b2 + 33792)
    wstr = [sba("wstr0", [128, 16 * 128], BF16, offset=_b2),
            sba("wstr1", [128, 16 * 128], BF16, offset=_b2 + 4096)]
    rowm = sba("rowm", [1, 512], F32, offset=_b2 + 8192)
    rowq = sba("rowq", [1, 512], F32, offset=_b2 + 10240)
    rowr = sba("rowr", [1, 512], F32, offset=_b2 + 12288)
    ksq = sba("ksq", [128, 512], F32, offset=_b2 + 14336)
    worhs = [sba("worhs0", [128, 512], BF16, offset=_b2),
             sba("worhs1", [128, 512], BF16, offset=_b2 + 1024)]
    dnw = [sba("dnw0", [128, 512], BF16, offset=_b2 + 2048),
           sba("dnw1", [128, 512], BF16, offset=_b2 + 3072)]
    upw = [sba("upw0", [128, 16 * 256], BF16, offset=_b2 + 4096),
           sba("upw1", [128, 16 * 256], BF16, offset=_b2 + 12288)]
    sq_t = sba("sq_t", [128, 2048], F32, offset=_b2 + 20480)
    dbrow = sba("dbrow", [1, 2048], F32, offset=_b2 + 34304)
    beta_sb = sba("beta_sb", [16, 512], F32, offset=_b2 + 42496)
    betaT4 = sba("betaT4", [32, 16 * 16], F32, offset=_b2 + 44544)
    rb_s = sba("rb_s", [1, 128], F32, offset=_b2 + 45568)
    rb_t = sba("rb_t", [1, 128], F32, offset=_b2 + 46080)
    rb_u = sba("rb_u", [1, 128], F32, offset=_b2 + 46592)
    rb_v = sba("rb_v", [1, 128], F32, offset=_b2 + 47104)
    xg1_buf = sba("xg1_buf", [128, 512], BF16, offset=_b2 + 47616)

    baseC = nc.bump_sbuf(16384)[0]
    x_cm = sba("x_cm", [128, 16 * 512], BF16, offset=baseC)
    x1b = sba("x1b", [128, 16 * 512], BF16, offset=baseC)

    psX = pst("psX", [128, 512], F32)
    psQ = pst("psQ", [128, 512], F32)
    psC = pst("psC", [128, 512], F32)
    psR = pst("psR", [128, 512], F32)
    psBt = pst("psBt", [128, 1024], BF16)
    psRow = pst("psRow", [128, 512], F32)

    sems = {}
    for nm in ("T", "V", "S", "SY", "GP"):
        sems[nm] = es.enter_context(nc.semaphore("sem" + nm))

    EPS = _f32(1e-6)
    EPS_LN = _f32(1e-5)
    pairs_off = [(b, b2) for b in range(4) for b2 in range(b + 1, 4)]

    def prog(T, V, S, SY, GP):
        SY.dma(ident_s[:], ident_d)
        SY.dma(mask_ln[:], mask_ln_d)
        SY.dma(mask_tu[:], mask_tu_d)
        SY.dma(gnc[:], gnc_d)
        SY.dma(ub[:], ub_d)
        m_c = SY.dma(dbrow[:], db_d)
        V.w(m_c)
        V.tensor_copy(out=identb[:], in_=ident_s[:])
        V.memset(ones_col[:], 1.0)
        V.memset(ones_row[:], 1.0)
        for h in range(16):
            V.memset(S_f[h][:], 0.0)
            V.memset(S_b[h][:], 0.0)

        mld = SY.dma(x_tm[0][0:16, :], Wb_d, wait=T.mark())
        T.w(mld)
        for kk in range(16):
            T.transpose(psX[:, (kk % 4) * 128:(kk % 4) * 128 + 16],
                        x_tm[0][0:16, kk * 128:(kk + 1) * 128],
                        ident_s[0:16, 0:16])
            if kk % 4 == 3:
                V.w(T.mark())
                for k2 in range(kk - 3, kk + 1):
                    V.tensor_copy(out=wbT[:, k2 * 16:(k2 + 1) * 16],
                                  in_=psX[:, (k2 % 4) * 128:(k2 % 4) * 128 + 16])
                if kk < 15:
                    T.w(V.mark())
        mld = SY.dma(x_tm[1][:], Wg1_d, wait=T.mark())
        T.w(mld)
        T.w(V.mark())
        for g in range(4):
            if g > 0:
                T.w(V.mark())
            for k4 in range(4):
                kk = g * 4 + k4
                T.transpose(psX[:, k4 * 128:(k4 + 1) * 128],
                            x_tm[1][:, kk * 128:(kk + 1) * 128], ident_s[:])
            V.w(T.mark())
            for k4 in range(4):
                kk = g * 4 + k4
                V.tensor_copy(out=wg1T[:, kk * 128:(kk + 1) * 128],
                              in_=psX[:, k4 * 128:(k4 + 1) * 128])
        for hh in range(16):
            mld = SY.dma(x_tm[hh % 2][:, 0:128], Wg2_d[hh * 128:(hh + 1) * 128, :],
                         wait=T.mark())
            T.w(mld)
            T.w(V.mark())
            T.transpose(psX[:, 0:128], x_tm[hh % 2][:, 0:128], ident_s[:])
            V.w(T.mark())
            V.tensor_copy(out=wg2T[:, hh * 128:(hh + 1) * 128], in_=psX[:, 0:128])

        for wi, (src_d, dst_d) in enumerate(((Wq_d, wqT_d), (Wk_d, wkT_d),
                                             (Wv_d, wvT_d), (Wo_d, woT_d))):
            for m in range(16):
                i = wi * 16 + m
                mld = SY.dma(x_tm[i % 2][:], src_d[m * 128:(m + 1) * 128, :],
                             wait=T.mark())
                T.w(mld)
                T.w(V.mark())
                T.w(GP.mark())
                for g in range(4):
                    if g > 0:
                        T.w(V.mark())
                    for k4 in range(4):
                        kk = g * 4 + k4
                        T.transpose(psX[:, k4 * 128:(k4 + 1) * 128],
                                    x_tm[i % 2][:, kk * 128:(kk + 1) * 128],
                                    ident_s[:])
                    V.w(T.mark())
                    for k4 in range(4):
                        kk = g * 4 + k4
                        V.tensor_copy(out=wstr[i % 2][:, kk * 128:(kk + 1) * 128],
                                      in_=psX[:, k4 * 128:(k4 + 1) * 128])
                if src_d is Wo_d:
                    GP.dma(dst_d[:, m * 128:(m + 1) * 128].rearrange("(a p) c -> p a c", p=128),
                           wstr[i % 2][:].rearrange("p (a c) -> p a c", a=16),
                           wait=V.mark())
                else:
                    GP.dma(dst_d[m * 128:(m + 1) * 128, :], wstr[i % 2][:],
                           wait=V.mark())

        NBQ = 64
        for src_d, dst_d, RT, CT in ((up_d, None, 2048, FF),
                                     (dn_d, dndq_d, FF, 2048)):
            idx = 0
            fwd_marks = {}
            for rt in range(RT // 128):
                for cs in range(CT // 2048):
                    par = idx % 2
                    mld = SY.dma(x_tm[par][:],
                                 src_d[rt * 128:(rt + 1) * 128, cs * 2048:(cs + 1) * 2048],
                                 wait=fwd_marks.get(idx - 2, T.mark() if idx < 2 else None))
                    T.w(mld)
                    T.w(S.mark())
                    if idx == 0:
                        T.w(V.mark())
                    for g in range(4):
                        if g > 0:
                            T.w(S.mark())
                        for k4 in range(4):
                            kk = g * 4 + k4
                            T.transpose(psX[:, k4 * 128:(k4 + 1) * 128],
                                        x_tm[par][:, kk * 128:(kk + 1) * 128],
                                        ident_s[:])
                        S.w(T.mark())
                        if g == 0:
                            S.w(V.mark())
                            if idx == 0:
                                S.w(GP.mark())
                        S.activation(out=qw_aa[:, g * 512:(g + 1) * 512],
                                     in_=psX[:, 0:512], func=AF.Abs)
                        S.activation(out=qw_sg[:, g * 512:(g + 1) * 512],
                                     in_=psX[:, 0:512], func=AF.Sign)
                    fwd_marks[idx] = T.mark()
                    V.w(S.mark())
                    V.tensor_reduce(out=qc_am[:],
                                    in_=qw_aa[:].rearrange("p (a b) -> p a b", b=32),
                                    axis=AX.X, op=ALU.max)
                    V.tensor_scalar(out=qc_mm[:], in0=qc_am[:], scalar1=_f32(1e-30),
                                    scalar2=_f32(2.0 ** 100), op0=ALU.max, op1=ALU.mult)
                    V.memset(qc_rr[:], _f32(2.0 ** 102))
                    for E in _STAGES:
                        V.tensor_scalar(out=qc_tt[:], in0=qc_mm[:], scalar1=_f32(2.0 ** E),
                                        scalar2=None, op0=ALU.is_lt)
                        V.tensor_scalar(out=qc_tt[:], in0=qc_tt[:],
                                        scalar1=_f32(1.0 - 2.0 ** (-E)),
                                        scalar2=_f32(2.0 ** (-E)), op0=ALU.mult, op1=ALU.add)
                        V.tensor_tensor(out=qc_mm[:], in0=qc_mm[:], in1=qc_tt[:], op=ALU.mult)
                        V.tensor_tensor(out=qc_rr[:], in0=qc_rr[:], in1=qc_tt[:], op=ALU.mult)
                    V.reciprocal(out=qc_ss[:], in_=qc_rr[:])
                    rbc = qc_rr[:].unsqueeze(2).broadcast_to([128, NBQ, 32])
                    V.tensor_tensor(out=qw_aa[:].rearrange("p (a b) -> p a b", b=32),
                                    in0=qw_aa[:].rearrange("p (a b) -> p a b", b=32),
                                    in1=rbc, op=ALU.mult)
                    xh = qw_aa
                    V.tensor_scalar(out=qw_ac[:], in0=xh[:], scalar1=0.25,
                                    scalar2=None, op0=ALU.is_ge)
                    for mmid in (0.75, 1.25, 1.75):
                        V.scalar_tensor_tensor(out=qw_ac[:], in0=xh[:], scalar=mmid,
                                               in1=qw_ac[:], op0=ALU.is_ge, op1=ALU.add)
                    V.tensor_scalar(out=qw_bb[:], in0=xh[:], scalar1=2.5,
                                    scalar2=None, op0=ALU.is_ge)
                    V.scalar_tensor_tensor(out=qw_bb[:], in0=xh[:], scalar=3.5,
                                           in1=qw_bb[:], op0=ALU.is_ge, op1=ALU.add)
                    V.tensor_scalar(out=qw_cc[:], in0=xh[:], scalar1=5.0,
                                    scalar2=None, op0=ALU.is_ge)
                    V.scalar_tensor_tensor(out=qw_ac[:], in0=qw_ac[:], scalar=0.5,
                                           in1=qw_bb[:], op0=ALU.mult, op1=ALU.add)
                    V.scalar_tensor_tensor(out=qw_ac[:], in0=qw_cc[:], scalar=2.0,
                                           in1=qw_ac[:], op0=ALU.mult, op1=ALU.add)
                    sbc = qc_ss[:].unsqueeze(2).broadcast_to([128, NBQ, 32])
                    V.tensor_tensor(out=qw_bb[:].rearrange("p (a b) -> p a b", b=32),
                                    in0=qw_ac[:].rearrange("p (a b) -> p a b", b=32),
                                    in1=sbc, op=ALU.mult)
                    V.tensor_tensor(out=qw_cc[:], in0=qw_bb[:], in1=qw_sg[:], op=ALU.mult)
                    T.w(V.mark())
                    for g in range(4):
                        if g > 0:
                            T.w(V.mark())
                        for k4 in range(4):
                            kk = g * 4 + k4
                            T.transpose(psBt[:, k4 * 128:(k4 + 1) * 128],
                                        qw_cc[:, kk * 128:(kk + 1) * 128], identb[:])
                        V.w(T.mark())
                        if g == 0:
                            V.w(GP.mark())
                        for k4 in range(4):
                            kk = g * 4 + k4
                            V.tensor_copy(out=qw_st[par][:, kk * 128:(kk + 1) * 128],
                                          in_=psBt[:, k4 * 128:(k4 + 1) * 128])
                    if dst_d is None:
                        for mg8 in range(8):
                            mgl = cs * 8 + mg8
                            GP.dma(upimg_d[mgl * 128:(mgl + 1) * 128,
                                           rt * 256:(rt + 1) * 256],
                                   qw_st[par][:, mg8 * 256:(mg8 + 1) * 256],
                                   wait=V.mark())
                    else:
                        GP.dma(dst_d[rt * 128:(rt + 1) * 128, cs * 2048:(cs + 1) * 2048],
                               qw_st[par][:], wait=V.mark())
                    idx += 1

        def proj_strip(si, xsrc_d, own):
            S.w(GP.mark())
            V.w(GP.mark())
            for tt in range(4):
                mld = SY.dma(x_tm[tt % 2][:],
                             xsrc_d[si * 512 + tt * 128: si * 512 + (tt + 1) * 128, :],
                             wait=T.mark())
                T.w(mld)
                T.w(V.mark())
                for g in range(4):
                    if g > 0:
                        T.w(V.mark())
                    for k4 in range(4):
                        kk = g * 4 + k4
                        T.transpose(psX[:, k4 * 128:(k4 + 1) * 128],
                                    x_tm[tt % 2][:, kk * 128:(kk + 1) * 128],
                                    ident_s[:])
                    V.w(T.mark())
                    for k4 in range(4):
                        kk = g * 4 + k4
                        V.tensor_copy(out=x_cm[:, kk * 512 + tt * 128: kk * 512 + tt * 128 + 128],
                                      in_=psX[:, k4 * 128:(k4 + 1) * 128])
            T.w(V.mark())
            T.w(S.mark())
            for kk in range(16):
                T.matmul(psRow[0:16, :], wbT[:, kk * 16:(kk + 1) * 16],
                         x_cm[:, kk * 512:(kk + 1) * 512],
                         start=(kk == 0), stop=(kk == 15))
            S.w(T.mark())
            S.activation(out=beta_sb[:], in_=psRow[0:16, :], func=AF.Sigmoid)
            V.w(S.mark())
            V.tensor_scalar(out=beta_sb[:], in0=beta_sb[:], scalar1=2.0,
                            scalar2=None, op0=ALU.mult)
            T.w(V.mark())
            for j16 in range(16):
                T.transpose(psX[0:32, (j16 % 8) * 16:(j16 % 8) * 16 + 16],
                            beta_sb[0:16, j16 * 32:(j16 + 1) * 32],
                            ident_s[0:16, 0:16])
                if j16 % 8 == 7:
                    V.w(T.mark())
                    for j2 in range(j16 - 7, j16 + 1):
                        V.tensor_copy(out=betaT4[:, j2 * 16:(j2 + 1) * 16],
                                      in_=psX[0:32, (j2 % 8) * 16:(j2 % 8) * 16 + 16])
                    if j16 < 15:
                        T.w(V.mark())
            mats = [(wkT_d, kr_h, "silu"), (wvT_d, v_h, "copy")]
            if own:
                mats.append((wqT_d, q_h, "silu"))
            evac_marks = {}
            use_marks = {}
            gi2 = 0
            for wT_d, dst, act in mats:
                for m in range(16):
                    mld = SY.dma(wstr[gi2 % 2][:],
                                 wT_d[m * 128:(m + 1) * 128, :],
                                 wait=use_marks.get(gi2 - 2, T.mark() if gi2 < 2 else None))
                    T.w(mld)
                    if gi2 >= 2:
                        T.w(evac_marks[gi2 - 2])
                    pq = psQ if gi2 % 2 == 0 else psC
                    for kk in range(16):
                        T.matmul(pq[:, :], wstr[gi2 % 2][:, kk * 128:(kk + 1) * 128],
                                 x_cm[:, kk * 512:(kk + 1) * 512],
                                 start=(kk == 0), stop=(kk == 15))
                    use_marks[gi2] = T.mark()
                    S.w(T.mark())
                    if act == "copy":
                        S.copy(out=dst[:, m * 512:(m + 1) * 512], in_=pq[:, :])
                    else:
                        S.activation(out=dst[:, m * 512:(m + 1) * 512], in_=pq[:, :],
                                     func=AF.Silu)
                    evac_marks[gi2] = S.mark()
                    gi2 += 1
            if own:
                T.w(V.mark())
                T.w(S.mark())
                for kk in range(16):
                    T.matmul(psQ[:, :], wg1T[:, kk * 128:(kk + 1) * 128],
                             x_cm[:, kk * 512:(kk + 1) * 512],
                             start=(kk == 0), stop=(kk == 15))
                V.w(T.mark())
                V.tensor_copy(out=xg1_buf[:], in_=psQ[:, :])
            s0set = sq_t[0:1, 0:2048]
            s1set = x_tm[0][0:1, 0:2048]
            s2set = x_tm[1][0:1, 0:2048]
            rbanks = [psRow, psQ, psC, psR]
            for hg in range(4):
                hs4 = [hg * 4 + i for i in range(4)]
                V.w(S.mark())
                if hg == 0:
                    V.w(T.mark())
                for i, h in enumerate(hs4):
                    V.tensor_tensor(out=sq_t[:, i * 512:(i + 1) * 512],
                                    in0=kr_h[:, h * 512:(h + 1) * 512],
                                    in1=kr_h[:, h * 512:(h + 1) * 512], op=ALU.mult)
                T.w(V.mark())
                for i in range(4):
                    T.matmul(rbanks[i][0:1, 0:512], ones_col[:],
                             sq_t[:, i * 512:(i + 1) * 512], start=True, stop=True)
                V.w(T.mark())
                for i in range(4):
                    V.tensor_scalar(out=s0set[0:1, i * 512:(i + 1) * 512],
                                    in0=rbanks[i][0:1, 0:512], scalar1=EPS,
                                    scalar2=None, op0=ALU.add)
                S.w(V.mark())
                S.activation(out=s1set, in_=s0set, func=AF.Sqrt)
                V.w(S.mark())
                V.reciprocal(out=s2set, in_=s1set)
                V.tensor_tensor(out=s1set, in0=s2set, in1=s2set, op=ALU.mult)
                V.tensor_tensor(out=s1set, in0=s1set, in1=s0set, op=ALU.mult)
                V.tensor_scalar(out=s1set, in0=s1set, scalar1=-0.5, scalar2=1.5,
                                op0=ALU.mult, op1=ALU.add)
                V.tensor_tensor(out=s2set, in0=s2set, in1=s1set, op=ALU.mult)
                T.w(V.mark())
                for i in range(4):
                    T.matmul(rbanks[i][:, 0:512], ones_row[:],
                             s2set[0:1, i * 512:(i + 1) * 512], start=True, stop=True)
                V.w(T.mark())
                for i, h in enumerate(hs4):
                    V.tensor_tensor(out=k_hh[:, h * 512:(h + 1) * 512],
                                    in0=kr_h[:, h * 512:(h + 1) * 512],
                                    in1=rbanks[i][:, 0:512], op=ALU.mult)

        def delta_chunk(c_loc, own):
            cs0 = c_loc * 128
            if c_loc == 0:
                V.w(S.mark())
                V.w(T.mark())
                for b in range(4):
                    V.memset(Nbd[b][:], 0.0)
                V.memset(AoffBD[:], 0.0)
            for grp in range(4):
                hs = [grp * 4 + i for i in range(4)]
                T.w(V.mark())
                T.w(S.mark())
                for i, h in enumerate(hs):
                    kk_ap = k_hh[:, h * 512 + cs0: h * 512 + cs0 + 128]
                    T.matmul(psQ[:, i * 128:(i + 1) * 128], kk_ap, kk_ap,
                             start=True, stop=True)
                for i, h in enumerate(hs):
                    T.matmul(psR[:, i * 128:(i + 1) * 128], S_b[h][:],
                             k_hh[:, h * 512 + cs0: h * 512 + cs0 + 128],
                             start=True, stop=True)
                V.w(T.mark())
                for b in range(4):
                    j16 = c_loc * 4 + b
                    for i, h in enumerate(hs):
                        V.tensor_copy(out=bstk[b][32 * i:32 * (i + 1), 0:1],
                                      in_=betaT4[:, j16 * 16 + h: j16 * 16 + h + 1])
                    V.tensor_scalar(out=nbstk[b][:, 0:1], in0=bstk[b][:, 0:1],
                                    scalar1=-1.0, scalar2=None, op0=ALU.mult)
                for i in range(4):
                    for b in range(4):
                        V.scalar_tensor_tensor(
                            out=Nbd[b][32 * i:32 * (i + 1), 32 * i:32 * (i + 1)],
                            in0=psQ[32 * b:32 * (b + 1),
                                    i * 128 + 32 * b: i * 128 + 32 * (b + 1)],
                            scalar=bstk[b][32 * i:32 * (i + 1), 0:1],
                            in1=mask_ln[32 * i:32 * (i + 1), 32 * i:32 * (i + 1)],
                            op0=ALU.mult, op1=ALU.mult)
                    for bi, (b, b2) in enumerate(pairs_off):
                        V.tensor_copy(
                            out=AoffBD[32 * i:32 * (i + 1),
                                       bi * 128 + 32 * i: bi * 128 + 32 * (i + 1)],
                            in_=psQ[32 * b:32 * (b + 1),
                                    i * 128 + 32 * b2: i * 128 + 32 * (b2 + 1)])
                for i, h in enumerate(hs):
                    V.scalar_tensor_tensor(out=RpreG[:, i * 128:(i + 1) * 128],
                                           in0=psR[:, i * 128:(i + 1) * 128],
                                           scalar=-1.0,
                                           in1=v_h[:, h * 512 + cs0: h * 512 + cs0 + 128],
                                           op0=ALU.mult, op1=ALU.add)
                T.w(V.mark())
                for i in range(4):
                    T.transpose(psBt[:, i * 128:(i + 1) * 128],
                                RpreG[:, i * 128:(i + 1) * 128], identb[:])
                    T.transpose(psBt[:, 512 + i * 128:512 + (i + 1) * 128],
                                k_hh[:, hs[i] * 512 + cs0: hs[i] * 512 + cs0 + 128],
                                identb[:])
                V.w(T.mark())
                for i in range(4):
                    for b in range(4):
                        j16 = c_loc * 4 + b
                        V.tensor_scalar(
                            out=rbS[b][32 * i:32 * (i + 1), :],
                            in0=psBt[32 * b:32 * (b + 1), i * 128:(i + 1) * 128],
                            scalar1=betaT4[:, j16 * 16 + hs[i]: j16 * 16 + hs[i] + 1],
                            scalar2=None, op0=ALU.mult)
                    V.tensor_copy(out=KtmG[:, i * 128:(i + 1) * 128],
                                  in_=psBt[:, 512 + i * 128:512 + (i + 1) * 128])
                for b in range(4):
                    V.tensor_copy(out=cW[b][0][:], in_=Nbd[b][:])
                T.w(V.mark())
                for b in range(4):
                    T.transpose(psBt[:, b * 128:(b + 1) * 128], cW[b][0][:], identb[:])
                V.w(T.mark())
                for b in range(4):
                    V.tensor_copy(out=cZ[b][0][:], in_=psBt[:, b * 128:(b + 1) * 128])
                    V.tensor_tensor(out=cG[b][0][:], in0=cZ[b][0][:], in1=identb[:],
                                    op=ALU.add)
                for j in range(1, 5):
                    cur, nxt = (j - 1) % 2, j % 2
                    T.w(V.mark())
                    for b in range(4):
                        T.matmul(psC[:, b * 128:(b + 1) * 128], cZ[b][cur][:],
                                 cW[b][cur][:], start=True, stop=True)
                    if j < 4:
                        for b in range(4):
                            T.matmul(psR[:, b * 128:(b + 1) * 128], cW[b][cur][:],
                                     cZ[b][cur][:], start=True, stop=True)
                    V.w(T.mark())
                    for b in range(4):
                        V.tensor_copy(out=cW[b][nxt][:], in_=psC[:, b * 128:(b + 1) * 128])
                        if j < 4:
                            V.tensor_copy(out=cZ[b][nxt][:], in_=psR[:, b * 128:(b + 1) * 128])
                    T.w(V.mark())
                    for b in range(4):
                        T.matmul(psQ[:, b * 128:(b + 1) * 128], cW[b][nxt][:],
                                 cG[b][cur][:], start=True, stop=True)
                    V.w(T.mark())
                    for b in range(4):
                        V.tensor_tensor(out=cG[b][nxt][:], in0=psQ[:, b * 128:(b + 1) * 128],
                                        in1=cG[b][cur][:], op=ALU.add)
                for b in range(4):
                    T.w(V.mark())
                    T.matmul(psR[:, b * 128:(b + 1) * 128], cG[b][0][:], rbS[b][:],
                             start=True, stop=True)
                    V.w(T.mark())
                    V.tensor_copy(out=ubS[b][:], in_=psR[:, b * 128:(b + 1) * 128])
                    for i in range(4):
                        V.tensor_copy(out=UtmG[32 * b:32 * (b + 1), i * 128:(i + 1) * 128],
                                      in_=ubS[b][32 * i:32 * (i + 1), :])
                    if b < 3:
                        T.w(V.mark())
                        for b2 in range(b + 1, 4):
                            bi = pairs_off.index((b, b2))
                            T.matmul(psC[:, b2 * 128:(b2 + 1) * 128],
                                     AoffBD[:, bi * 128:(bi + 1) * 128],
                                     ubS[b][:], start=True, stop=True)
                        V.w(T.mark())
                        for b2 in range(b + 1, 4):
                            V.scalar_tensor_tensor(out=rbS[b2][:],
                                                   in0=psC[:, b2 * 128:(b2 + 1) * 128],
                                                   scalar=nbstk[b2][:, 0:1], in1=rbS[b2][:],
                                                   op0=ALU.mult, op1=ALU.add)
                T.w(V.mark())
                T.w(S.mark())
                for i, h in enumerate(hs):
                    T.matmul(psC[:, i * 128:(i + 1) * 128], KtmG[:, i * 128:(i + 1) * 128],
                             UtmG[:, i * 128:(i + 1) * 128], start=True, stop=True)
                    if own:
                        T.matmul(psQ[:, i * 128:(i + 1) * 128],
                                 k_hh[:, h * 512 + cs0: h * 512 + cs0 + 128],
                                 q_h[:, h * 512 + cs0: h * 512 + cs0 + 128],
                                 start=True, stop=True)
                V.w(T.mark())
                for i, h in enumerate(hs):
                    V.tensor_tensor(out=S_f[h][:], in0=S_f[h][:],
                                    in1=psC[:, i * 128:(i + 1) * 128], op=ALU.add)
                if own:
                    for i in range(4):
                        V.tensor_tensor(out=PTh4[:, i * 128:(i + 1) * 128],
                                        in0=psQ[:, i * 128:(i + 1) * 128],
                                        in1=mask_tu[:], op=ALU.mult)
                    T.w(V.mark())
                    for i, h in enumerate(hs):
                        T.matmul(psR[:, i * 128:(i + 1) * 128], S_b[h][:],
                                 q_h[:, h * 512 + cs0: h * 512 + cs0 + 128],
                                 start=True, stop=False)
                        T.matmul(psR[:, i * 128:(i + 1) * 128],
                                 UtmG[:, i * 128:(i + 1) * 128],
                                 PTh4[:, i * 128:(i + 1) * 128], start=False, stop=True)
                        T.matmul(psRow[:, i * 128:(i + 1) * 128],
                                 wg2T[:, h * 128:(h + 1) * 128],
                                 xg1_buf[:, cs0:cs0 + 128], start=True, stop=True)
                    S.w(T.mark())
                    for i in range(4):
                        S.activation(out=gate4[:, i * 128:(i + 1) * 128],
                                     in_=psRow[:, i * 128:(i + 1) * 128], func=AF.Sigmoid)
                    V.w(S.mark())
                    for i in range(4):
                        V.tensor_tensor(out=ogf4[:, i * 128:(i + 1) * 128],
                                        in0=psR[:, i * 128:(i + 1) * 128],
                                        in1=gate4[:, i * 128:(i + 1) * 128], op=ALU.mult)
                    V.tensor_tensor(out=ksq[:], in0=ogf4[:], in1=ogf4[:], op=ALU.mult)
                    T.w(V.mark())
                    for i in range(4):
                        T.matmul(psC[0:1, i * 128:(i + 1) * 128], ones_col[:],
                                 ksq[:, i * 128:(i + 1) * 128], start=True, stop=True)
                    V.w(T.mark())
                    V.tensor_scalar(out=rowm[:], in0=psC[0:1, 0:512],
                                    scalar1=_f32(1.0 / 128.0), scalar2=EPS,
                                    op0=ALU.mult, op1=ALU.add)
                    S.w(V.mark())
                    S.activation(out=rowq[:], in_=rowm[:], func=AF.Sqrt)
                    V.w(S.mark())
                    V.reciprocal(out=rowr[:], in_=rowq[:])
                    V.tensor_tensor(out=rowq[:], in0=rowr[:], in1=rowr[:], op=ALU.mult)
                    V.tensor_tensor(out=rowq[:], in0=rowq[:], in1=rowm[:], op=ALU.mult)
                    V.tensor_scalar(out=rowq[:], in0=rowq[:], scalar1=-0.5,
                                    scalar2=1.5, op0=ALU.mult, op1=ALU.add)
                    V.tensor_tensor(out=rowr[:], in0=rowr[:], in1=rowq[:], op=ALU.mult)
                    T.w(V.mark())
                    T.matmul(psX[:, 0:512], ones_row[:], rowr[0:1, :], start=True, stop=True)
                    V.w(T.mark())
                    V.tensor_tensor(out=ogf4[:], in0=ogf4[:], in1=psX[:, :], op=ALU.mult)
                    for i, h in enumerate(hs):
                        V.tensor_scalar(out=og_all[:, h * 512 + cs0: h * 512 + cs0 + 128],
                                        in0=ogf4[:, i * 128:(i + 1) * 128],
                                        scalar1=gnc[:, h:h + 1],
                                        scalar2=None, op0=ALU.mult)
                for i, h in enumerate(hs):
                    V.tensor_copy(out=S_b[h][:], in_=S_f[h][:])

        def ln_tm(tile):
            V.tensor_reduce(out=colm[:], in_=tile[:], axis=AX.X, op=ALU.add)
            S.w(V.mark())
            S.activation(out=sq_t[:], in_=tile[:], func=AF.Square)
            V.w(S.mark())
            V.tensor_reduce(out=colq[:], in_=sq_t[:], axis=AX.X, op=ALU.add)
            V.tensor_scalar(out=colm[:], in0=colm[:], scalar1=_f32(1.0 / 2048.0),
                            scalar2=None, op0=ALU.mult)
            V.tensor_scalar(out=colq[:], in0=colq[:], scalar1=_f32(1.0 / 2048.0),
                            scalar2=None, op0=ALU.mult)
            V.tensor_tensor(out=colt[:], in0=colm[:], in1=colm[:], op=ALU.mult)
            V.tensor_tensor(out=colq[:], in0=colq[:], in1=colt[:], op=ALU.subtract)
            V.tensor_scalar(out=colq[:], in0=colq[:], scalar1=EPS_LN, scalar2=None,
                            op0=ALU.add)
            S.w(V.mark())
            S.activation(out=colt[:], in_=colq[:], func=AF.Sqrt)
            V.w(S.mark())
            V.reciprocal(out=colr[:], in_=colt[:])
            V.tensor_tensor(out=colt[:], in0=colr[:], in1=colr[:], op=ALU.mult)
            V.tensor_tensor(out=colt[:], in0=colt[:], in1=colq[:], op=ALU.mult)
            V.tensor_scalar(out=colt[:], in0=colt[:], scalar1=-0.5, scalar2=1.5,
                            op0=ALU.mult, op1=ALU.add)
            V.tensor_tensor(out=colr[:], in0=colr[:], in1=colt[:], op=ALU.mult)
            V.tensor_scalar(out=tile[:], in0=tile[:], scalar1=colm[:, 0:1],
                            scalar2=None, op0=ALU.subtract)
            V.tensor_scalar(out=tile[:], in0=tile[:], scalar1=colr[:, 0:1],
                            scalar2=None, op0=ALU.mult)

        def tail_strip(si):
            t0 = si * 512
            banks = [psX, psQ, psC, psR]
            wo_marks = {}
            for mc in range(4):
                for kk in range(16):
                    gwo = mc * 16 + kk
                    mld = SY.dma(worhs[kk % 2][:],
                                 woT_d[kk * 128:(kk + 1) * 128, mc * 512:(mc + 1) * 512],
                                 wait=wo_marks.get(gwo - 2, T.mark() if gwo < 2 else None))
                    T.w(mld)
                    if kk == 0:
                        T.w(V.mark())
                        T.w(S.mark())
                    for tt in range(4):
                        T.matmul(banks[tt][:, 0:512],
                                 og_all[:, kk * 512 + tt * 128: kk * 512 + tt * 128 + 128],
                                 worhs[kk % 2][:], start=(kk == 0), stop=(kk == 15))
                    wo_marks[gwo] = T.mark()
                V.w(T.mark())
                for tt in range(4):
                    V.tensor_copy(out=y1[tt][:, mc * 512:(mc + 1) * 512],
                                  in_=banks[tt][:, 0:512])
            for tt in range(4):
                mld = SY.dma(x_tm[tt % 2][:],
                             x_own_d[t0 + tt * 128: t0 + (tt + 1) * 128, :],
                             wait=V.mark())
                V.w(mld)
                V.tensor_tensor(out=y1[tt][:], in0=y1[tt][:], in1=x_tm[tt % 2][:],
                                op=ALU.add)
                ln_tm(y1[tt])
                T.w(V.mark())
                for g in range(4):
                    if g > 0:
                        T.w(V.mark())
                    for k4 in range(4):
                        kk = g * 4 + k4
                        T.transpose(psX[:, k4 * 128:(k4 + 1) * 128],
                                    y1[tt][:, kk * 128:(kk + 1) * 128], ident_s[:])
                    V.w(T.mark())
                    for k4 in range(4):
                        kk = g * 4 + k4
                        V.tensor_copy(out=x1b[:, kk * 512 + tt * 128: kk * 512 + tt * 128 + 128],
                                      in_=psX[:, k4 * 128:(k4 + 1) * 128])
            NH = NF // 2
            for half in range(2):
                f0 = half * NH
                gel_marks = {}
                upuse_marks = {}
                for mg in range(NH // 2):
                    mgl = f0 // 2 + mg
                    mld = SY.dma(upw[mg % 2][:],
                                 upimg_d[mgl * 128:(mgl + 1) * 128, :],
                                 wait=upuse_marks.get(mg - 2, T.mark() if mg < 2 else None))
                    T.w(mld)
                    if mg < 2:
                        T.w(S.mark())
                    else:
                        T.w(gel_marks[mg - 2])
                    if mg == 0:
                        T.w(V.mark())
                    bk = [banks[(2 * mg) % 4], banks[(2 * mg + 1) % 4]]
                    for m2 in range(2):
                        for kk in range(16):
                            T.matmul(bk[m2][:, 0:512],
                                     upw[mg % 2][:, kk * 256 + m2 * 128: kk * 256 + m2 * 128 + 128],
                                     x1b[:, kk * 512:(kk + 1) * 512],
                                     start=(kk == 0), stop=(kk == 15))
                    upuse_marks[mg] = T.mark()
                    S.w(T.mark())
                    for m2 in range(2):
                        m = mg * 2 + m2
                        S.activation(out=h_sb[:, m * 512:(m + 1) * 512], in_=bk[m2][:, 0:512],
                                     func=AF.Gelu_apprx_tanh, bias=ub[:, f0 + m:f0 + m + 1])
                    gel_marks[mg] = S.mark()
                dn_marks = {}
                for mc in range(4):
                    T.w(V.mark())
                    T.w(S.mark())
                    if half == 0:
                        for tt in range(4):
                            T.matmul(banks[tt][:, 0:512], ones_row[:],
                                     dbrow[0:1, mc * 512:(mc + 1) * 512],
                                     start=True, stop=False)
                    for kf in range(NH):
                        gdn = mc * NH + kf
                        mld = SY.dma(dnw[kf % 2][:],
                                     dndq_d[(f0 + kf) * 128:(f0 + kf + 1) * 128,
                                            mc * 512:(mc + 1) * 512],
                                     wait=dn_marks.get(gdn - 2, T.mark() if gdn < 2 else None))
                        T.w(mld)
                        for tt in range(4):
                            T.matmul(banks[tt][:, 0:512],
                                     h_sb[:, kf * 512 + tt * 128: kf * 512 + tt * 128 + 128],
                                     dnw[kf % 2][:],
                                     start=(half == 1 and kf == 0),
                                     stop=(kf == NH - 1))
                        dn_marks[gdn] = T.mark()
                    V.w(T.mark())
                    for tt in range(4):
                        V.tensor_tensor(out=y1[tt][:, mc * 512:(mc + 1) * 512],
                                        in0=y1[tt][:, mc * 512:(mc + 1) * 512],
                                        in1=banks[tt][:, 0:512], op=ALU.add)
            for tt in range(4):
                ln_tm(y1[tt])
                GP.dma(y_d[t0 + tt * 128: t0 + (tt + 1) * 128, :], y1[tt][:],
                       wait=V.mark())
            V.w(GP.mark())

        for si in range(NS_PRE):
            proj_strip(si, x_pre_d, own=False)
            for c_loc in range(4):
                delta_chunk(c_loc, own=False)
        for si in range(NS_OWN):
            proj_strip(si, x_own_d, own=True)
            for c_loc in range(4):
                delta_chunk(c_loc, own=True)
            tail_strip(si)
        GP.drain()
        SY.drain()

    run_passes(nc, sems, prog)
    return nc


import ml_dtypes

_prog_cache = {}


def _program():
    if "nc" not in _prog_cache:
        _prog_cache["nc"] = build_fused(n_own=1024, n_pre=3072, FF=8192)
    return _prog_cache["nc"]


def kernel(**ins):
    from concourse.bass_utils import run_bass_kernel_spmd
    x = np.asarray(ins["L0_attn_x"], dtype=np.float32)
    ii, jj = np.indices((128, 128))
    shared = {
        "Wq": np.asarray(ins["L0_attn_Wq"], np.float32),
        "Wk": np.asarray(ins["L0_attn_Wk"], np.float32),
        "Wv": np.asarray(ins["L0_attn_Wv"], np.float32),
        "Wbeta": np.asarray(ins["L0_attn_Wbeta"], np.float32),
        "Wg1": np.asarray(ins["L0_attn_Wg1"], np.float32),
        "Wg2": np.asarray(ins["L0_attn_Wg2"], np.float32),
        "gnorm_cols": np.ascontiguousarray(
            np.asarray(ins["L0_attn_g_norm"], np.float32).reshape(16, 128).T),
        "Wo": np.asarray(ins["L0_attn_Wo"], np.float32),
        "up_w": np.asarray(ins["L0_ffn_up_kernel"], np.float32),
        "dn_w": np.asarray(ins["L0_ffn_down_kernel"], np.float32),
        "ub_cols": np.ascontiguousarray(
            np.asarray(ins["L0_ffn_up_bias"], np.float32).reshape(64, 128).T),
        "db_row": np.ascontiguousarray(
            np.asarray(ins["L0_ffn_down_bias"], np.float32).reshape(1, 2048)),
        "ident": np.eye(128, dtype=np.float32),
        "mask_lowneg": np.where(ii > jj, -1.0, 0.0).astype(np.float32),
        "mask_triu": np.where(ii <= jj, 1.0, 0.0).astype(ml_dtypes.bfloat16),
    }
    nc = _program()
    in_maps = []
    for c in range(8):
        b, p = c // 4, c % 4
        xp = np.zeros((3072, 2048), np.float32)
        if p:
            xp[3072 - 1024 * p:] = x[b, :1024 * p]
        m = dict(shared)
        m["x_pre"] = xp
        m["x_own"] = np.ascontiguousarray(x[b, 1024 * p:1024 * (p + 1)])
        in_maps.append(m)
    res = run_bass_kernel_spmd(nc, in_maps, core_ids=list(range(8)))
    out = np.empty((2, 4096, 2048), np.float32)
    for c in range(8):
        b, p = c // 4, c % 4
        out[b, 1024 * p:1024 * (p + 1)] = res.results[c]["y_out"]
    return out
